# version 03
# baseline (speedup 1.0000x reference)
import math
import numpy as np
import concourse.bass as bass
import concourse.mybir as mybir
from concourse.bass_utils import run_bass_kernel_spmd

F32 = mybir.dt.float32
BF16 = mybir.dt.bfloat16
AF = mybir.ActivationFunctionType
ALU = mybir.AluOpType
AX = mybir.AxisListType

S, D = 4096, 1024
NT = S // 128
NCHUNK = S // 256
CORES = 8
EPS_LN = 1e-5
EPS_LA = 1e-6

ENGINES = ("tensor", "vector", "scalar", "gpsimd", "sync")
DBG = None
L1F_LAYERS = 6
L1F_SUB = "af"
DBG_KF = 0
DBG_NKF = 0
DBG_SERIAL = 0


def dbg_tap(cx, name, ap, rows, wait=()):
    if DBG != name:
        return
    p = cx.p
    v = p.op("gpsimd", "dma_start", cx.dram["dbg"][rows[0]:rows[1], :ap.free_size()],
             ap, wait=wait, inc="WT", incn=16)
    p.wait("vector", [("WT", v)])
    p.wait("scalar", [("WT", v)])
    p.wait("tensor", [("WT", v)])


def host_consts(ins):
    c = {}
    f32 = np.float32
    c["T_eye"] = np.eye(128, dtype=f32)
    tk = np.arange(256)
    c["T_mask0"] = (tk[:, None] <= tk[None, :]).astype(f32)
    i = np.arange(128)[:, None]
    jj = np.arange(384)[None, :]
    c["T_mask1"] = np.where((jj >= i) & (jj <= i + 256), 0.0, -1e30).astype(f32)
    half = np.arange(0, 64, 2, dtype=f32)
    inv_freq = (1.0 / 10000.0 ** (half / 64)).astype(f32)
    t = np.arange(S, dtype=f32)
    fr = t[:, None] * inv_freq[None, :]
    fr = np.concatenate([fr, fr], -1).astype(f32)
    base = ((half + 0.4 * 64) / (1.4 * 64)).astype(f32)
    power = (t - S // 2) / 128.0
    sc = (base[None, :] ** power[:, None]).astype(f32)
    sc = np.concatenate([sc, sc], -1).astype(f32)
    cos, sin = np.cos(fr).astype(f32), np.sin(fr).astype(f32)
    sign = np.concatenate([-np.ones(32, f32), np.ones(32, f32)])
    c["T_CS"] = cos * sc
    c["T_SSgn"] = sin * sc * sign
    c["T_CI"] = (cos / sc).astype(f32)
    c["T_SIgn"] = (sin / sc * sign).astype(f32)
    c["T_gwT"] = np.ascontiguousarray(ins["L0_ffn_gate_w"].T)
    c["T_wgT"] = np.ascontiguousarray(ins["L0_ffn_wg"].transpose(0, 2, 1))
    c["T_wuT"] = np.ascontiguousarray(ins["L0_ffn_wu"].transpose(0, 2, 1))
    c["T_wdT"] = np.ascontiguousarray(ins["L0_ffn_wd"].transpose(0, 2, 1))
    c["T_sgT"] = np.ascontiguousarray(ins["L0_ffn_sg"].T)
    c["T_suT"] = np.ascontiguousarray(ins["L0_ffn_su"].T)
    c["T_sdT"] = np.ascontiguousarray(ins["L0_ffn_sd"].T)
    c["T_WsT"] = np.ascontiguousarray(ins["L1_attn_Ws"].T)
    c["T_WiT"] = np.ascontiguousarray(ins["L1_attn_Wi"].T)
    c["T_WoT1"] = np.ascontiguousarray(ins["L1_attn_Wo"].T)
    c["T_WgT"] = np.ascontiguousarray(ins["L1_attn_Wg"].T)
    c["T_AT"] = np.ascontiguousarray(ins["L1_attn_A"].T)
    qs = ins["L1_ffn_q_scale"]
    ks = ins["L1_ffn_k_scale"]
    c["T_qsrot"] = np.ascontiguousarray(np.concatenate([qs[:, 32:], qs[:, :32]], 1))
    c["T_ksrot"] = np.ascontiguousarray(np.concatenate([ks[:, 32:], ks[:, :32]], 1))
    dup = np.zeros((64, 128), f32)
    dup[np.arange(64), np.arange(64)] = 1.0
    dup[np.arange(64), 64 + np.arange(64)] = 1.0
    c["T_dupA"] = dup
    c["T_shift1"] = np.eye(128, k=1).astype(f32)
    e127 = np.zeros((128, 128), f32)
    e127[127, 0] = 1.0
    c["T_e127"] = e127
    W1 = ins["L1_ffn_W1"]
    W2 = ins["L1_ffn_W2"]
    b1 = ins["L1_ffn_b1"]
    W1p = np.zeros((6, 1024, 5632), f32)
    W1p[:, :, 0:2730] = W1[:, :, 0:2730]
    W1p[:, :, 2816:2816 + 2730] = W1[:, :, 2730:5460]
    W2p = np.zeros((6, 2816, 1024), f32)
    W2p[:, 0:2730, :] = W2
    b1p = np.zeros((6, 5632), f32)
    b1p[:, 0:2730] = b1[:, 0:2730]
    b1p[:, 2816:2816 + 2730] = b1[:, 2730:5460]
    c["T_W1p"] = W1p
    c["T_W2p"] = W2p
    c["T_b1c"] = np.ascontiguousarray(b1p.reshape(6, 44, 128).transpose(0, 2, 1))
    return c


class P:
    def __init__(self, nc):
        self.nc = nc
        self.q = {e: [] for e in ENGINES}
        self.blocks = []
        self.sems = {}
        self.cnt = {}
        self.lastw = {e: {} for e in ENGINES}
        self.names = set()

    def sem(self, name):
        if name not in self.sems:
            self.sems[name] = self.nc.alloc_semaphore("s_" + name)
            self.cnt[name] = 0
        return name

    def val(self, name):
        return self.cnt[name]

    def op(self, eng, meth, *a, wait=(), inc=None, incn=1, **kw):
        ret = None
        if inc is not None:
            self.sem(inc)
            self.cnt[inc] += incn
            ret = self.cnt[inc]
        for s, _ in wait:
            self.sem(s)
        sems = self.sems
        lw = self.lastw[eng]
        rw = []
        for s, v in wait:
            if lw.get(s, -1) < v:
                lw[s] = v
                rw.append((s, v))
        def th(e, a=a, kw=kw, rw=rw, inc=inc, incn=incn, meth=meth):
            for nm, v in rw:
                e.wait_ge(sems[nm], v)
            ins = getattr(e, meth)(*a, **kw)
            if inc is not None:
                ins.then_inc(sems[inc], incn)
        self.q[eng].append(th)
        return ret

    def wait(self, eng, wait):
        self.op(eng, "nop", wait=wait)

    def raw(self, eng, fn):
        self.q[eng].append(fn)

    def barrier(self):
        if any(self.q[e] for e in ENGINES):
            self.blocks.append(self.q)
        self.q = {e: [] for e in ENGINES}
        for e in ENGINES:
            pass

    def build(self):
        self.barrier()
        for qd in self.blocks:
            with self.nc.Block() as blk:
                for eng in ENGINES:
                    if qd[eng]:
                        def runner(e, lst=qd[eng]):
                            for f in lst:
                                f(e)
                        getattr(blk, eng)(runner)


class Arena:
    def __init__(self, nc, base, size):
        self.nc = nc
        self.base = base
        self.size = size
        self.cur = base
        self.uid = 0

    def reset(self):
        self.cur = self.base

    def tile(self, name, shape, dtype):
        per = int(np.prod(shape[1:])) * mybir.dt.size(dtype)
        per = (per + 31) // 32 * 32
        off = self.cur
        self.cur += per
        assert self.cur <= self.base + self.size, \
            f"arena overflow at {name}: {self.cur-self.base} > {self.size}"
        self.uid += 1
        return self.nc.alloc_sbuf_tensor_at(f"ar_{name}_{self.uid}", shape, dtype, offset=off)


class Ctx:
    pass


def stage_fence(cx):
    p = cx.p
    w = [(s, p.val(s)) for s in ("ST", "LD", "WT")]
    for e in ENGINES:
        p.wait(e, list(w))
    p.barrier()


def ln_stats2(cx, xt_ap, sq_buf, mu, rstd, nmr, tmp, eps=EPS_LN, wait_v=(), wait_a=()):
    p = cx.p
    N = xt_ap.shape[-1]
    inv = 1.0 / N
    v1 = p.op("vector", "reduce_sum", mu, xt_ap, axis=AX.X, wait=wait_v, inc="V")
    a_sq = p.op("scalar", "activation", sq_buf, xt_ap, AF.Square, wait=wait_a, inc="A")
    v2 = p.op("vector", "reduce_sum", tmp, sq_buf, axis=AX.X, wait=[("A", a_sq)], inc="V")
    v3 = p.op("vector", "tensor_scalar_mul", mu, mu, inv, wait=[("V", v1)], inc="V")
    v4 = p.op("vector", "tensor_scalar_mul", tmp, tmp, inv, wait=[("V", v2)], inc="V")
    v5 = p.op("vector", "scalar_tensor_tensor", rstd, mu, -1.0, mu, ALU.mult, ALU.mult,
              wait=[("V", v3)], inc="V")
    va = p.op("vector", "tensor_tensor", tmp, tmp, rstd, ALU.add,
              wait=[("V", v5), ("V", v4)], inc="V")
    a_sd = p.op("scalar", "activation", tmp, tmp, AF.Sqrt, bias=cx.c_epsln[:],
                wait=[("V", va)], inc="A")
    v6 = p.op("vector", "reciprocal", rstd, tmp, wait=[("A", a_sd)], inc="V")
    vr = p.op("vector", "scalar_tensor_tensor", nmr, mu, -1.0, rstd, ALU.mult, ALU.mult,
              wait=[("V", v6)], inc="V")
    return vr


def emit_l0a(cx, src_dram, dst_dram):
    p, nc, ar = cx.p, cx.nc, cx.ar
    ar.reset()

    Wq = ar.tile("Wq", [128, 8, 1024], BF16)
    Wk = ar.tile("Wk", [128, 8, 256], BF16)
    Wv = ar.tile("Wv", [128, 8, 256], BF16)
    Wc = ar.tile("Wc", [128, 2, 64], BF16)
    We = ar.tile("We", [64, 256], BF16)
    Wo = ar.tile("Wo", [128, 8, 1024], BF16)
    brow = ar.tile("brow", [1, 4096], BF16)
    gamb = ar.tile("gamb", [128, 1024], F32)
    betb = ar.tile("betb", [128, 1024], F32)
    grow = ar.tile("grow", [1, 2048], F32)
    mask0 = ar.tile("mask0", [128, 2, 256], BF16)

    g = cx.g
    d = cx.dram
    p.op("gpsimd", "dma_start", Wq[:], d["L0_attn_Wq"].rearrange("(k p) n -> p k n", p=128), inc="WT", incn=16)
    p.op("gpsimd", "dma_start", Wk[:], d["L0_attn_Wk"].rearrange("(k p) n -> p k n", p=128), inc="WT", incn=16)
    p.op("gpsimd", "dma_start", Wv[:], d["L0_attn_Wv"].rearrange("(k p) n -> p k n", p=128), inc="WT", incn=16)
    p.op("gpsimd", "dma_start", Wc[:], d["L0_attn_Wc"].rearrange("(k p) n -> p k n", p=128), inc="WT", incn=16)
    p.op("gpsimd", "dma_start", We[:], d["L0_attn_We"], inc="WT", incn=16)
    p.op("gpsimd", "dma_start", Wo[:], d["L0_attn_Wo"].rearrange("(k p) n -> p k n", p=128), inc="WT", incn=16)
    p.op("gpsimd", "dma_start", brow[:, 0:1024], d["L0_attn_bq"].unsqueeze(0), inc="WT", incn=16)
    p.op("gpsimd", "dma_start", brow[:, 1024:1280], d["L0_attn_bk"].unsqueeze(0), inc="WT", incn=16)
    p.op("gpsimd", "dma_start", brow[:, 1280:1536], d["L0_attn_bv"].unsqueeze(0), inc="WT", incn=16)
    p.op("gpsimd", "dma_start", brow[:, 1536:2560], d["L0_attn_bo"].unsqueeze(0), inc="WT", incn=16)
    p.op("gpsimd", "dma_start", mask0[:], d["T_mask0"].rearrange("(r p) n -> p r n", p=128), inc="WT", incn=16)
    wt_g = p.op("sync", "dma_start", grow[:, 0:1024], d["L0_attn_gamma"].unsqueeze(0), inc="LD", incn=16)
    wt_g = p.op("sync", "dma_start", grow[:, 1024:2048], d["L0_attn_beta"].unsqueeze(0), inc="LD", incn=16)
    wt_w = p.val("WT")

    PS0, PS1, PS2, PS3, PS4, PS5 = cx.PS0, cx.PS1, cx.PS2, cx.PS3, cx.PS4, cx.PS5
    t0 = p.op("tensor", "matmul", PS0[:, 0:512], cx.ones_f[:, 0:128], grow[:, 0:512],
              wait=[("LD", wt_g)], start=True, stop=True, inc="T")
    p.op("tensor", "matmul", PS1[:, 0:512], cx.ones_f[:, 0:128], grow[:, 512:1024], start=True, stop=True)
    p.op("tensor", "matmul", PS2[:, 0:512], cx.ones_f[:, 0:128], grow[:, 1024:1536], start=True, stop=True)
    t0 = p.op("tensor", "matmul", PS3[:, 0:512], cx.ones_f[:, 0:128], grow[:, 1536:2048], start=True, stop=True, inc="T")
    p.op("vector", "tensor_copy", gamb[:, 0:512], PS0[:, 0:512], wait=[("T", t0)])
    p.op("vector", "tensor_copy", gamb[:, 512:1024], PS1[:, 0:512])
    p.op("vector", "tensor_copy", betb[:, 0:512], PS2[:, 0:512])
    v0 = p.op("vector", "tensor_copy", betb[:, 512:1024], PS3[:, 0:512], inc="V")
    p.barrier()

    xt = [ar.tile(f"xt{i}", [128, 1024], F32) for i in range(2)]
    sqb = ar.tile("sqb", [128, 1024], BF16)
    sqb2 = ar.tile("sqb2", [128, 1024], BF16)
    ebuf = ar.tile("ebuf", [128, 1024], BF16)
    lnb = ar.tile("lnb", [128, 1024], BF16)
    xTt = ar.tile("xTt", [128, 8, 128], BF16)
    stt = ar.tile("stt", [128, NT, 4], F32)
    qf_t = ar.tile("qf_t", [128, 1024], BF16)
    qfp = ar.tile("qfp", [128, 1024], BF16)
    k1t = ar.tile("k1t", [128, 256], BF16)
    v1t = ar.tile("v1t", [128, 256], BF16)
    k1T = ar.tile("k1T", [128, 2, 128], BF16)
    v1T = ar.tile("v1T", [128, 2, 128], BF16)
    k2t = ar.tile("k2t", [128, 64], BF16)
    v2t = ar.tile("v2t", [128, 64], BF16)
    k2T = ar.tile("k2T", [64, 128], BF16)
    v2T = ar.tile("v2T", [64, 128], BF16)
    qfT = ar.tile("qfT", [128, 8, 256], BF16)
    kfT = ar.tile("kfT", [128, 2, 256], BF16)
    kfc = ar.tile("kfc", [128, 2, 256], BF16)
    vag = ar.tile("vag", [128, 2, 4, 65], BF16)
    simb = ar.tile("simb", [128, 2, 256], BF16)
    epsr = ar.tile("epsr", [1, 130], BF16)
    recd = ar.tile("recd", [128, 2], F32)
    otc = ar.tile("otc", [128, 2, 1024], BF16)
    oTt = ar.tile("oTt", [128, 8, 128], BF16)
    dupA = ar.tile("dupA", [64, 128], BF16)
    Sbf0 = ar.tile("Sbf0", [64, 65], BF16)
    Saug = ar.tile("Saug", [64, 4, 65], F32)
    Sbf2 = ar.tile("Sbf2", [128, 2, 65], BF16)
    lnep = ar.tile("lnep", [128, 1024], F32)
    ysb = ar.tile("ysb", [128, 1024], F32)
    Rt = [ar.tile(f"Rt{i}", [128, 1024], F32) for i in range(2)]

    p.op("vector", "memset", Saug[:], 0.0)
    p.op("vector", "memset", Sbf2[:], 0.0)
    p.op("vector", "memset", vag[:], 1.0)
    p.op("vector", "memset", epsr[:], 0.0)
    p.op("vector", "memset", epsr[:, 64:65], EPS_LA)
    p.op("vector", "memset", epsr[:, 129:130], EPS_LA)
    wdup = p.op("gpsimd", "dma_start", dupA[:], d["T_dupA"], inc="WT", incn=16)
    p.wait("gpsimd", [("WT", wdup)])
    p.barrier()

    PSB = cx.PSB
    ones_b = cx.ones_b

    ep_v = {}
    st_v = {}
    t_tr_prev = 0
    t_qkv_prev = 0
    for ch in range(NCHUNK):
        p.wait("vector", [("T", p.val("T"))])
        prep_v = {}
        for half in range(2):
            tau = 2 * ch + half
            w = [("V", ep_v[tau - 2])] if tau - 2 in ep_v else []
            ld = p.op("sync", "dma_start", xt[half][:], src_dram[tau * 128:(tau + 1) * 128, :],
                      wait=w, inc="LD", incn=16)
            mu = stt[:, tau, 0:1]; rstd = stt[:, tau, 1:2]; nmr = stt[:, tau, 2:3]; tmp = stt[:, tau, 3:4]
            vr = ln_stats2(cx, xt[half][:], sqb[:], mu, rstd, nmr, tmp,
                           wait_v=[("LD", ld)], wait_a=[("LD", ld)])
            a_ln = p.op("vector", "tensor_scalar", lnb[:], xt[half][:], rstd, nmr,
                        ALU.mult, ALU.add, wait=[("V", vr), ("T", t_tr_prev)], inc="V")
            dbg_tap(cx, "ln", lnb[:], (tau * 128, tau * 128 + 128), wait=[("V", a_ln)])
            for c in range(8):
                w = [("V", a_ln)] if c == 0 else []
                tt = p.op("tensor", "transpose", PSB[:, c * 128:(c + 1) * 128],
                          lnb[:, c * 128:(c + 1) * 128], cx.eye_b[:],
                          wait=w, inc="T")
            t_tr_prev = tt
            v_xt = p.op("vector", "tensor_copy", xTt[:].rearrange("p a b -> p (a b)"), PSB[:, 0:1024],
                        wait=[("T", tt), ("T", t_qkv_prev)], inc="V")

            tq0 = p.op("tensor", "matmul", PS0[:, 0:512], ones_b[:, 0:128], brow[:, 0:512],
                       wait=[("V", v_xt), ("WT", wt_w), ("V", p.val("V"))], start=True, stop=False)
            p.op("tensor", "matmul", PS1[:, 0:512], ones_b[:, 0:128], brow[:, 512:1024], start=True, stop=False)
            p.op("tensor", "matmul", PS2[:, 0:256], ones_b[:, 0:128], brow[:, 1024:1280], start=True, stop=False)
            p.op("tensor", "matmul", PS3[:, 0:256], ones_b[:, 0:128], brow[:, 1280:1536], start=True, stop=False)
            for k in range(8):
                st = False
                sp = (k == 7)
                p.op("tensor", "matmul", PS0[:, 0:512], xTt[:, k, :], Wq[:, k, 0:512], start=st, stop=sp)
                p.op("tensor", "matmul", PS1[:, 0:512], xTt[:, k, :], Wq[:, k, 512:1024], start=st, stop=sp)
                p.op("tensor", "matmul", PS2[:, 0:256], xTt[:, k, :], Wk[:, k, :], start=st, stop=sp)
                t_qkv = p.op("tensor", "matmul", PS3[:, 0:256], xTt[:, k, :], Wv[:, k, :],
                             start=st, stop=sp, inc="T")
            for h2 in range(2):
                ps = (PS0 if h2 == 0 else PS1)
                vm = p.op("vector", "tensor_scalar_min", ebuf[:, h2 * 512:(h2 + 1) * 512], ps[:, 0:512], 0.0,
                          wait=[("T", t_qkv)], inc="V")
                ae = p.op("scalar", "activation", ebuf[:, h2 * 512:(h2 + 1) * 512],
                          ebuf[:, h2 * 512:(h2 + 1) * 512], AF.Exp, wait=[("V", vm)], inc="A")
                v_qf = p.op("vector", "scalar_tensor_tensor", qf_t[:, h2 * 512:(h2 + 1) * 512],
                            ps[:, 0:512], 0.0, ebuf[:, h2 * 512:(h2 + 1) * 512], ALU.max, ALU.add,
                            wait=[("A", ae)], inc="V")
            dbg_tap(cx, "qf", qf_t[:], (tau * 128, tau * 128 + 128), wait=[("V", p.val("V"))])
            p.op("vector", "tensor_copy", k1t[:], PS2[:, 0:256], wait=[("T", t_qkv)])
            v_kv1 = p.op("vector", "tensor_copy", v1t[:], PS3[:, 0:256], inc="V")

            tt = None
            for c in range(2):
                w = [("V", v_kv1)] if c == 0 else []
                p.op("tensor", "transpose", PSB[:, c * 128:(c + 1) * 128], k1t[:, c * 128:(c + 1) * 128],
                     cx.eye_b[:], wait=w)
                tt = p.op("tensor", "transpose", PSB[:, 256 + c * 128:256 + (c + 1) * 128],
                          v1t[:, c * 128:(c + 1) * 128], cx.eye_b[:], inc="T")
            p.op("vector", "tensor_copy", k1T[:].rearrange("p a b -> p (a b)"), PSB[:, 0:256], wait=[("T", tt)])
            v_t1 = p.op("vector", "tensor_copy", v1T[:].rearrange("p a b -> p (a b)"), PSB[:, 256:512], inc="V")
            p.op("tensor", "matmul", PS4[:, 0:64], k1T[:, 0, :], Wc[:, 0, :],
                 wait=[("V", v_t1)], start=True, stop=False)
            p.op("tensor", "matmul", PS4[:, 0:64], k1T[:, 1, :], Wc[:, 1, :], start=False, stop=True)
            p.op("tensor", "matmul", PS4[:, 64:128], v1T[:, 0, :], Wc[:, 0, :], start=True, stop=False)
            t_k2 = p.op("tensor", "matmul", PS4[:, 64:128], v1T[:, 1, :], Wc[:, 1, :],
                        start=False, stop=True, inc="T")
            p.op("vector", "tensor_copy", k2t[:], PS4[:, 0:64], wait=[("T", t_k2)])
            v_k2 = p.op("vector", "tensor_copy", v2t[:], PS4[:, 64:128], inc="V")
            p.op("tensor", "transpose", PSB[0:64, 512:640], k2t[:], cx.eye_b[:], wait=[("V", v_k2)])
            t_k2T = p.op("tensor", "transpose", PSB[0:64, 640:768], v2t[:], cx.eye_b[:], inc="T")
            p.op("vector", "tensor_copy", k2T[:], PSB[0:64, 512:640], wait=[("T", t_k2T)])
            v_k2T = p.op("vector", "tensor_copy", v2T[:], PSB[0:64, 640:768], inc="V")
            p.op("tensor", "matmul", PS5[:, 0:256], k2T[:], We[:], wait=[("V", v_k2T)], start=True, stop=True)
            t_k3 = p.op("tensor", "matmul", PS5[:, 256:512], v2T[:], We[:], start=True, stop=True, inc="T")
            vm = p.op("vector", "tensor_scalar_min", ebuf[:, 0:256], PS5[:, 0:256], 0.0,
                      wait=[("T", t_k3)], inc="V")
            ae = p.op("scalar", "activation", ebuf[:, 0:256], ebuf[:, 0:256], AF.Exp,
                      wait=[("V", vm)], inc="A")
            v_kf = p.op("vector", "scalar_tensor_tensor", kfc[:, half, :], PS5[:, 0:256], 0.0,
                        ebuf[:, 0:256], ALU.max, ALU.add, wait=[("A", ae)], inc="V")
            p.op("vector", "tensor_copy",
                 vag[:, half, :, 0:64],
                 PS5[:, 256:512].rearrange("p (a b) -> p a b", a=4))
            v_qp = p.op("vector", "tensor_copy",
                        qfp[:].rearrange("p (g r q f) -> p g r q f", g=2, r=4, q=2),
                        qf_t[:].rearrange("p (g q r f) -> p g q r f", g=2, q=2, r=4)
                        .transpose([0, 1, 3, 2, 4]),
                        wait=[("T", t_qkv_prev), ("V", v_qf)], inc="V")
            tt = None
            for c in range(8):
                w = [("V", v_qp)] if c == 0 else []
                tt = p.op("tensor", "transpose", PSB[:, c * 128:(c + 1) * 128],
                          qfp[:, c * 128:(c + 1) * 128], cx.eye_b[:], wait=w, inc="T")
            v_qT = p.op("vector", "tensor_copy", qfT[:, :, half * 128:(half + 1) * 128],
                        PSB[:, 0:1024].rearrange("p (a b) -> p a b", a=8),
                        wait=[("T", tt)], inc="V")
            for c in range(2):
                tt = p.op("tensor", "transpose", PSB[:, c * 128:(c + 1) * 128],
                          kfc[:, half, c * 128:(c + 1) * 128], cx.eye_b[:],
                          wait=[("V", v_qT), ("V", v_kf)], inc="T")
            v_pr = p.op("vector", "tensor_copy", kfT[:, :, half * 128:(half + 1) * 128],
                        PSB[:, 0:256].rearrange("p (a b) -> p a b", a=2),
                        wait=[("T", tt)], inc="V")
            dbg_tap(cx, "kf", kfc[:, half, :], (tau * 128, tau * 128 + 128), wait=[("V", v_kf)])
            dbg_tap(cx, "vag", vag[:, half, :, :].rearrange("p a b -> p (a b)"),
                    (tau * 128, tau * 128 + 128), wait=[("V", p.val("V"))])
            prep_v[half] = v_pr
            t_qkv_prev = tt

        v_prep = prep_v[1]
        for kvh in range(4):
            b = (kvh % 2) * 64
            for qh in range(kvh * 4, kvh * 4 + 4):
                qsl = qfT[b:b + 64, (qh // 8) * 4 + (qh % 4), :]
                for j in range(2):
                    ksl = kfT[b:b + 64, kvh // 2, j * 128:(j + 1) * 128]
                    w = [("V", v_prep)] if (qh == kvh * 4 and j == 0) else []
                    if j == 0:
                        w.append(("V", p.val("V")))
                    t_sim = p.op("tensor", "matmul", PS0[:, j * 256:(j + 1) * 256], ksl, qsl,
                                 wait=w, start=True, stop=True, inc="T")
                v_sim = p.op("vector", "tensor_tensor", simb[:], PS0[:, 0:512].rearrange("p (a b) -> p a b", a=2),
                             mask0[:], ALU.mult, wait=[("T", t_sim)], inc="V")
                t_num = None
                for h2 in range(2):
                    reg = PS1[:, h2 * 65:(h2 + 1) * 65]
                    w = [("V", v_sim), ("V", p.val("V"))] if h2 == 0 else []
                    p.op("tensor", "matmul", reg, simb[:, 0, h2 * 128:(h2 + 1) * 128],
                         vag[:, 0, kvh, :], wait=w, start=True, stop=False)
                    p.op("tensor", "matmul", reg, simb[:, 1, h2 * 128:(h2 + 1) * 128],
                         vag[:, 1, kvh, :], start=False, stop=False)
                    p.op("tensor", "matmul", reg, qsl[:, h2 * 128:(h2 + 1) * 128],
                         Sbf2[b:b + 64, kvh // 2, :], start=False, stop=False)
                    t_num = p.op("tensor", "matmul", reg, ones_b[:, 0:128],
                                 epsr[:, h2 * 65:(h2 + 1) * 65],
                                 start=False, stop=True, inc="T")
                p.op("vector", "reciprocal", recd[:, 0:1], PS1[:, 64:65], wait=[("T", t_num)])
                v_rc = p.op("vector", "reciprocal", recd[:, 1:2], PS1[:, 129:130], inc="V")
                p.op("vector", "tensor_scalar", otc[:, 0, qh * 64:(qh + 1) * 64],
                     PS1[:, 0:64], recd[:, 0:1], None, ALU.mult, wait=[("V", v_rc)])
                v_o = p.op("vector", "tensor_scalar", otc[:, 1, qh * 64:(qh + 1) * 64],
                           PS1[:, 65:129], recd[:, 1:2], None, ALU.mult, inc="V")
            p.op("tensor", "matmul", PS4[0:64, 0:65], kfc[:, 0, kvh * 64:(kvh + 1) * 64],
                 vag[:, 0, kvh, :], wait=[("V", p.val("V"))], start=True, stop=False)
            t_st = p.op("tensor", "matmul", PS4[0:64, 0:65], kfc[:, 1, kvh * 64:(kvh + 1) * 64],
                        vag[:, 1, kvh, :], start=False, stop=True, inc="T")
            v_sa = p.op("vector", "tensor_tensor", Saug[:, kvh, :], Saug[:, kvh, :], PS4[0:64, 0:65],
                        ALU.add, wait=[("T", t_st)], inc="V")
            v_s = p.op("vector", "tensor_copy", Sbf0[:], Saug[:, kvh, :], wait=[("V", v_sa)], inc="V")
            t_dp = p.op("tensor", "matmul", PS3[:, 0:65], dupA[:], Sbf0[:],
                        wait=[("V", v_s)], start=True, stop=True, inc="T")
            p.op("vector", "tensor_copy", Sbf2[b:b + 64, kvh // 2, :], PS3[b:b + 64, 0:65],
                 wait=[("T", t_dp)])

        for hh in range(2):
            dbg_tap(cx, "ot", otc[:, hh, :], ((2 * ch + hh) * 128, (2 * ch + hh) * 128 + 128),
                    wait=[("V", p.val("V"))])
        v_scan = p.val("V")
        v_gate = v_scan
        for half in range(2):
            tau = 2 * ch + half
            for c in range(8):
                w = [("V", v_gate)] if c == 0 else []
                tt = p.op("tensor", "transpose", PSB[:, c * 128:(c + 1) * 128],
                          otc[:, half, c * 128:(c + 1) * 128], cx.eye_b[:], wait=w, inc="T")
            v_oT = p.op("vector", "tensor_copy", oTt[:].rearrange("p a b -> p (a b)"), PSB[:, 0:1024],
                        wait=[("T", tt)], inc="V")
            v_gate = v_oT
            t0 = p.op("tensor", "matmul", PS0[:, 0:512], ones_b[:, 0:128], brow[:, 1536:2048],
                      wait=[("V", v_oT)], start=True, stop=False)
            p.op("tensor", "matmul", PS1[:, 0:512], ones_b[:, 0:128], brow[:, 2048:2560],
                 start=True, stop=False)
            for k in range(8):
                sp = (k == 7)
                p.op("tensor", "matmul", PS0[:, 0:512], oTt[:, k, :],
                     Wo[:, k, 0:512], start=False, stop=sp)
                t_y = p.op("tensor", "matmul", PS1[:, 0:512], oTt[:, k, :],
                           Wo[:, k, 512:1024], start=False, stop=sp, inc="T")
            a_ep = p.op("vector", "tensor_scalar", lnep[:], xt[half][:],
                        stt[:, tau, 1:2], stt[:, tau, 2:3], ALU.mult, ALU.add,
                        wait=[("V", v_scan), ("V", ep_v.get(tau - 1, 0))], inc="V")
            p.op("vector", "tensor_tensor", ysb[:, 0:512], PS0[:, 0:512], lnep[:, 0:512], ALU.add,
                 wait=[("T", t_y), ("V", a_ep)])
            v_y = p.op("vector", "tensor_tensor", ysb[:, 512:1024], PS1[:, 0:512], lnep[:, 512:1024],
                       ALU.add, inc="V")
            dbg_tap(cx, "y", ysb[:], (tau * 128, tau * 128 + 128), wait=[("V", v_y)])
            mu = stt[:, tau, 0:1]; rstd = stt[:, tau, 1:2]; nmr = stt[:, tau, 2:3]; tmp = stt[:, tau, 3:4]
            vr = ln_stats2(cx, ysb[:], sqb2[:], mu, rstd, nmr, tmp,
                           wait_v=[("V", v_y)], wait_a=[("V", v_y)])
            a_z = p.op("vector", "tensor_scalar", lnep[:], ysb[:], rstd, nmr,
                       ALU.mult, ALU.add, wait=[("V", vr)], inc="V")
            a_z = p.op("vector", "tensor_tensor", lnep[:], lnep[:], gamb[:], ALU.mult,
                       wait=[("V", a_z)], inc="V")
            a_z = p.op("vector", "tensor_tensor", lnep[:], lnep[:], betb[:], ALU.add,
                       wait=[("V", a_z)], inc="V")
            wst = [("ST", st_v[tau - 2])] if tau - 2 in st_v else []
            v_ep = p.op("vector", "tensor_tensor", Rt[half][:], lnep[:], xt[half][:], ALU.add,
                        wait=wst + [("V", a_z)], inc="V")
            ep_v[tau] = v_ep
            st_v[tau] = p.op("sync", "dma_start", dst_dram[tau * 128:(tau + 1) * 128, :], Rt[half][:],
                             wait=[("V", v_ep)], inc="ST", incn=16)
    p.barrier()


def emit_l0f(cx, src_dram, dst_dram):
    p, nc, ar = cx.p, cx.nc, cx.ar
    stage_fence(cx)
    ar.reset()
    d = cx.dram
    PS = [cx.PS0, cx.PS1, cx.PS2, cx.PS3, cx.PS4, cx.PS5]
    PSR, PSB = cx.PSR, cx.PSB
    ones_b = cx.ones_b

    gwT = ar.tile("gwT", [128, 8, 8], BF16)
    sgT = ar.tile("sgT", [128, 8, 512], BF16)
    suT = ar.tile("suT", [128, 8, 512], BF16)
    sdT = ar.tile("sdT", [128, 4, 1024], BF16)
    wg_b = [ar.tile(f"wg{i}", [128, 8, 512], BF16) for i in range(2)]
    wu_b = [ar.tile(f"wu{i}", [128, 8, 512], BF16) for i in range(2)]
    wd_b = [ar.tile(f"wd{i}", [128, 4, 1024], BF16) for i in range(2)]

    p.op("gpsimd", "dma_start", gwT[:], d["T_gwT"].rearrange("(k p) n -> p k n", p=128), inc="WT", incn=16)
    p.op("gpsimd", "dma_start", sgT[:], d["T_sgT"].rearrange("(k p) n -> p k n", p=128), inc="WT", incn=16)
    p.op("gpsimd", "dma_start", suT[:], d["T_suT"].rearrange("(k p) n -> p k n", p=128), inc="WT", incn=16)
    wt_sh = p.op("gpsimd", "dma_start", sdT[:], d["T_sdT"].rearrange("(k p) n -> p k n", p=128), inc="WT", incn=16)

    def load_expert(e, buf):
        p.op("gpsimd", "dma_start", wg_b[buf][:], d["T_wgT"][e].rearrange("(k p) n -> p k n", p=128), inc="WT", incn=16)
        p.op("gpsimd", "dma_start", wu_b[buf][:], d["T_wuT"][e].rearrange("(k p) n -> p k n", p=128), inc="WT", incn=16)
        return p.op("gpsimd", "dma_start", wd_b[buf][:], d["T_wdT"][e].rearrange("(k p) n -> p k n", p=128), inc="WT", incn=16)

    xtg = ar.tile("xtg", [128, 3, 1024], F32)
    sqb = ar.tile("sqb", [128, 1024], BF16)
    lnb = ar.tile("lnb", [128, 1024], BF16)
    xTg = ar.tile("xTg", [128, 3, 8, 128], BF16)
    stt = ar.tile("stt", [128, 3, 4], F32)
    smf = ar.tile("smf", [128, 3, 8], F32)
    cw = ar.tile("cw", [128, 3, 8], F32)
    rnk = ar.tile("rnk", [128, 8], F32)
    t8 = ar.tile("t8", [128, 8], F32)
    den8 = ar.tile("den8", [128, 2], F32)
    sgl = ar.tile("sgl", [128, 512], BF16)
    hx = ar.tile("hx", [128, 512], BF16)
    hT = ar.tile("hT", [128, 4, 128], BF16)
    Rt = [ar.tile(f"Rt{i}", [128, 1024], F32) for i in range(2)]

    groups = [list(range(g, min(g + 3, NT))) for g in range(0, NT, 3)]
    wt_e = {}
    t_wuse = {}
    v_ep_prev = 0
    st_v = {}
    nst = 0

    for gi, tiles in enumerate(groups):
        G = len(tiles)
        for e in range(2):
            if gi == 0 and e < 2:
                pass
        a_ln_prev = p.val("T")
        v_ln = {}
        for ti, tau in enumerate(tiles):
            ld = p.op("sync", "dma_start", xtg[:, ti, :], src_dram[tau * 128:(tau + 1) * 128, :],
                      wait=[("V", v_ep_prev)], inc="LD", incn=16)
            mu = stt[:, ti, 0:1]; rstd = stt[:, ti, 1:2]; nmr = stt[:, ti, 2:3]; tmp = stt[:, ti, 3:4]
            vr = ln_stats2(cx, xtg[:, ti, :], sqb[:], mu, rstd, nmr, tmp,
                           wait_v=[("LD", ld)], wait_a=[("LD", ld)])
            a_ln = p.op("vector", "tensor_scalar", lnb[:], xtg[:, ti, :], rstd, nmr,
                        ALU.mult, ALU.add, wait=[("V", vr), ("T", p.val("T"))], inc="V")
            for c in range(8):
                w = [("V", a_ln)] if c == 0 else []
                tt = p.op("tensor", "transpose", PSB[:, c * 128:(c + 1) * 128],
                          lnb[:, c * 128:(c + 1) * 128], cx.eye_b[:], wait=w, inc="T")
            v_xt = p.op("vector", "tensor_copy", xTg[:, ti, :, :].rearrange("p a b -> p (a b)"),
                        PSB[:, 0:1024], wait=[("T", tt)], inc="V")
            for k in range(8):
                w = [("V", v_xt), ("WT", wt_sh)] if k == 0 else []
                t_s = p.op("tensor", "matmul", PS[0][:, ti * 8:(ti + 1) * 8], xTg[:, ti, k, :],
                           gwT[:, k, :], wait=w, start=(k == 0), stop=(k == 7), inc="T")
            v1 = p.op("vector", "reduce_max", den8[:, 0:1], PS[0][:, ti * 8:(ti + 1) * 8],
                      axis=AX.X, negate=True, wait=[("T", t_s)], inc="V")
            a_e = p.op("scalar", "activation", smf[:, ti, :], PS[0][:, ti * 8:(ti + 1) * 8],
                       AF.Exp, bias=den8[:, 0:1], accum_out=den8[:, 1:2],
                       wait=[("V", v1)], inc="A")
            v2 = p.op("vector", "reciprocal", den8[:, 0:1], den8[:, 1:2], wait=[("A", a_e)], inc="V")
            v3 = p.op("vector", "tensor_scalar", smf[:, ti, :], smf[:, ti, :], den8[:, 0:1], None,
                      ALU.mult, wait=[("V", v2)], inc="V")
            vm = p.op("vector", "memset", rnk[:], 0.0, inc="V")
            for e in range(8):
                vc = p.op("vector", "tensor_scalar", t8[:], smf[:, ti, :], smf[:, ti, e:e + 1], None,
                          ALU.is_lt, wait=[("V", v3 if e == 0 else vm)], inc="V")
                vm = p.op("vector", "tensor_tensor", rnk[:], rnk[:], t8[:], ALU.add,
                          wait=[("V", vc)], inc="V")
            vs = p.op("vector", "tensor_scalar", t8[:], rnk[:], 1.5, None, ALU.is_lt,
                      wait=[("V", vm)], inc="V")
            v_cw = p.op("vector", "tensor_tensor", cw[:, ti, :], smf[:, ti, :], t8[:], ALU.mult,
                        wait=[("V", vs)], inc="V")
            v_ln[ti] = v_cw

        def gu_h(ti, wgt, wut, cw_ap, wait_first):
            for k in range(8):
                w = list(wait_first) if k == 0 else []
                t_g = p.op("tensor", "matmul", PSR[:, 0:512], xTg[:, ti, k, :], wgt[:, k, :],
                           wait=w, start=(k == 0), stop=(k == 7), inc="T" if k == 7 else None)
            a_s = p.op("scalar", "activation", sgl[:], PSR[:, 0:512], AF.Silu,
                       wait=[("T", t_g)], inc="A")
            for k in range(8):
                w = [("A", a_s)] if k == 0 else []
                t_u = p.op("tensor", "matmul", PSR[:, 0:512], xTg[:, ti, k, :], wut[:, k, :],
                           wait=w, start=(k == 0), stop=(k == 7), inc="T" if k == 7 else None)
            if cw_ap is None:
                v_h = p.op("vector", "tensor_tensor", hx[:], PSR[:, 0:512], sgl[:], ALU.mult,
                           wait=[("T", t_u)], inc="V")
            else:
                v_h = p.op("vector", "scalar_tensor_tensor", hx[:], PSR[:, 0:512], cw_ap, sgl[:],
                           ALU.mult, ALU.mult, wait=[("T", t_u)], inc="V")
            return v_h

        def h_down(ti, wdt, v_h, first, last):
            for c in range(4):
                w = [("V", v_h)] if c == 0 else []
                tt = p.op("tensor", "transpose", PSB[:, c * 128:(c + 1) * 128],
                          hx[:, c * 128:(c + 1) * 128], cx.eye_b[:], wait=w, inc="T")
            v_hT = p.op("vector", "tensor_copy", hT[:].rearrange("p a b -> p (a b)"),
                        PSB[:, 0:512], wait=[("T", tt)], inc="V")
            t_y = None
            for c in range(4):
                w = [("V", v_hT)] if c == 0 else []
                sp = last and c == 3
                p.op("tensor", "matmul", PS[2 * ti][:, 0:512], hT[:, c, :], wdt[:, c, 0:512],
                     wait=w, start=(first and c == 0), stop=sp)
                t_y = p.op("tensor", "matmul", PS[2 * ti + 1][:, 0:512], hT[:, c, :], wdt[:, c, 512:1024],
                           start=(first and c == 0), stop=sp, inc="T")
            return t_y

        t_y_tile = {}
        for ti in range(G):
            v_h = gu_h(ti, sgT, suT, None, [("V", v_ln[ti]), ("V", p.val("V"))])
            t_y_tile[ti] = h_down(ti, sdT, v_h, True, False)
        for e in range(8):
            buf = e % 2
            if gi == 0 and e < 2:
                wt_e[buf] = load_expert(e, buf)
            for ti in range(G):
                cw_ap = cw[:, ti, e:e + 1]
                wf = [("WT", wt_e[buf]), ("V", p.val("V"))]
                v_h = gu_h(ti, wg_b[buf], wu_b[buf], cw_ap, wf)
                t_y_tile[ti] = h_down(ti, wd_b[buf], v_h, False, e == 7)
            nxt = e + 2 if e < 6 else ((e - 6) if gi + 1 < len(groups) else None)
            if nxt is not None:
                p.wait("gpsimd", [("T", p.val("T"))])
                wt_e[buf] = load_expert(nxt, buf)
        for ti, tau in enumerate(tiles):
            wst = [("ST", st_v[nst - 2])] if nst - 2 in st_v else []
            p.op("vector", "tensor_tensor", Rt[nst % 2][:, 0:512], PS[2 * ti][:, 0:512],
                 xtg[:, ti, 0:512], ALU.add, wait=wst + [("T", t_y_tile[ti])])
            v_ep = p.op("vector", "tensor_tensor", Rt[nst % 2][:, 512:1024], PS[2 * ti + 1][:, 0:512],
                        xtg[:, ti, 512:1024], ALU.add, inc="V")
            st_v[nst] = p.op("sync", "dma_start", dst_dram[tau * 128:(tau + 1) * 128, :],
                             Rt[nst % 2][:], wait=[("V", v_ep)], inc="ST", incn=16)
            nst += 1
            v_ep_prev = v_ep
    p.barrier()


def emit_l1a(cx, src_dram, dst_dram):
    p, nc, ar = cx.p, cx.nc, cx.ar
    stage_fence(cx)
    ar.reset()
    d = cx.dram
    PS0, PS1, PS2, PS3, PS4, PS5 = cx.PS0, cx.PS1, cx.PS2, cx.PS3, cx.PS4, cx.PS5
    PSB = cx.PSB
    ones_b, ones_f = cx.ones_b, cx.ones_f

    WgT = ar.tile("WgT", [128, 8, 1024], BF16)
    Wu = ar.tile("Wu", [128, 8, 16], BF16)
    Wsf = ar.tile("Wsf", [128, 8, 16], F32)
    Wif = ar.tile("Wif", [128, 8, 16], F32)
    WoTf = ar.tile("WoTf", [16, 1024], F32)
    ATf = ar.tile("ATf", [16, 16], F32)
    borow = ar.tile("borow", [1, 1024], F32)
    bgrow = ar.tile("bgrow", [1, 1024], BF16)
    bsrow = ar.tile("bsrow", [1, 16], F32)
    birow = ar.tile("birow", [1, 16], F32)
    burow = ar.tile("burow", [1, 16], BF16)
    lnwb = ar.tile("lnwb", [128, 1024], F32)
    lnbb = ar.tile("lnbb", [128, 1024], F32)
    grow = ar.tile("grow", [1, 2048], F32)
    UT = ar.tile("UT", [16, S], F32)
    H = ar.tile("H", [16, S], F32)
    h0 = ar.tile("h0", [16, 1], F32)

    p.op("gpsimd", "dma_start", WgT[:], d["T_WgT"].rearrange("(k p) n -> p k n", p=128), inc="WT", incn=16)
    wt_g = p.op("gpsimd", "dma_start", bgrow[:], d["L1_attn_bg"].unsqueeze(0), inc="WT", incn=16)
    p.op("sync", "dma_start", Wsf[:], d["T_WsT"].rearrange("(k p) n -> p k n", p=128), inc="LD", incn=16)
    p.op("sync", "dma_start", Wif[:], d["T_WiT"].rearrange("(k p) n -> p k n", p=128), inc="LD", incn=16)
    p.op("sync", "dma_start", WoTf[:], d["T_WoT1"], inc="LD", incn=16)
    p.op("sync", "dma_start", ATf[:], d["T_AT"], inc="LD", incn=16)
    p.op("sync", "dma_start", borow[:], d["L1_attn_bo"].unsqueeze(0), inc="LD", incn=16)
    p.op("sync", "dma_start", bsrow[:], d["L1_attn_bs"].unsqueeze(0), inc="LD", incn=16)
    p.op("sync", "dma_start", birow[:], d["L1_attn_bi"].unsqueeze(0), inc="LD", incn=16)
    p.op("sync", "dma_start", grow[:, 0:1024], d["L1_attn_ln_w"].unsqueeze(0), inc="LD", incn=16)
    ld_w = p.op("sync", "dma_start", grow[:, 1024:2048], d["L1_attn_ln_b"].unsqueeze(0), inc="LD", incn=16)
    v_wu = p.op("vector", "tensor_tensor", Wu[:], Wsf[:], Wif[:], ALU.add,
                wait=[("LD", ld_w)], inc="V")
    v_bu = p.op("vector", "tensor_tensor", burow[:], bsrow[:], birow[:], ALU.add, inc="V")
    p.op("vector", "memset", h0[:], 0.0)
    t0 = p.op("tensor", "matmul", PS0[:, 0:512], ones_f[:, 0:128], grow[:, 0:512],
              wait=[("LD", ld_w), ("WT", wt_g)], start=True, stop=True)
    p.op("tensor", "matmul", PS1[:, 0:512], ones_f[:, 0:128], grow[:, 512:1024], start=True, stop=True)
    p.op("tensor", "matmul", PS2[:, 0:512], ones_f[:, 0:128], grow[:, 1024:1536], start=True, stop=True)
    t0 = p.op("tensor", "matmul", PS3[:, 0:512], ones_f[:, 0:128], grow[:, 1536:2048], start=True, stop=True, inc="T")
    p.op("vector", "tensor_copy", lnwb[:, 0:512], PS0[:, 0:512], wait=[("T", t0)])
    p.op("vector", "tensor_copy", lnwb[:, 512:1024], PS1[:, 0:512])
    p.op("vector", "tensor_copy", lnbb[:, 0:512], PS2[:, 0:512])
    p.op("vector", "tensor_copy", lnbb[:, 512:1024], PS3[:, 0:512], inc="V")
    p.barrier()

    xt = [ar.tile(f"xt{i}", [128, 1024], F32) for i in range(2)]
    sqb = ar.tile("sqb", [128, 1024], BF16)
    ln2 = ar.tile("ln2", [128, 1024], F32)
    xnb = ar.tile("xnb", [128, 1024], BF16)
    xnT = ar.tile("xnT", [128, 8, 128], BF16)
    usb = ar.tile("usb", [128, 16], F32)
    stt = ar.tile("stt", [128, NT, 8], F32)
    g_sb = ar.tile("g_sb", [128, 1024], F32)
    t1f = ar.tile("t1f", [128, 1024], F32)
    Rt = [ar.tile(f"Rt{i}", [128, 1024], F32) for i in range(2)]

    def two_ln(tau, half, xt_ap, first_wait):
        mu = stt[:, tau, 4:5]; tmp = stt[:, tau, 5:6]
        r1 = stt[:, tau, 0:1]; n1 = stt[:, tau, 1:2]
        r2 = stt[:, tau, 2:3]; n2 = stt[:, tau, 3:4]
        vr1 = ln_stats2(cx, xt_ap, sqb[:], mu, r1, n1, tmp, wait_v=first_wait, wait_a=first_wait)
        v_l2 = p.op("vector", "tensor_scalar", ln2[:], xt_ap, r1, n1, ALU.mult, ALU.add,
                    wait=[("V", vr1)], inc="V")
        vr2 = ln_stats2(cx, ln2[:], sqb[:], mu, r2, n2, tmp,
                        wait_v=[("V", v_l2)], wait_a=[("V", v_l2)])
        v_xr = p.op("vector", "tensor_scalar", t1f[:], ln2[:], r2, n2, ALU.mult, ALU.add,
                    wait=[("V", vr2)], inc="V")
        v_xa = p.op("vector", "tensor_tensor", t1f[:], t1f[:], lnwb[:], ALU.mult,
                    wait=[("V", v_xr)], inc="V")
        v_xn = p.op("vector", "tensor_tensor", xnb[:], t1f[:], lnbb[:], ALU.add,
                    wait=[("V", v_xa)], inc="V")
        return v_xn

    def xn_transposes(v_xn):
        for c in range(8):
            w = [("V", v_xn)] if c == 0 else []
            tt = p.op("tensor", "transpose", PSB[:, c * 128:(c + 1) * 128],
                      xnb[:, c * 128:(c + 1) * 128], cx.eye_b[:], wait=w, inc="T")
        return p.op("vector", "tensor_copy", xnT[:].rearrange("p a b -> p (a b)"),
                    PSB[:, 0:1024], wait=[("T", tt)], inc="V")

    ep_v = {}
    for tau in range(NT):
        w = [("V", ep_v[tau - 2])] if tau - 2 in ep_v else [("V", p.val("V"))]
        ld = p.op("sync", "dma_start", xt[tau % 2][:], src_dram[tau * 128:(tau + 1) * 128, :],
                  wait=w, inc="LD", incn=16)
        v_xn = two_ln(tau, tau % 2, xt[tau % 2][:], [("LD", ld), ("T", p.val("T"))])
        v_xT = xn_transposes(v_xn)
        p.op("tensor", "matmul", PS0[:, 0:16], ones_b[:, 0:128], burow[:],
             wait=[("V", v_xT), ("V", v_bu), ("WT", wt_g), ("V", p.val("V"))],
             start=True, stop=False)
        for k in range(8):
            t_u = p.op("tensor", "matmul", PS0[:, 0:16], xnT[:, k, :], Wu[:, k, :],
                       start=False, stop=(k == 7), inc="T" if k == 7 else None)
        v_u = p.op("vector", "tensor_copy", usb[:], PS0[:, 0:16], wait=[("T", t_u)], inc="V")
        t_uT = p.op("tensor", "transpose", PS1[0:16, 0:128], usb[:], cx.eye_f[:],
                    wait=[("V", v_u)], inc="T")
        v_UT = p.op("vector", "tensor_copy", UT[:, tau * 128:(tau + 1) * 128], PS1[0:16, 0:128],
                    wait=[("T", t_uT)], inc="V")
        ep_v[tau] = v_UT
    p.barrier()

    a_prev = p.val("A")
    t_prev = None
    for t in range(S):
        wmm = [("A", p.val("A"))] if t == 0 else [("A", a_prev)]
        psr = (PS2 if t % 2 == 0 else PS3)
        rhs = h0[:] if t == 0 else H[:, t - 1:t]
        t_mm = p.op("tensor", "matmul", psr[0:16, 0:1], ATf[:], rhs, wait=wmm,
                    start=True, stop=True, inc="T")
        a_prev = p.op("scalar", "activation", H[:, t:t + 1], psr[0:16, 0:1], AF.Tanh,
                      bias=UT[:, t:t + 1], wait=[("T", t_mm)], inc="A")
    p.barrier()

    st_v = {}
    ep_v = {}
    for tau in range(NT):
        w = [("V", ep_v[tau - 2])] if tau - 2 in ep_v else [("V", p.val("V"))]
        ld = p.op("sync", "dma_start", xt[tau % 2][:], src_dram[tau * 128:(tau + 1) * 128, :],
                  wait=w, inc="LD", incn=16)
        r1 = stt[:, tau, 0:1]; n1 = stt[:, tau, 1:2]
        r2 = stt[:, tau, 2:3]; n2 = stt[:, tau, 3:4]
        v_l2 = p.op("vector", "tensor_scalar", ln2[:], xt[tau % 2][:], r1, n1, ALU.mult, ALU.add,
                    wait=[("LD", ld), ("T", p.val("T")), ("V", p.val("V"))], inc="V")
        v_xr = p.op("vector", "tensor_scalar", t1f[:], ln2[:], r2, n2, ALU.mult, ALU.add,
                    wait=[("V", v_l2)], inc="V")
        v_xa = p.op("vector", "tensor_tensor", t1f[:], t1f[:], lnwb[:], ALU.mult,
                    wait=[("V", v_xr)], inc="V")
        v_xn = p.op("vector", "tensor_tensor", xnb[:], t1f[:], lnbb[:], ALU.add,
                    wait=[("V", v_xa)], inc="V")
        v_xT = xn_transposes(v_xn)
        p.op("tensor", "matmul", PS4[:, 0:512], ones_b[:, 0:128], bgrow[:, 0:512],
             wait=[("V", v_xT)], start=True, stop=False)
        p.op("tensor", "matmul", PS5[:, 0:512], ones_b[:, 0:128], bgrow[:, 512:1024],
             start=True, stop=False)
        for k in range(8):
            p.op("tensor", "matmul", PS4[:, 0:512], xnT[:, k, :], WgT[:, k, 0:512],
                 start=False, stop=(k == 7))
            t_gt = p.op("tensor", "matmul", PS5[:, 0:512], xnT[:, k, :], WgT[:, k, 512:1024],
                        start=False, stop=(k == 7), inc="T" if k == 7 else None)
        a_g1 = p.op("scalar", "activation", g_sb[:, 0:512], PS4[:, 0:512], AF.Sigmoid,
                    wait=[("T", t_gt), ("V", ep_v.get(tau - 1, 0))], inc="A")
        a_g = p.op("scalar", "activation", g_sb[:, 512:1024], PS5[:, 0:512], AF.Sigmoid, inc="A")
        p.op("tensor", "matmul", PS0[:, 0:512], ones_f[:, 0:128], borow[:, 0:512],
             wait=[("V", p.val("V"))], start=True, stop=False)
        p.op("tensor", "matmul", PS0[:, 0:512], H[:, tau * 128:(tau + 1) * 128], WoTf[:, 0:512],
             start=False, stop=True)
        p.op("tensor", "matmul", PS1[:, 0:512], ones_f[:, 0:128], borow[:, 512:1024],
             start=True, stop=False)
        t_o = p.op("tensor", "matmul", PS1[:, 0:512], H[:, tau * 128:(tau + 1) * 128],
                   WoTf[:, 512:1024], start=False, stop=True, inc="T")
        wst = [("ST", st_v[tau - 2])] if tau - 2 in st_v else []
        p.op("vector", "tensor_tensor", t1f[:, 0:512], PS0[:, 0:512], g_sb[:, 0:512], ALU.mult,
             wait=[("T", t_o), ("A", a_g)] + wst)
        v_t1 = p.op("vector", "tensor_tensor", t1f[:, 512:1024], PS1[:, 0:512], g_sb[:, 512:1024],
                    ALU.mult, inc="V")
        v_t2 = p.op("vector", "tensor_tensor", t1f[:], t1f[:], ln2[:], ALU.add,
                    wait=[("V", v_t1)], inc="V")
        v_ep = p.op("vector", "tensor_tensor", Rt[tau % 2][:], t1f[:], xt[tau % 2][:], ALU.add,
                    wait=[("V", v_t2)], inc="V")
        ep_v[tau] = v_ep
        st_v[tau] = p.op("sync", "dma_start", dst_dram[tau * 128:(tau + 1) * 128, :],
                         Rt[tau % 2][:], wait=[("V", v_ep)], inc="ST", incn=16)
    p.barrier()


def emit_l1f_init(cx, src_dram, dst_dram):
    p, ar = cx.p, cx.ar
    stage_fence(cx)
    ar.reset()
    xt = [ar.tile(f"xt{i}", [128, 1024], F32) for i in range(2)]
    ot = [ar.tile(f"ot{i}", [128, 1024], F32) for i in range(2)]
    sqb = ar.tile("sqb", [128, 1024], BF16)
    stt = ar.tile("stt", [128, 4], F32)
    st_v = {}
    ep_v = {}
    for tau in range(NT):
        w = [("V", ep_v[tau - 2])] if tau - 2 in ep_v else []
        ld = p.op("sync", "dma_start", xt[tau % 2][:], src_dram[tau * 128:(tau + 1) * 128, :],
                  wait=w, inc="LD", incn=16)
        mu = stt[:, 0:1]; rstd = stt[:, 1:2]; nmr = stt[:, 2:3]; tmp = stt[:, 3:4]
        vr = ln_stats2(cx, xt[tau % 2][:], sqb[:], mu, rstd, nmr, tmp,
                       wait_v=[("LD", ld)], wait_a=[("LD", ld)])
        wst = [("ST", st_v[tau - 2])] if tau - 2 in st_v else []
        v_ep = p.op("vector", "tensor_scalar", ot[tau % 2][:], xt[tau % 2][:], rstd, nmr,
                    ALU.mult, ALU.add, wait=[("V", vr)] + wst, inc="V")
        ep_v[tau] = v_ep
        st_v[tau] = p.op("sync", "dma_start", dst_dram[tau * 128:(tau + 1) * 128, :],
                         ot[tau % 2][:], wait=[("V", v_ep)], inc="ST", incn=16)
    p.barrier()


def emit_l1f_attn(cx, lay, x2):
    p, nc, ar, d = cx.p, cx.nc, cx.ar, cx.dram
    stage_fence(cx)
    ar.reset()
    PS0, PS1, PS2, PS3, PS4, PS5 = cx.PS0, cx.PS1, cx.PS2, cx.PS3, cx.PS4, cx.PS5
    PSR, PSB = cx.PSR, cx.PSB
    ones_b = cx.ones_b

    Wqkv = ar.tile("Wqkv", [128, 8, 3072], BF16)
    Wo = ar.tile("Wo", [128, 8, 1024], BF16)
    tabs = ar.tile("tabs", [128, 4, 32, 64], BF16)
    tbl = ar.tile("tbl", [128, 32, 64], BF16)
    srow = ar.tile("srow", [1, 4, 64], BF16)
    sb64 = ar.tile("sb64", [128, 4, 64], BF16)
    mask1 = ar.tile("mask1", [128, 384], F32)
    kTr = ar.tile("kTr", [128, 8, 3, 128], BF16)
    vr_ = ar.tile("vr", [128, 3, 1024], BF16)
    xs = [ar.tile(f"xs{i}", [128, 512], F32) for i in range(2)]
    x2u = [ar.tile(f"x2u{i}", [128, 512], F32) for i in range(3)]
    S1f = ar.tile("S1f", [128, 128], F32)
    E127f = ar.tile("E127f", [128, 128], F32)
    xsb = ar.tile("xsb", [128, 1024], BF16)
    xsT = ar.tile("xsT", [128, 8, 128], BF16)
    q_t = ar.tile("q_t", [128, 1024], F32)
    k_t = ar.tile("k_t", [128, 1024], F32)
    sqq = ar.tile("sqq", [128, 1024], BF16)
    ss16 = ar.tile("ss16", [128, 16], F32)
    l2r = ar.tile("l2r", [128, 16], F32)
    qn = ar.tile("qn", [128, 1024], BF16)
    ta = ar.tile("ta", [128, 1024], BF16)
    tb = ar.tile("tb", [128, 1024], BF16)
    qp = ar.tile("qp", [128, 1024], BF16)
    qT = ar.tile("qT", [128, 8, 128], BF16)
    smf = ar.tile("smf", [128, 384], F32)
    pb = ar.tile("pb", [128, 384], BF16)
    pT = ar.tile("pT", [128, 3, 128], BF16)
    nmd = ar.tile("nmd", [128, 4], F32)
    o_t = ar.tile("o_t", [128, 1024], BF16)
    oT = ar.tile("oT", [128, 8, 128], BF16)
    X2n = [ar.tile(f"X2n{i}", [128, 1024], F32) for i in range(2)]

    p.op("gpsimd", "dma_start", Wqkv[:], d["L1_ffn_Wqkv"][lay].rearrange("(k p) n -> p k n", p=128),
         inc="WT", incn=16)
    p.op("gpsimd", "dma_start", Wo[:], d["L1_ffn_Wo"][lay].rearrange("(k p) n -> p k n", p=128),
         inc="WT", incn=16)
    p.op("gpsimd", "dma_start", mask1[:], d["T_mask1"], inc="WT", incn=16)
    p.op("sync", "dma_start", S1f[:], d["T_shift1"], inc="LD", incn=16)
    ld_sh = p.op("sync", "dma_start", E127f[:], d["T_e127"], inc="LD", incn=16)
    p.op("gpsimd", "dma_start", srow[:, 0, :], d["L1_ffn_q_scale"][lay].unsqueeze(0), inc="WT", incn=16)
    p.op("gpsimd", "dma_start", srow[:, 1, :], d["T_qsrot"][lay].unsqueeze(0), inc="WT", incn=16)
    p.op("gpsimd", "dma_start", srow[:, 2, :], d["L1_ffn_k_scale"][lay].unsqueeze(0), inc="WT", incn=16)
    wt_s = p.op("gpsimd", "dma_start", srow[:, 3, :], d["T_ksrot"][lay].unsqueeze(0), inc="WT", incn=16)
    for j in range(4):
        w = [("WT", wt_s)] if j == 0 else []
        t_b = p.op("tensor", "matmul", PS0[:, j * 64:(j + 1) * 64], ones_b[:, 0:128],
                   srow[:, j, :], wait=w, start=True, stop=True, inc="T")
    v_s = p.op("vector", "tensor_copy", sb64[:].rearrange("p a b -> p (a b)"), PS0[:, 0:256],
               wait=[("T", t_b)], inc="V")
    v_tb = v_s
    for j, src in enumerate(["T_CS", "T_SSgn", "T_CI", "T_SIgn"]):
        wld = p.op("gpsimd", "dma_start", tbl[:], d[src].rearrange("(t p) n -> p t n", p=128),
                   wait=[("V", v_tb)], inc="WT", incn=16)
        v_tb = p.op("vector", "tensor_tensor", tabs[:, j, :, :], tbl[:],
                    sb64[:, j, :].unsqueeze(1).broadcast_to((128, 32, 64)), ALU.mult,
                    wait=[("WT", wld), ("V", v_s)], inc="V")
        dbg_tap(cx, "l1f_tab", tabs[:, j, 0:16, :].rearrange("p a b -> p (a b)"),
                (j * 128, j * 128 + 128), wait=[("V", v_tb)])
    p.barrier()

    st_v = {}
    ld_v = {}
    ep_v = {}

    def load_xs(tau):
        w = [("V", ep_v[tau - 2])] if tau - 2 in ep_v else []
        p.op("sync", "dma_start", xs[tau % 2][:], x2[tau * 128:(tau + 1) * 128, 0:512],
             wait=w, inc="LD", incn=16)
        return p.op("sync", "dma_start", x2u[tau % 3][:], x2[tau * 128:(tau + 1) * 128, 512:1024],
                    inc="LD", incn=16)

    def rot_apply(src_f32, tA, tB, out_bf, wsem, v_in):
        a_sq = p.op("scalar", "activation", sqq[:], src_f32[:], AF.Square,
                    wait=[(wsem, v_in)], inc="A")
        v1 = p.op("vector", "reduce_sum", ss16[:],
                  sqq[:].rearrange("p (h f) -> p h f", h=16), axis=AX.X,
                  wait=[("A", a_sq)], inc="V")
        a_sr = p.op("scalar", "activation", ss16[:], ss16[:], AF.Sqrt,
                    wait=[("V", v1)], inc="A")
        v2 = p.op("vector", "tensor_scalar_max", ss16[:], ss16[:], 1e-12,
                  wait=[("A", a_sr)], inc="V")
        v3 = p.op("vector", "reciprocal", l2r[:], ss16[:], wait=[("V", v2)], inc="V")
        v4 = p.op("vector", "tensor_tensor", qn[:].rearrange("p (h f) -> p h f", h=16),
                  src_f32[:].rearrange("p (h f) -> p h f", h=16),
                  l2r[:].unsqueeze(2).broadcast_to((128, 16, 64)), ALU.mult,
                  wait=[("V", v3)], inc="V")
        v5 = p.op("vector", "tensor_tensor", ta[:].rearrange("p (h f) -> p h f", h=16),
                  qn[:].rearrange("p (h f) -> p h f", h=16),
                  tA.unsqueeze(1).broadcast_to((128, 16, 64)), ALU.mult,
                  wait=[("V", v4)], inc="V")
        p.op("vector", "tensor_tensor", tb[:].rearrange("p (h u f) -> p h u f", h=16, u=2)[:, :, 0, :],
             qn[:].rearrange("p (h u f) -> p h u f", h=16, u=2)[:, :, 1, :],
             tB[:, 0:32].unsqueeze(1).broadcast_to((128, 16, 32)), ALU.mult,
             wait=[("V", v4)])
        v6 = p.op("vector", "tensor_tensor", tb[:].rearrange("p (h u f) -> p h u f", h=16, u=2)[:, :, 1, :],
                  qn[:].rearrange("p (h u f) -> p h u f", h=16, u=2)[:, :, 0, :],
                  tB[:, 32:64].unsqueeze(1).broadcast_to((128, 16, 32)), ALU.mult, inc="V")
        v7 = p.op("vector", "tensor_tensor", out_bf[:], ta[:], tb[:], ALU.add,
                  wait=[("V", v6), ("V", v5)], inc="V")
        if DBG in ("l1f_qn", "l1f_ta", "l1f_tb") and out_bf is qp:
            src_map = {"l1f_qn": qn, "l1f_ta": ta, "l1f_tb": tb}
            dbg_tap(cx, DBG, src_map[DBG][:], (dbg_rows[0], dbg_rows[1]), wait=[("V", v7)])
        return v7

    def transp8(src_bf, dst, v_in, extra=()):
        for c in range(8):
            w = [("V", v_in)] + list(extra) if c == 0 else []
            tt = p.op("tensor", "transpose", PSB[:, c * 128:(c + 1) * 128],
                      src_bf[:, c * 128:(c + 1) * 128], cx.eye_b[:], wait=w, inc="T")
        return p.op("vector", "tensor_copy", dst, PSB[:, 0:1024].rearrange("p (a b) -> p a b", a=8),
                    wait=[("T", tt)], inc="V")

    ld_v[0] = load_xs(0)
    for tau in range(NT):
        slot = tau % 3
        ld = ld_v[tau]
        if tau + 1 < NT:
            ld_v[tau + 1] = load_xs(tau + 1)
        w0 = [("LD", ld), ("LD", ld_sh), ("V", p.val("V"))]
        if tau == 0:
            t_sh = p.op("tensor", "matmul", PSR[:, 0:512], S1f[:], x2u[0][:],
                        wait=w0, start=True, stop=True, inc="T")
        else:
            p.op("tensor", "matmul", PSR[:, 0:512], S1f[:], x2u[tau % 3][:],
                 wait=w0, start=True, stop=False)
            t_sh = p.op("tensor", "matmul", PSR[:, 0:512], E127f[:], x2u[(tau - 1) % 3][:],
                        start=False, stop=True, inc="T")
        v_xb = p.op("vector", "tensor_copy", xsb[:, 0:512], xs[tau % 2][:],
                    wait=[("LD", ld), ("T", p.val("T"))], inc="V")
        v_sh = p.op("vector", "tensor_copy", xsb[:, 512:1024], PSR[:, 0:512],
                    wait=[("T", t_sh)], inc="V")
        v_xT = transp8(xsb, xsT[:], v_sh)
        dbg_tap(cx, "l1f_xs", xsb[:], (tau * 128, tau * 128 + 128), wait=[("V", v_sh)])
        dbg_tap(cx, "l1f_xsb", xsb[:], (tau * 128, tau * 128 + 128), wait=[("V", v_xb)])
        dbg_tap(cx, "l1f_xsT", xsT[:].rearrange("p a b -> p (a b)"), (tau * 128, tau * 128 + 128),
                wait=[("V", v_xT)])
        for k in range(8):
            w = [("V", v_xT), ("WT", p.val("WT")), ("V", p.val("V"))] if k == 0 else []
            st = (k == 0)
            sp = (k == 7)
            p.op("tensor", "matmul", PS0[:, 0:512], xsT[:, k, :], Wqkv[:, k, 0:512],
                 wait=w, start=st, stop=sp)
            p.op("tensor", "matmul", PS1[:, 0:512], xsT[:, k, :], Wqkv[:, k, 512:1024], start=st, stop=sp)
            p.op("tensor", "matmul", PS2[:, 0:512], xsT[:, k, :], Wqkv[:, k, 1024:1536], start=st, stop=sp)
            p.op("tensor", "matmul", PS3[:, 0:512], xsT[:, k, :], Wqkv[:, k, 1536:2048], start=st, stop=sp)
            p.op("tensor", "matmul", PS4[:, 0:512], xsT[:, k, :], Wqkv[:, k, 2048:2560], start=st, stop=sp)
            t_qkv = p.op("tensor", "matmul", PS5[:, 0:512], xsT[:, k, :], Wqkv[:, k, 2560:3072],
                         start=st, stop=sp, inc="T" if sp else None)
        a_q1 = p.op("scalar", "copy", q_t[:, 0:512], PS0[:, 0:512], wait=[("T", t_qkv)], inc="A")
        a_q2 = p.op("scalar", "copy", q_t[:, 512:1024], PS1[:, 0:512], inc="A")
        p.op("vector", "tensor_copy", k_t[:, 0:512], PS2[:, 0:512], wait=[("T", t_qkv)])
        v_k = p.op("vector", "tensor_copy", k_t[:, 512:1024], PS3[:, 0:512], inc="V")
        p.op("vector", "tensor_copy", vr_[:, slot, 0:512], PS4[:, 0:512])
        v_v = p.op("vector", "tensor_copy", vr_[:, slot, 512:1024], PS5[:, 0:512], inc="V")
        dbg_rows = (tau * 128, tau * 128 + 128)
        dbg_tap(cx, "l1f_qt", q_t[:], (tau * 128, tau * 128 + 128), wait=[("A", a_q2)])
        v_qp = rot_apply(q_t, tabs[:, 0, tau, :], tabs[:, 1, tau, :], qp, "A", a_q2)
        v_qT = transp8(qp, qT[:], v_qp)
        v_kp = rot_apply(k_t, tabs[:, 2, tau, :], tabs[:, 3, tau, :], qp, "V", v_k)
        v_kT = transp8(qp, kTr[:, :, slot, :], v_kp)
        nc_ = min(tau + 1, 3)
        for qh in range(16):
            b = (qh % 2) * 64
            qsl = qT[b:b + 64, qh // 2, :]
            for j in range(nc_):
                ksl = kTr[b:b + 64, qh // 2, (tau - nc_ + 1 + j) % 3, :]
                w = [("V", v_kT), ("V", p.val("V"))] if j == 0 else []
                t_sim = p.op("tensor", "matmul", PS0[:, j * 128:(j + 1) * 128], qsl, ksl,
                             wait=w, start=True, stop=True, inc="T" if j == nc_ - 1 else None)
            mw = 384 - nc_ * 128
            v_m = p.op("vector", "scalar_tensor_tensor", smf[:, 0:nc_ * 128],
                       PS0[:, 0:nc_ * 128], 8.0, mask1[:, mw:384], ALU.mult, ALU.add,
                       wait=[("T", t_sim), ("A", p.val("A"))], inc="V")
            v_nm = p.op("vector", "reduce_max", nmd[:, 0:1], smf[:, 0:nc_ * 128], axis=AX.X,
                        negate=True, wait=[("V", v_m)], inc="V")
            a_e = p.op("scalar", "activation", pb[:, 0:nc_ * 128], smf[:, 0:nc_ * 128], AF.Exp,
                       bias=nmd[:, 0:1], accum_out=nmd[:, 1:2], wait=[("V", v_nm)], inc="A")
            v_rc = p.op("vector", "reciprocal", nmd[:, 2:3], nmd[:, 1:2], wait=[("A", a_e)], inc="V")
            if tau < 3 and qh == 15:
                dbg_tap(cx, "l1f_pb", pb[:, 0:nc_ * 128], (tau * 128, tau * 128 + 128),
                        wait=[("A", a_e)])
                dbg_tap(cx, "l1f_sm", smf[:, 0:nc_ * 128], (tau * 128, tau * 128 + 128),
                        wait=[("V", v_m)])
            for j in range(nc_):
                w = [("V", v_rc)] if j == 0 else []
                tt = p.op("tensor", "transpose", PSB[:, j * 128:(j + 1) * 128],
                          pb[:, j * 128:(j + 1) * 128], cx.eye_b[:], wait=w, inc="T")
            v_pT = p.op("vector", "tensor_copy", pT[:, 0:nc_, :].rearrange("p a b -> p (a b)"),
                        PSB[:, 0:nc_ * 128], wait=[("T", tt)], inc="V")
            for j in range(nc_):
                w = [("V", v_pT)] if j == 0 else []
                t_pv = p.op("tensor", "matmul", PS1[:, 0:64], pT[:, j, :],
                            vr_[:, (tau - nc_ + 1 + j) % 3, qh * 64:(qh + 1) * 64],
                            wait=w, start=(j == 0), stop=(j == nc_ - 1),
                            inc="T" if j == nc_ - 1 else None)
            v_o = p.op("vector", "tensor_scalar", o_t[:, qh * 64:(qh + 1) * 64], PS1[:, 0:64],
                       nmd[:, 2:3], None, ALU.mult, wait=[("T", t_pv), ("V", v_rc)], inc="V")
        dbg_tap(cx, "l1f_ot", o_t[:], (tau * 128, tau * 128 + 128), wait=[("V", v_o)])
        dbg_tap(cx, "l1f_qp", qT[:].rearrange("p a b -> p (a b)"), (tau * 128, tau * 128 + 128),
                wait=[("V", v_qT)])
        v_oT = transp8(o_t, oT[:], v_o)
        for k in range(8):
            w = [("V", v_oT)] if k == 0 else []
            st = (k == 0)
            sp = (k == 7)
            p.op("tensor", "matmul", PS2[:, 0:512], oT[:, k, :], Wo[:, k, 0:512],
                 wait=w, start=st, stop=sp)
            t_wo = p.op("tensor", "matmul", PS3[:, 0:512], oT[:, k, :], Wo[:, k, 512:1024],
                        start=st, stop=sp, inc="T" if sp else None)
        wst = [("ST", st_v[tau - 2])] if tau - 2 in st_v else []
        p.op("vector", "tensor_tensor", X2n[tau % 2][:, 0:512], PS2[:, 0:512],
             xs[tau % 2][:, 0:512], ALU.add, wait=[("T", t_wo)] + wst)
        v_ep = p.op("vector", "tensor_tensor", X2n[tau % 2][:, 512:1024], PS3[:, 0:512],
                    x2u[tau % 3][:], ALU.add, inc="V")
        ep_v[tau] = v_ep
        st_v[tau] = p.op("sync", "dma_start", x2[tau * 128:(tau + 1) * 128, :], X2n[tau % 2][:],
                         wait=[("V", v_ep)], inc="ST", incn=16)
        if DBG_SERIAL and tau < 3:
            stage_fence(cx)
    p.barrier()


def emit_l1f_ffn(cx, lay, x2):
    p, nc, ar, d = cx.p, cx.nc, cx.ar, cx.dram
    stage_fence(cx)
    ar.reset()
    PS = [cx.PS0, cx.PS1, cx.PS2, cx.PS3, cx.PS4, cx.PS5]
    PSR, PSB = cx.PSR, cx.PSB
    ones_b, ones_f = cx.ones_b, cx.ones_f
    NKF = 22

    W1p = ar.tile("W1p", [128, 8, 5632], BF16)
    W2s = [ar.tile(f"W2s{i}", [128, 1024], BF16) for i in range(2)]
    b1c = ar.tile("b1c", [128, 44], F32)
    b2row = ar.tile("b2row", [1, 1024], BF16)
    grow = ar.tile("grow", [1, 2048], F32)
    lnwb = ar.tile("lnwb", [128, 1024], F32)
    lnbb = ar.tile("lnbb", [128, 1024], F32)
    xsg = [ar.tile(f"xsg{i}", [128, 3, 1024], F32) for i in range(2)]
    x2u = [ar.tile(f"x2u{i}", [128, 3, 512], F32) for i in range(2)]
    S1f = ar.tile("S1f", [128, 128], F32)
    E127f = ar.tile("E127f", [128, 128], F32)
    sqb = ar.tile("sqb", [128, 1024], BF16)
    lnb = ar.tile("lnb", [128, 1024], BF16)
    xnT = ar.tile("xnT", [128, 8, 3, 128], BF16)
    stt = ar.tile("stt", [128, 3, 4], F32)
    a_sb = ar.tile("a_sb", [128, 384], BF16)
    gel = ar.tile("gel", [128, 384], BF16)
    hk = ar.tile("hk", [128, 384], BF16)
    hx_dbg = ar.tile("hx_dbg", [128, 512], F32)
    X2n = ar.tile("X2n", [128, 3, 1024], F32)

    p.op("gpsimd", "dma_start", W1p[:], d["T_W1p"][lay].rearrange("(k p) n -> p k n", p=128),
         inc="WT", incn=16)
    p.op("sync", "dma_start", b1c[:], d["T_b1c"][lay], inc="LD", incn=16)
    p.op("sync", "dma_start", S1f[:], d["T_shift1"], inc="LD", incn=16)
    ld_sh = p.op("sync", "dma_start", E127f[:], d["T_e127"], inc="LD", incn=16)
    wt_b = p.op("gpsimd", "dma_start", b2row[:], d["L1_ffn_b2"][lay].unsqueeze(0), inc="WT", incn=16)
    p.op("sync", "dma_start", grow[:, 0:1024], d["L1_ffn_ln_g"][lay].unsqueeze(0), inc="LD", incn=16)
    ld_g = p.op("sync", "dma_start", grow[:, 1024:2048], d["L1_ffn_ln_b"][lay].unsqueeze(0),
                inc="LD", incn=16)
    t0 = p.op("tensor", "matmul", cx.PS0[:, 0:512], ones_f[:, 0:128], grow[:, 0:512],
              wait=[("LD", ld_g)], start=True, stop=True)
    p.op("tensor", "matmul", cx.PS1[:, 0:512], ones_f[:, 0:128], grow[:, 512:1024], start=True, stop=True)
    p.op("tensor", "matmul", cx.PS2[:, 0:512], ones_f[:, 0:128], grow[:, 1024:1536], start=True, stop=True)
    t0 = p.op("tensor", "matmul", cx.PS3[:, 0:512], ones_f[:, 0:128], grow[:, 1536:2048],
              start=True, stop=True, inc="T")
    p.op("vector", "tensor_copy", lnwb[:, 0:512], cx.PS0[:, 0:512], wait=[("T", t0)])
    p.op("vector", "tensor_copy", lnwb[:, 512:1024], cx.PS1[:, 0:512])
    p.op("vector", "tensor_copy", lnbb[:, 0:512], cx.PS2[:, 0:512])
    p.op("vector", "tensor_copy", lnbb[:, 512:1024], cx.PS3[:, 0:512], inc="V")
    p.barrier()

    groups = [list(range(g, min(g + 3, NT))) for g in range(0, NT, 3)]
    st_v = {}
    nst = 0
    ep_v = 0
    ld_blk = {}
    w2_wt = {}

    def load_block(bi):
        tiles = groups[bi]
        buf = bi % 2
        last = None
        for ti, tau in enumerate(tiles):
            p.op("sync", "dma_start", xsg[buf][:, ti, 0:512],
                 x2[tau * 128:(tau + 1) * 128, 0:512], inc="LD", incn=16)
            last = p.op("sync", "dma_start", x2u[buf][:, ti, :],
                        x2[tau * 128:(tau + 1) * 128, 512:1024], inc="LD", incn=16)
        return last

    ld_blk[0] = load_block(0)
    w2_wt[0] = p.op("gpsimd", "dma_start", W2s[0][:], d["T_W2p"][lay][0:128, :], inc="WT", incn=16)
    for bi, tiles in enumerate(groups):
        G = len(tiles)
        buf = bi % 2
        ld = ld_blk[bi]
        v_shift = 0
        for ti, tau in enumerate(tiles):
            w0 = [("LD", ld), ("LD", ld_sh), ("V", p.val("V")), ("A", p.val("A"))]
            if tau == 0:
                t_sh = p.op("tensor", "matmul", PSR[:, 0:512], S1f[:], x2u[buf][:, 0, :],
                            wait=w0, start=True, stop=True, inc="T")
            else:
                prev = x2u[buf][:, ti - 1, :] if ti > 0 else x2u[1 - buf][:, len(groups[bi - 1]) - 1, :]
                p.op("tensor", "matmul", PSR[:, 0:512], S1f[:], x2u[buf][:, ti, :],
                     wait=w0, start=True, stop=False)
                t_sh = p.op("tensor", "matmul", PSR[:, 0:512], E127f[:], prev,
                            start=False, stop=True, inc="T")
            v_shift = p.op("vector", "tensor_copy", xsg[buf][:, ti, 512:1024], PSR[:, 0:512],
                           wait=[("T", t_sh), ("V", ep_v)], inc="V")
        v_xT = {}
        for ti in range(G):
            mu = stt[:, ti, 0:1]; rstd = stt[:, ti, 1:2]; nmr = stt[:, ti, 2:3]; tmp = stt[:, ti, 3:4]
            vr = ln_stats2(cx, xsg[buf][:, ti, :], sqb[:], mu, rstd, nmr, tmp,
                           wait_v=[("LD", ld), ("V", v_shift)], wait_a=[("LD", ld), ("V", v_shift)])
            v1 = p.op("vector", "tensor_scalar", lnb[:], xsg[buf][:, ti, :], rstd, nmr,
                      ALU.mult, ALU.add, wait=[("V", vr), ("T", p.val("T"))], inc="V")
            v2 = p.op("vector", "tensor_tensor", lnb[:], lnb[:], lnwb[:], ALU.mult,
                      wait=[("V", v1)], inc="V")
            v3 = p.op("vector", "tensor_tensor", lnb[:], lnb[:], lnbb[:], ALU.add,
                      wait=[("V", v2)], inc="V")
            dbg_tap(cx, "l1f_xn", lnb[:], (tiles[ti] * 128, tiles[ti] * 128 + 128),
                    wait=[("V", v3)])
            for c in range(8):
                w = [("V", v3)] if c == 0 else []
                tt = p.op("tensor", "transpose", PSB[:, c * 128:(c + 1) * 128],
                          lnb[:, c * 128:(c + 1) * 128], cx.eye_b[:], wait=w, inc="T")
            v_xT[ti] = p.op("vector", "tensor_copy", xnT[:, :, ti, :],
                            PSB[:, 0:1024].rearrange("p (a b) -> p a b", a=8),
                            wait=[("T", tt)], inc="V")
        if bi + 1 < len(groups):
            p.wait("sync", [("V", v_shift)])
            ld_blk[bi + 1] = load_block(bi + 1)
        NB = G * 128
        xn_rhs = xnT[:].rearrange("p a g b -> p a (g b)")
        t_y = {}
        NKF_RUN = DBG_NKF if DBG_NKF else NKF
        for kf in range(NKF_RUN):
            wbuf = kf % 2
            if kf + 1 < NKF_RUN or bi + 1 < len(groups):
                nxt = kf + 1 if kf + 1 < NKF_RUN else 0
                p.wait("gpsimd", [("T", p.val("T"))])
                w2_wt[(kf + 1) % 2] = p.op("gpsimd", "dma_start", W2s[(kf + 1) % 2][:],
                                           d["T_W2p"][lay][nxt * 128:(nxt + 1) * 128, :],
                                           inc="WT", incn=16)
            for k in range(8):
                w = [("V", v_xT[G - 1]), ("V", p.val("V")), ("A", p.val("A")),
                     ("WT", wt_b)] if k == 0 else []
                t_a = p.op("tensor", "matmul", PSR[:, 0:NB], W1p[:, k, kf * 128:(kf + 1) * 128],
                           xn_rhs[:, k, 0:NB], wait=w, start=(k == 0), stop=(k == 7),
                           inc="T" if k == 7 else None)
            v_a = p.op("vector", "tensor_scalar", a_sb[:, 0:NB], PSR[:, 0:NB],
                       b1c[:, kf:kf + 1], None, ALU.add, wait=[("T", t_a)], inc="V")
            for k in range(8):
                w = [("V", v_a)] if k == 0 else []
                t_g = p.op("tensor", "matmul", PSR[:, 0:NB],
                           W1p[:, k, 2816 + kf * 128:2816 + (kf + 1) * 128],
                           xn_rhs[:, k, 0:NB], wait=w, start=(k == 0), stop=(k == 7),
                           inc="T" if k == 7 else None)
            a_gl = p.op("scalar", "activation", gel[:, 0:NB], PSR[:, 0:NB], AF.Gelu,
                        bias=b1c[:, 22 + kf:23 + kf], wait=[("T", t_g)], inc="A")
            v_h = p.op("vector", "tensor_tensor", hk[:, 0:NB], a_sb[:, 0:NB], gel[:, 0:NB],
                       ALU.mult, wait=[("A", a_gl)], inc="V")
            if kf == DBG_KF:
                dbg_tap(cx, "l1f_a", a_sb[:, 0:NB], (bi * 128, bi * 128 + 128),
                        wait=[("V", v_a)])
                dbg_tap(cx, "l1f_h", hk[:, 0:NB], (bi * 128, bi * 128 + 128),
                        wait=[("V", v_h)])
                dbg_tap(cx, "l1f_g", gel[:, 0:NB], (bi * 128, bi * 128 + 128),
                        wait=[("A", a_gl)])
            for ti in range(G):
                if kf == 0:
                    p.op("tensor", "matmul", PS[2 * ti][:, 0:512], ones_b[:, 0:128],
                         b2row[:, 0:512],
                         wait=([("V", v_h), ("WT", w2_wt[wbuf])] if ti == 0 else []),
                         start=True, stop=False)
                    p.op("tensor", "matmul", PS[2 * ti + 1][:, 0:512], ones_b[:, 0:128],
                         b2row[:, 512:1024], start=True, stop=False)
                w = [("V", v_h), ("WT", w2_wt[wbuf])] if (ti == 0 and kf > 0) else []
                sp = (kf == NKF_RUN - 1)
                p.op("tensor", "matmul", PS[2 * ti][:, 0:512], hk[:, ti * 128:(ti + 1) * 128],
                     W2s[wbuf][:, 0:512], wait=w, start=False, stop=sp)
                t_y[ti] = p.op("tensor", "matmul", PS[2 * ti + 1][:, 0:512],
                               hk[:, ti * 128:(ti + 1) * 128], W2s[wbuf][:, 512:1024],
                               start=False, stop=sp, inc="T")
            if DBG == "l1f_y" and kf == DBG_KF:
                ty = p.op("tensor", "nop", inc="T")
                vy = p.op("vector", "tensor_copy", hx_dbg[:], PS[0][:, 0:512],
                          wait=[("T", ty)], inc="V")
                dbg_tap(cx, "l1f_y", hx_dbg[:], (bi * 128, bi * 128 + 128),
                        wait=[("V", vy)])
        for ti, tau in enumerate(tiles):
            wst = [("ST", st_v[nst - 3])] if nst - 3 in st_v else []
            p.op("vector", "tensor_tensor", X2n[:, ti, 0:512], PS[2 * ti][:, 0:512],
                 xsg[buf][:, ti, 0:512], ALU.add, wait=wst + [("T", t_y[ti])])
            v_ep = p.op("vector", "tensor_tensor", X2n[:, ti, 512:1024], PS[2 * ti + 1][:, 0:512],
                        x2u[buf][:, ti, :], ALU.add, inc="V")
            st_v[nst] = p.op("sync", "dma_start", x2[tau * 128:(tau + 1) * 128, :],
                             X2n[:, ti, :], wait=[("V", v_ep)], inc="ST", incn=16)
            nst += 1
        ep_v = v_ep
    p.barrier()


def emit_fin(cx, a_dram, b_dram, out_dram):
    p, ar = cx.p, cx.ar
    stage_fence(cx)
    ar.reset()
    at = [ar.tile(f"fa{i}", [128, 1024], F32) for i in range(2)]
    bt = [ar.tile(f"fb{i}", [128, 1024], F32) for i in range(2)]
    ot = [ar.tile(f"fo{i}", [128, 1024], F32) for i in range(2)]
    st_v = {}
    for tau in range(NT):
        w = [("ST", st_v[tau - 2])] if tau - 2 in st_v else []
        p.op("sync", "dma_start", at[tau % 2][:], a_dram[tau * 128:(tau + 1) * 128, :],
             wait=w, inc="LD", incn=16)
        ld = p.op("sync", "dma_start", bt[tau % 2][:], b_dram[tau * 128:(tau + 1) * 128, :],
                  inc="LD", incn=16)
        v_o = p.op("vector", "tensor_tensor", ot[tau % 2][:], at[tau % 2][:], bt[tau % 2][:],
                   ALU.add, wait=[("LD", ld), ("ST", st_v.get(tau - 2, 0))], inc="V")
        st_v[tau] = p.op("sync", "dma_start", out_dram[tau * 128:(tau + 1) * 128, :],
                         ot[tau % 2][:], wait=[("V", v_o)], inc="ST", incn=16)
    p.barrier()


def build_nc(stages):
    nc = bass.Bass("TRN2", target_bir_lowering=False)
    cx = Ctx()
    cx.nc = nc
    cx.sbuf_usable0 = int(nc.sbuf_bytes_remaining)
    cx.p = P(nc)
    p = cx.p
    for s in ("LD", "WT", "ST", "T", "V", "A", "G"):
        p.sem(s)

    d = {}
    d["x"] = nc.dram_tensor("x", [S, D], F32, kind="ExternalInput").ap()
    WLIST = [
        ("L0_attn_Wq", [D, D]), ("L0_attn_bq", [D]), ("L0_attn_Wk", [D, 256]), ("L0_attn_bk", [256]),
        ("L0_attn_Wv", [D, 256]), ("L0_attn_bv", [256]), ("L0_attn_Wc", [256, 64]), ("L0_attn_We", [64, 256]),
        ("L0_attn_Wo", [D, D]), ("L0_attn_bo", [D]), ("L0_attn_gamma", [D]), ("L0_attn_beta", [D]),
        ("L0_ffn_gate_w", [8, D]), ("L0_ffn_wg", [8, 512, D]), ("L0_ffn_wu", [8, 512, D]),
        ("L0_ffn_wd", [8, D, 512]), ("L0_ffn_sg", [512, D]), ("L0_ffn_su", [512, D]), ("L0_ffn_sd", [D, 512]),
        ("L1_attn_Ws", [16, D]), ("L1_attn_bs", [16]), ("L1_attn_Wi", [16, D]), ("L1_attn_bi", [16]),
        ("L1_attn_Wo", [D, 16]), ("L1_attn_bo", [D]), ("L1_attn_Wg", [D, D]), ("L1_attn_bg", [D]),
        ("L1_attn_ln_w", [D]), ("L1_attn_ln_b", [D]), ("L1_attn_A", [16, 16]),
        ("L1_ffn_Wqkv", [6, D, 3072]), ("L1_ffn_q_scale", [6, 64]), ("L1_ffn_k_scale", [6, 64]),
        ("L1_ffn_Wo", [6, D, D]), ("L1_ffn_ln_g", [6, D]), ("L1_ffn_ln_b", [6, D]),
        ("L1_ffn_W1", [6, D, 5460]), ("L1_ffn_b1", [6, 5460]), ("L1_ffn_W2", [6, 2730, D]),
        ("L1_ffn_b2", [6, D]),
        ("T_eye", [128, 128]), ("T_mask0", [256, 256]), ("T_mask1", [128, 384]),
        ("T_CS", [S, 64]), ("T_SSgn", [S, 64]), ("T_CI", [S, 64]), ("T_SIgn", [S, 64]),
        ("T_gwT", [D, 8]), ("T_wgT", [8, D, 512]), ("T_wuT", [8, D, 512]), ("T_wdT", [8, 512, D]),
        ("T_sgT", [D, 512]), ("T_suT", [D, 512]), ("T_sdT", [512, D]),
        ("T_WsT", [D, 16]), ("T_WiT", [D, 16]), ("T_WoT1", [16, D]), ("T_WgT", [D, D]),
        ("T_AT", [16, 16]), ("T_qsrot", [6, 64]), ("T_ksrot", [6, 64]), ("T_dupA", [64, 128]),
        ("T_shift1", [128, 128]), ("T_e127", [128, 128]),
        ("T_W1p", [6, D, 5632]), ("T_W2p", [6, 2816, D]), ("T_b1c", [6, 128, 44]),
    ]
    for nm, shp in WLIST:
        d[nm] = nc.dram_tensor(nm, shp, F32, kind="ExternalInput").ap()
    d["out"] = nc.dram_tensor("out", [S, D], F32, kind="ExternalOutput").ap()
    d["dbg"] = nc.dram_tensor("dbg", [S, D], F32, kind="ExternalOutput").ap()
    d["R"] = nc.dram_tensor("Rres", [S, D], F32, kind="Internal").ap()
    d["X2"] = nc.dram_tensor("X2res", [S, D], F32, kind="Internal").ap()
    cx.dram = d
    cx.g = None

    cx.PS0 = nc.alloc_psum_tensor("PS0", [128, 512], F32)
    cx.PS1 = nc.alloc_psum_tensor("PS1", [128, 512], F32)
    cx.PS2 = nc.alloc_psum_tensor("PS2", [128, 512], F32)
    cx.PS3 = nc.alloc_psum_tensor("PS3", [128, 512], F32)
    cx.PS4 = nc.alloc_psum_tensor("PS4", [128, 512], F32)
    cx.PS5 = nc.alloc_psum_tensor("PS5", [128, 512], F32)
    cx.PSB = nc.alloc_psum_tensor("PSB", [128, 1024], BF16)
    cx.PSR = nc.alloc_psum_tensor("PSR", [128, 512], F32)

    cx.eye_b = nc.alloc_sbuf_tensor("eye_b", [128, 128], BF16)
    cx.eye_f = nc.alloc_sbuf_tensor("eye_f", [128, 128], F32)
    cx.ones_b = nc.alloc_sbuf_tensor("ones_b", [1, 128], BF16)
    cx.ones_f = nc.alloc_sbuf_tensor("ones_f", [1, 128], F32)
    cx.c_epsln = nc.alloc_sbuf_tensor("c_epsln", [128, 1], F32)

    p.op("gpsimd", "dma_start", cx.eye_b[:], d["T_eye"], inc="WT", incn=16)
    ld0 = p.op("sync", "dma_start", cx.eye_f[:], d["T_eye"], inc="LD", incn=16)
    p.op("vector", "memset", cx.ones_b[:], 1.0)
    p.op("vector", "memset", cx.ones_f[:], 1.0)
    p.op("vector", "memset", cx.c_epsln[:], EPS_LN)
    p.wait("gpsimd", [("WT", p.val("WT"))])
    p.wait("sync", [("LD", ld0)])
    p.barrier()

    rem = int(nc.sbuf_bytes_remaining)
    top = int(nc.sbuf_top)
    base = (top - rem + 256 + 31) // 32 * 32
    cx.ar = Arena(nc, base, top - 2048 - base)

    cur = d["x"]
    for st in stages:
        if st == "l0a":
            emit_l0a(cx, cur, d["R"])
            cur = d["R"]
        elif st == "l0f":
            emit_l0f(cx, cur, d["R"])
            cur = d["R"]
        elif st == "l1a":
            emit_l1a(cx, cur, d["R"])
            cur = d["R"]
        elif st == "l1f":
            emit_l1f_init(cx, cur, d["X2"])
            for lay in range(L1F_LAYERS):
                if "a" in L1F_SUB or lay < L1F_LAYERS - 1:
                    emit_l1f_attn(cx, lay, d["X2"])
                if "f" in L1F_SUB or lay < L1F_LAYERS - 1:
                    emit_l1f_ffn(cx, lay, d["X2"])
            emit_fin(cx, cur, d["X2"], d["out"])
            cur = None
        else:
            raise ValueError(st)

    if cur is not None:
        emit_copyout(cx, cur, d["out"])

    p.wait("sync", [("ST", p.val("ST"))])
    p.barrier()
    p.build()
    return nc


def emit_copyout(cx, src, dst):
    p = cx.p
    ar = cx.ar
    buf = [ar.tile(f"cpo{i}", [128, 4096], F32) for i in range(2)]
    stv = {}
    for t4 in range(NT // 4):
        w = [("ST", p.val("ST"))] if t4 == 0 else []
        if t4 - 2 in stv:
            w.append(("ST", stv[t4 - 2]))
        ld = p.op("sync", "dma_start", buf[t4 % 2][:].rearrange("p (a n) -> p a n", a=4),
                  src[t4 * 512:(t4 + 1) * 512, :].rearrange("(a p) n -> p a n", p=128),
                  wait=w, inc="LD", incn=16)
        stv[t4] = p.op("sync", "dma_start",
                       dst[t4 * 512:(t4 + 1) * 512, :].rearrange("(a p) n -> p a n", p=128),
                       buf[t4 % 2][:].rearrange("p (a n) -> p a n", a=4),
                       wait=[("LD", ld)], inc="ST", incn=16)
    p.barrier()


STAGES_ALL = ["l0a", "l0f", "l1a", "l1f"]


def kernel(**inputs):
    consts = host_consts(inputs)
    nc = build_nc(STAGES_ALL)
    x = inputs["L0_attn_x"]
    base = {k: np.ascontiguousarray(v, dtype=np.float32) for k, v in inputs.items() if k != "L0_attn_x"}
    base.update({k: np.ascontiguousarray(v, dtype=np.float32) for k, v in consts.items()})
    in_maps = []
    for c in range(CORES):
        m = dict(base)
        m["x"] = np.ascontiguousarray(x[c], dtype=np.float32)
        in_maps.append(m)
    res = run_bass_kernel_spmd(nc, in_maps, core_ids=list(range(CORES)))
    out = np.stack([res.results[c]["out"] for c in range(CORES)], axis=0)
    return out


# version 04
# speedup vs baseline: 1.0862x; 1.0862x over previous
import math
import numpy as np
import concourse.bass as bass
import concourse.mybir as mybir
from concourse.bass_utils import run_bass_kernel_spmd

F32 = mybir.dt.float32
BF16 = mybir.dt.bfloat16
AF = mybir.ActivationFunctionType
ALU = mybir.AluOpType
AX = mybir.AxisListType

S, D = 4096, 1024
NT = S // 128
NCHUNK = S // 256
CORES = 8
EPS_LN = 1e-5
EPS_LA = 1e-6

ENGINES = ("tensor", "vector", "scalar", "gpsimd", "sync")
DBG = None
L1F_LAYERS = 6
L1F_SUB = "af"
DBG_KF = 0
DBG_NKF = 0
DBG_SERIAL = 0


def dbg_tap(cx, name, ap, rows, wait=()):
    if DBG != name:
        return
    p = cx.p
    v = p.op("gpsimd", "dma_start", cx.dram["dbg"][rows[0]:rows[1], :ap.free_size()],
             ap, wait=wait, inc="WT", incn=16)
    p.wait("vector", [("WT", v)])
    p.wait("scalar", [("WT", v)])
    p.wait("tensor", [("WT", v)])


def host_consts(ins):
    c = {}
    f32 = np.float32
    c["T_eye"] = np.eye(128, dtype=f32)
    tk = np.arange(256)
    c["T_mask0"] = (tk[:, None] <= tk[None, :]).astype(f32)
    i = np.arange(128)[:, None]
    jj = np.arange(384)[None, :]
    c["T_mask1"] = np.where((jj >= i) & (jj <= i + 256), 0.0, -1e30).astype(f32)
    half = np.arange(0, 64, 2, dtype=f32)
    inv_freq = (1.0 / 10000.0 ** (half / 64)).astype(f32)
    t = np.arange(S, dtype=f32)
    fr = t[:, None] * inv_freq[None, :]
    fr = np.concatenate([fr, fr], -1).astype(f32)
    base = ((half + 0.4 * 64) / (1.4 * 64)).astype(f32)
    power = (t - S // 2) / 128.0
    sc = (base[None, :] ** power[:, None]).astype(f32)
    sc = np.concatenate([sc, sc], -1).astype(f32)
    cos, sin = np.cos(fr).astype(f32), np.sin(fr).astype(f32)
    sign = np.concatenate([-np.ones(32, f32), np.ones(32, f32)])
    c["T_CS"] = cos * sc
    c["T_SSgn"] = sin * sc * sign
    c["T_CI"] = (cos / sc).astype(f32)
    c["T_SIgn"] = (sin / sc * sign).astype(f32)
    c["T_gwT"] = np.ascontiguousarray(ins["L0_ffn_gate_w"].T)
    c["T_wgT"] = np.ascontiguousarray(ins["L0_ffn_wg"].transpose(0, 2, 1))
    c["T_wuT"] = np.ascontiguousarray(ins["L0_ffn_wu"].transpose(0, 2, 1))
    c["T_wdT"] = np.ascontiguousarray(ins["L0_ffn_wd"].transpose(0, 2, 1))
    c["T_sgT"] = np.ascontiguousarray(ins["L0_ffn_sg"].T)
    c["T_suT"] = np.ascontiguousarray(ins["L0_ffn_su"].T)
    c["T_sdT"] = np.ascontiguousarray(ins["L0_ffn_sd"].T)
    c["T_WsT"] = np.ascontiguousarray(ins["L1_attn_Ws"].T)
    c["T_WiT"] = np.ascontiguousarray(ins["L1_attn_Wi"].T)
    c["T_WoT1"] = np.ascontiguousarray(ins["L1_attn_Wo"].T)
    c["T_WgT"] = np.ascontiguousarray(ins["L1_attn_Wg"].T)
    c["T_AT"] = np.ascontiguousarray(ins["L1_attn_A"].T)
    qs = ins["L1_ffn_q_scale"]
    ks = ins["L1_ffn_k_scale"]
    c["T_qsrot"] = np.ascontiguousarray(np.concatenate([qs[:, 32:], qs[:, :32]], 1))
    c["T_ksrot"] = np.ascontiguousarray(np.concatenate([ks[:, 32:], ks[:, :32]], 1))
    dup = np.zeros((64, 128), f32)
    dup[np.arange(64), np.arange(64)] = 1.0
    dup[np.arange(64), 64 + np.arange(64)] = 1.0
    c["T_dupA"] = dup
    c["T_shift1"] = np.eye(128, k=1).astype(f32)
    e127 = np.zeros((128, 128), f32)
    e127[127, 0] = 1.0
    c["T_e127"] = e127
    W1 = ins["L1_ffn_W1"]
    W2 = ins["L1_ffn_W2"]
    b1 = ins["L1_ffn_b1"]
    W1p = np.zeros((6, 1024, 5632), f32)
    W1p[:, :, 0:2730] = W1[:, :, 0:2730]
    W1p[:, :, 2816:2816 + 2730] = W1[:, :, 2730:5460]
    W2p = np.zeros((6, 2816, 1024), f32)
    W2p[:, 0:2730, :] = W2
    b1p = np.zeros((6, 5632), f32)
    b1p[:, 0:2730] = b1[:, 0:2730]
    b1p[:, 2816:2816 + 2730] = b1[:, 2730:5460]
    c["T_W1p"] = W1p
    c["T_W2p"] = W2p
    c["T_b1c"] = np.ascontiguousarray(b1p.reshape(6, 44, 128).transpose(0, 2, 1))
    return c


class P:
    def __init__(self, nc):
        self.nc = nc
        self.q = {e: [] for e in ENGINES}
        self.blocks = []
        self.sems = {}
        self.cnt = {}
        self.lastw = {e: {} for e in ENGINES}
        self.names = set()

    def sem(self, name):
        if name not in self.sems:
            self.sems[name] = self.nc.alloc_semaphore("s_" + name)
            self.cnt[name] = 0
        return name

    def val(self, name):
        return self.cnt[name]

    def op(self, eng, meth, *a, wait=(), inc=None, incn=1, **kw):
        ret = None
        if inc is not None:
            self.sem(inc)
            self.cnt[inc] += incn
            ret = self.cnt[inc]
        for s, _ in wait:
            self.sem(s)
        sems = self.sems
        lw = self.lastw[eng]
        rw = []
        for s, v in wait:
            if lw.get(s, -1) < v:
                lw[s] = v
                rw.append((s, v))
        def th(e, a=a, kw=kw, rw=rw, inc=inc, incn=incn, meth=meth):
            for nm, v in rw:
                e.wait_ge(sems[nm], v)
            ins = getattr(e, meth)(*a, **kw)
            if inc is not None:
                ins.then_inc(sems[inc], incn)
        self.q[eng].append(th)
        return ret

    def wait(self, eng, wait):
        self.op(eng, "nop", wait=wait)

    def raw(self, eng, fn):
        self.q[eng].append(fn)

    def barrier(self):
        if any(self.q[e] for e in ENGINES):
            self.blocks.append(self.q)
        self.q = {e: [] for e in ENGINES}
        for e in ENGINES:
            pass

    def build(self):
        self.barrier()
        for qd in self.blocks:
            with self.nc.Block() as blk:
                for eng in ENGINES:
                    if qd[eng]:
                        def runner(e, lst=qd[eng]):
                            for f in lst:
                                f(e)
                        getattr(blk, eng)(runner)


class Arena:
    def __init__(self, nc, base, size):
        self.nc = nc
        self.base = base
        self.size = size
        self.cur = base
        self.uid = 0

    def reset(self):
        self.cur = self.base

    def tile(self, name, shape, dtype):
        per = int(np.prod(shape[1:])) * mybir.dt.size(dtype)
        per = (per + 31) // 32 * 32
        off = self.cur
        self.cur += per
        assert self.cur <= self.base + self.size, \
            f"arena overflow at {name}: {self.cur-self.base} > {self.size}"
        self.uid += 1
        return self.nc.alloc_sbuf_tensor_at(f"ar_{name}_{self.uid}", shape, dtype, offset=off)


class Ctx:
    pass


def stage_fence(cx):
    p = cx.p
    w = [(s, p.val(s)) for s in ("ST", "LD", "WT")]
    for e in ENGINES:
        p.wait(e, list(w))
    p.barrier()


def ln_stats2(cx, xt_ap, sq_buf, mu, rstd, nmr, tmp, eps=EPS_LN, wait_v=(), wait_a=()):
    p = cx.p
    N = xt_ap.shape[-1]
    inv = 1.0 / N
    v1 = p.op("vector", "reduce_sum", mu, xt_ap, axis=AX.X, wait=wait_v, inc="V")
    a_sq = p.op("scalar", "activation", sq_buf, xt_ap, AF.Square, wait=wait_a, inc="A")
    v2 = p.op("vector", "reduce_sum", tmp, sq_buf, axis=AX.X, wait=[("A", a_sq)], inc="V")
    v3 = p.op("vector", "tensor_scalar_mul", mu, mu, inv, wait=[("V", v1)], inc="V")
    v4 = p.op("vector", "tensor_scalar_mul", tmp, tmp, inv, wait=[("V", v2)], inc="V")
    v5 = p.op("vector", "scalar_tensor_tensor", rstd, mu, -1.0, mu, ALU.mult, ALU.mult,
              wait=[("V", v3)], inc="V")
    va = p.op("vector", "tensor_tensor", tmp, tmp, rstd, ALU.add,
              wait=[("V", v5), ("V", v4)], inc="V")
    a_sd = p.op("scalar", "activation", tmp, tmp, AF.Sqrt, bias=cx.c_epsln[:],
                wait=[("V", va)], inc="A")
    v6 = p.op("vector", "reciprocal", rstd, tmp, wait=[("A", a_sd)], inc="V")
    vr = p.op("vector", "scalar_tensor_tensor", nmr, mu, -1.0, rstd, ALU.mult, ALU.mult,
              wait=[("V", v6)], inc="V")
    return vr


def emit_l0a(cx, src_dram, dst_dram):
    p, nc, ar = cx.p, cx.nc, cx.ar
    ar.reset()

    Wq = ar.tile("Wq", [128, 8, 1024], BF16)
    Wk = ar.tile("Wk", [128, 8, 256], BF16)
    Wv = ar.tile("Wv", [128, 8, 256], BF16)
    Wc = ar.tile("Wc", [128, 2, 64], BF16)
    We = ar.tile("We", [64, 256], BF16)
    Wo = ar.tile("Wo", [128, 8, 1024], BF16)
    brow = ar.tile("brow", [1, 4096], BF16)
    gamb = ar.tile("gamb", [128, 1024], F32)
    betb = ar.tile("betb", [128, 1024], F32)
    grow = ar.tile("grow", [1, 2048], F32)
    mask0 = ar.tile("mask0", [128, 2, 256], BF16)

    g = cx.g
    d = cx.dram
    p.op("gpsimd", "dma_start", Wq[:], d["L0_attn_Wq"].rearrange("(k p) n -> p k n", p=128), inc="WT", incn=16)
    p.op("gpsimd", "dma_start", Wk[:], d["L0_attn_Wk"].rearrange("(k p) n -> p k n", p=128), inc="WT", incn=16)
    p.op("gpsimd", "dma_start", Wv[:], d["L0_attn_Wv"].rearrange("(k p) n -> p k n", p=128), inc="WT", incn=16)
    p.op("gpsimd", "dma_start", Wc[:], d["L0_attn_Wc"].rearrange("(k p) n -> p k n", p=128), inc="WT", incn=16)
    p.op("gpsimd", "dma_start", We[:], d["L0_attn_We"], inc="WT", incn=16)
    p.op("gpsimd", "dma_start", Wo[:], d["L0_attn_Wo"].rearrange("(k p) n -> p k n", p=128), inc="WT", incn=16)
    p.op("gpsimd", "dma_start", brow[:, 0:1024], d["L0_attn_bq"].unsqueeze(0), inc="WT", incn=16)
    p.op("gpsimd", "dma_start", brow[:, 1024:1280], d["L0_attn_bk"].unsqueeze(0), inc="WT", incn=16)
    p.op("gpsimd", "dma_start", brow[:, 1280:1536], d["L0_attn_bv"].unsqueeze(0), inc="WT", incn=16)
    p.op("gpsimd", "dma_start", brow[:, 1536:2560], d["L0_attn_bo"].unsqueeze(0), inc="WT", incn=16)
    p.op("gpsimd", "dma_start", mask0[:], d["T_mask0"].rearrange("(r p) n -> p r n", p=128), inc="WT", incn=16)
    wt_g = p.op("sync", "dma_start", grow[:, 0:1024], d["L0_attn_gamma"].unsqueeze(0), inc="LD", incn=16)
    wt_g = p.op("sync", "dma_start", grow[:, 1024:2048], d["L0_attn_beta"].unsqueeze(0), inc="LD", incn=16)
    wt_w = p.val("WT")

    PS0, PS1, PS2, PS3, PS4, PS5 = cx.PS0, cx.PS1, cx.PS2, cx.PS3, cx.PS4, cx.PS5
    t0 = p.op("tensor", "matmul", PS0[:, 0:512], cx.ones_f[:, 0:128], grow[:, 0:512],
              wait=[("LD", wt_g)], start=True, stop=True, inc="T")
    p.op("tensor", "matmul", PS1[:, 0:512], cx.ones_f[:, 0:128], grow[:, 512:1024], start=True, stop=True)
    p.op("tensor", "matmul", PS2[:, 0:512], cx.ones_f[:, 0:128], grow[:, 1024:1536], start=True, stop=True)
    t0 = p.op("tensor", "matmul", PS3[:, 0:512], cx.ones_f[:, 0:128], grow[:, 1536:2048], start=True, stop=True, inc="T")
    p.op("vector", "tensor_copy", gamb[:, 0:512], PS0[:, 0:512], wait=[("T", t0)])
    p.op("vector", "tensor_copy", gamb[:, 512:1024], PS1[:, 0:512])
    p.op("vector", "tensor_copy", betb[:, 0:512], PS2[:, 0:512])
    v0 = p.op("vector", "tensor_copy", betb[:, 512:1024], PS3[:, 0:512], inc="V")
    p.barrier()

    xt = [ar.tile(f"xt{i}", [128, 1024], F32) for i in range(2)]
    sqb = ar.tile("sqb", [128, 1024], BF16)
    sqb2 = ar.tile("sqb2", [128, 1024], BF16)
    ebuf = ar.tile("ebuf", [128, 1024], BF16)
    lnb = ar.tile("lnb", [128, 1024], BF16)
    xTt = ar.tile("xTt", [128, 8, 128], BF16)
    stt = ar.tile("stt", [128, NT, 4], F32)
    qf_t = ar.tile("qf_t", [128, 1024], BF16)
    qfp = ar.tile("qfp", [128, 1024], BF16)
    k1t = ar.tile("k1t", [128, 256], BF16)
    v1t = ar.tile("v1t", [128, 256], BF16)
    k1T = ar.tile("k1T", [128, 2, 128], BF16)
    v1T = ar.tile("v1T", [128, 2, 128], BF16)
    k2t = ar.tile("k2t", [128, 64], BF16)
    v2t = ar.tile("v2t", [128, 64], BF16)
    k2T = ar.tile("k2T", [64, 128], BF16)
    v2T = ar.tile("v2T", [64, 128], BF16)
    qfT = ar.tile("qfT", [128, 8, 256], BF16)
    kfT = ar.tile("kfT", [128, 2, 256], BF16)
    kfc = ar.tile("kfc", [128, 2, 256], BF16)
    vag = ar.tile("vag", [128, 2, 4, 65], BF16)
    simb = ar.tile("simb", [128, 2, 256], BF16)
    epsr = ar.tile("epsr", [1, 130], BF16)
    recd = ar.tile("recd", [128, 2], F32)
    otc = ar.tile("otc", [128, 2, 1024], BF16)
    oTt = ar.tile("oTt", [128, 8, 128], BF16)
    dupA = ar.tile("dupA", [64, 128], BF16)
    Sbf0 = ar.tile("Sbf0", [64, 65], BF16)
    Saug = ar.tile("Saug", [64, 4, 65], F32)
    Sbf2 = ar.tile("Sbf2", [128, 2, 65], BF16)
    lnep = ar.tile("lnep", [128, 1024], F32)
    ysb = ar.tile("ysb", [128, 1024], F32)
    Rt = [ar.tile(f"Rt{i}", [128, 1024], F32) for i in range(2)]

    p.op("vector", "memset", Saug[:], 0.0)
    p.op("vector", "memset", Sbf2[:], 0.0)
    p.op("vector", "memset", vag[:], 1.0)
    p.op("vector", "memset", epsr[:], 0.0)
    p.op("vector", "memset", epsr[:, 64:65], EPS_LA)
    p.op("vector", "memset", epsr[:, 129:130], EPS_LA)
    wdup = p.op("gpsimd", "dma_start", dupA[:], d["T_dupA"], inc="WT", incn=16)
    p.wait("gpsimd", [("WT", wdup)])
    p.barrier()

    PSB = cx.PSB
    ones_b = cx.ones_b

    ep_v = {}
    st_v = {}
    t_tr_prev = 0
    t_qkv_prev = 0
    for ch in range(NCHUNK):
        p.wait("vector", [("T", p.val("T"))])
        prep_v = {}
        for half in range(2):
            tau = 2 * ch + half
            w = [("V", ep_v[tau - 2])] if tau - 2 in ep_v else []
            ld = p.op("sync", "dma_start", xt[half][:], src_dram[tau * 128:(tau + 1) * 128, :],
                      wait=w, inc="LD", incn=16)
            mu = stt[:, tau, 0:1]; rstd = stt[:, tau, 1:2]; nmr = stt[:, tau, 2:3]; tmp = stt[:, tau, 3:4]
            vr = ln_stats2(cx, xt[half][:], sqb[:], mu, rstd, nmr, tmp,
                           wait_v=[("LD", ld)], wait_a=[("LD", ld)])
            a_ln = p.op("vector", "tensor_scalar", lnb[:], xt[half][:], rstd, nmr,
                        ALU.mult, ALU.add, wait=[("V", vr), ("T", t_tr_prev)], inc="V")
            dbg_tap(cx, "ln", lnb[:], (tau * 128, tau * 128 + 128), wait=[("V", a_ln)])
            for c in range(8):
                w = [("V", a_ln)] if c == 0 else []
                tt = p.op("tensor", "transpose", PSB[:, c * 128:(c + 1) * 128],
                          lnb[:, c * 128:(c + 1) * 128], cx.eye_b[:],
                          wait=w, inc="T")
            t_tr_prev = tt
            v_xt = p.op("vector", "tensor_copy", xTt[:].rearrange("p a b -> p (a b)"), PSB[:, 0:1024],
                        wait=[("T", tt), ("T", t_qkv_prev)], inc="V")

            tq0 = p.op("tensor", "matmul", PS0[:, 0:512], ones_b[:, 0:128], brow[:, 0:512],
                       wait=[("V", v_xt), ("WT", wt_w), ("V", p.val("V"))], start=True, stop=False)
            p.op("tensor", "matmul", PS1[:, 0:512], ones_b[:, 0:128], brow[:, 512:1024], start=True, stop=False)
            p.op("tensor", "matmul", PS2[:, 0:256], ones_b[:, 0:128], brow[:, 1024:1280], start=True, stop=False)
            p.op("tensor", "matmul", PS3[:, 0:256], ones_b[:, 0:128], brow[:, 1280:1536], start=True, stop=False)
            for k in range(8):
                st = False
                sp = (k == 7)
                p.op("tensor", "matmul", PS0[:, 0:512], xTt[:, k, :], Wq[:, k, 0:512], start=st, stop=sp)
                p.op("tensor", "matmul", PS1[:, 0:512], xTt[:, k, :], Wq[:, k, 512:1024], start=st, stop=sp)
                p.op("tensor", "matmul", PS2[:, 0:256], xTt[:, k, :], Wk[:, k, :], start=st, stop=sp)
                t_qkv = p.op("tensor", "matmul", PS3[:, 0:256], xTt[:, k, :], Wv[:, k, :],
                             start=st, stop=sp, inc="T")
            for h2 in range(2):
                ps = (PS0 if h2 == 0 else PS1)
                vm = p.op("vector", "tensor_scalar_min", ebuf[:, h2 * 512:(h2 + 1) * 512], ps[:, 0:512], 0.0,
                          wait=[("T", t_qkv)], inc="V")
                ae = p.op("scalar", "activation", ebuf[:, h2 * 512:(h2 + 1) * 512],
                          ebuf[:, h2 * 512:(h2 + 1) * 512], AF.Exp, wait=[("V", vm)], inc="A")
                v_qf = p.op("vector", "scalar_tensor_tensor", qf_t[:, h2 * 512:(h2 + 1) * 512],
                            ps[:, 0:512], 0.0, ebuf[:, h2 * 512:(h2 + 1) * 512], ALU.max, ALU.add,
                            wait=[("A", ae)], inc="V")
            dbg_tap(cx, "qf", qf_t[:], (tau * 128, tau * 128 + 128), wait=[("V", p.val("V"))])
            p.op("vector", "tensor_copy", k1t[:], PS2[:, 0:256], wait=[("T", t_qkv)])
            v_kv1 = p.op("vector", "tensor_copy", v1t[:], PS3[:, 0:256], inc="V")

            tt = None
            for c in range(2):
                w = [("V", v_kv1)] if c == 0 else []
                p.op("tensor", "transpose", PSB[:, c * 128:(c + 1) * 128], k1t[:, c * 128:(c + 1) * 128],
                     cx.eye_b[:], wait=w)
                tt = p.op("tensor", "transpose", PSB[:, 256 + c * 128:256 + (c + 1) * 128],
                          v1t[:, c * 128:(c + 1) * 128], cx.eye_b[:], inc="T")
            p.op("vector", "tensor_copy", k1T[:].rearrange("p a b -> p (a b)"), PSB[:, 0:256], wait=[("T", tt)])
            v_t1 = p.op("vector", "tensor_copy", v1T[:].rearrange("p a b -> p (a b)"), PSB[:, 256:512], inc="V")
            p.op("tensor", "matmul", PS4[:, 0:64], k1T[:, 0, :], Wc[:, 0, :],
                 wait=[("V", v_t1)], start=True, stop=False)
            p.op("tensor", "matmul", PS4[:, 0:64], k1T[:, 1, :], Wc[:, 1, :], start=False, stop=True)
            p.op("tensor", "matmul", PS4[:, 64:128], v1T[:, 0, :], Wc[:, 0, :], start=True, stop=False)
            t_k2 = p.op("tensor", "matmul", PS4[:, 64:128], v1T[:, 1, :], Wc[:, 1, :],
                        start=False, stop=True, inc="T")
            p.op("vector", "tensor_copy", k2t[:], PS4[:, 0:64], wait=[("T", t_k2)])
            v_k2 = p.op("vector", "tensor_copy", v2t[:], PS4[:, 64:128], inc="V")
            p.op("tensor", "transpose", PSB[0:64, 512:640], k2t[:], cx.eye_b[:], wait=[("V", v_k2)])
            t_k2T = p.op("tensor", "transpose", PSB[0:64, 640:768], v2t[:], cx.eye_b[:], inc="T")
            p.op("vector", "tensor_copy", k2T[:], PSB[0:64, 512:640], wait=[("T", t_k2T)])
            v_k2T = p.op("vector", "tensor_copy", v2T[:], PSB[0:64, 640:768], inc="V")
            p.op("tensor", "matmul", PS5[:, 0:256], k2T[:], We[:], wait=[("V", v_k2T)], start=True, stop=True)
            t_k3 = p.op("tensor", "matmul", PS5[:, 256:512], v2T[:], We[:], start=True, stop=True, inc="T")
            vm = p.op("vector", "tensor_scalar_min", ebuf[:, 0:256], PS5[:, 0:256], 0.0,
                      wait=[("T", t_k3)], inc="V")
            ae = p.op("scalar", "activation", ebuf[:, 0:256], ebuf[:, 0:256], AF.Exp,
                      wait=[("V", vm)], inc="A")
            v_kf = p.op("vector", "scalar_tensor_tensor", kfc[:, half, :], PS5[:, 0:256], 0.0,
                        ebuf[:, 0:256], ALU.max, ALU.add, wait=[("A", ae)], inc="V")
            p.op("vector", "tensor_copy",
                 vag[:, half, :, 0:64],
                 PS5[:, 256:512].rearrange("p (a b) -> p a b", a=4))
            v_qp = p.op("vector", "tensor_copy",
                        qfp[:].rearrange("p (g r q f) -> p g r q f", g=2, r=4, q=2),
                        qf_t[:].rearrange("p (g q r f) -> p g q r f", g=2, q=2, r=4)
                        .transpose([0, 1, 3, 2, 4]),
                        wait=[("T", t_qkv_prev), ("V", v_qf)], inc="V")
            tt = None
            for c in range(8):
                w = [("V", v_qp)] if c == 0 else []
                tt = p.op("tensor", "transpose", PSB[:, c * 128:(c + 1) * 128],
                          qfp[:, c * 128:(c + 1) * 128], cx.eye_b[:], wait=w, inc="T")
            v_qT = p.op("vector", "tensor_copy", qfT[:, :, half * 128:(half + 1) * 128],
                        PSB[:, 0:1024].rearrange("p (a b) -> p a b", a=8),
                        wait=[("T", tt)], inc="V")
            for c in range(2):
                tt = p.op("tensor", "transpose", PSB[:, c * 128:(c + 1) * 128],
                          kfc[:, half, c * 128:(c + 1) * 128], cx.eye_b[:],
                          wait=[("V", v_qT), ("V", v_kf)], inc="T")
            v_pr = p.op("vector", "tensor_copy", kfT[:, :, half * 128:(half + 1) * 128],
                        PSB[:, 0:256].rearrange("p (a b) -> p a b", a=2),
                        wait=[("T", tt)], inc="V")
            dbg_tap(cx, "kf", kfc[:, half, :], (tau * 128, tau * 128 + 128), wait=[("V", v_kf)])
            dbg_tap(cx, "vag", vag[:, half, :, :].rearrange("p a b -> p (a b)"),
                    (tau * 128, tau * 128 + 128), wait=[("V", p.val("V"))])
            prep_v[half] = v_pr
            t_qkv_prev = tt

        v_prep = prep_v[1]
        for kvh in range(4):
            b = (kvh % 2) * 64
            for qh in range(kvh * 4, kvh * 4 + 4):
                qsl = qfT[b:b + 64, (qh // 8) * 4 + (qh % 4), :]
                for j in range(2):
                    ksl = kfT[b:b + 64, kvh // 2, j * 128:(j + 1) * 128]
                    w = [("V", v_prep)] if (qh == kvh * 4 and j == 0) else []
                    if j == 0:
                        w.append(("V", p.val("V")))
                    t_sim = p.op("tensor", "matmul", PS0[:, j * 256:(j + 1) * 256], ksl, qsl,
                                 wait=w, start=True, stop=True, inc="T")
                v_sim = p.op("vector", "tensor_tensor", simb[:], PS0[:, 0:512].rearrange("p (a b) -> p a b", a=2),
                             mask0[:], ALU.mult, wait=[("T", t_sim)], inc="V")
                t_num = None
                for h2 in range(2):
                    reg = PS1[:, h2 * 65:(h2 + 1) * 65]
                    w = [("V", v_sim), ("V", p.val("V"))] if h2 == 0 else []
                    p.op("tensor", "matmul", reg, simb[:, 0, h2 * 128:(h2 + 1) * 128],
                         vag[:, 0, kvh, :], wait=w, start=True, stop=False)
                    p.op("tensor", "matmul", reg, simb[:, 1, h2 * 128:(h2 + 1) * 128],
                         vag[:, 1, kvh, :], start=False, stop=False)
                    p.op("tensor", "matmul", reg, qsl[:, h2 * 128:(h2 + 1) * 128],
                         Sbf2[b:b + 64, kvh // 2, :], start=False, stop=False)
                    t_num = p.op("tensor", "matmul", reg, ones_b[:, 0:128],
                                 epsr[:, h2 * 65:(h2 + 1) * 65],
                                 start=False, stop=True, inc="T")
                p.op("vector", "reciprocal", recd[:, 0:1], PS1[:, 64:65], wait=[("T", t_num)])
                v_rc = p.op("vector", "reciprocal", recd[:, 1:2], PS1[:, 129:130], inc="V")
                p.op("vector", "tensor_scalar", otc[:, 0, qh * 64:(qh + 1) * 64],
                     PS1[:, 0:64], recd[:, 0:1], None, ALU.mult, wait=[("V", v_rc)])
                v_o = p.op("vector", "tensor_scalar", otc[:, 1, qh * 64:(qh + 1) * 64],
                           PS1[:, 65:129], recd[:, 1:2], None, ALU.mult, inc="V")
            p.op("tensor", "matmul", PS4[0:64, 0:65], kfc[:, 0, kvh * 64:(kvh + 1) * 64],
                 vag[:, 0, kvh, :], wait=[("V", p.val("V"))], start=True, stop=False)
            t_st = p.op("tensor", "matmul", PS4[0:64, 0:65], kfc[:, 1, kvh * 64:(kvh + 1) * 64],
                        vag[:, 1, kvh, :], start=False, stop=True, inc="T")
            v_sa = p.op("vector", "tensor_tensor", Saug[:, kvh, :], Saug[:, kvh, :], PS4[0:64, 0:65],
                        ALU.add, wait=[("T", t_st)], inc="V")
            v_s = p.op("vector", "tensor_copy", Sbf0[:], Saug[:, kvh, :], wait=[("V", v_sa)], inc="V")
            t_dp = p.op("tensor", "matmul", PS3[:, 0:65], dupA[:], Sbf0[:],
                        wait=[("V", v_s)], start=True, stop=True, inc="T")
            p.op("vector", "tensor_copy", Sbf2[b:b + 64, kvh // 2, :], PS3[b:b + 64, 0:65],
                 wait=[("T", t_dp)])

        for hh in range(2):
            dbg_tap(cx, "ot", otc[:, hh, :], ((2 * ch + hh) * 128, (2 * ch + hh) * 128 + 128),
                    wait=[("V", p.val("V"))])
        v_scan = p.val("V")
        v_gate = v_scan
        for half in range(2):
            tau = 2 * ch + half
            for c in range(8):
                w = [("V", v_gate)] if c == 0 else []
                tt = p.op("tensor", "transpose", PSB[:, c * 128:(c + 1) * 128],
                          otc[:, half, c * 128:(c + 1) * 128], cx.eye_b[:], wait=w, inc="T")
            v_oT = p.op("vector", "tensor_copy", oTt[:].rearrange("p a b -> p (a b)"), PSB[:, 0:1024],
                        wait=[("T", tt)], inc="V")
            v_gate = v_oT
            t0 = p.op("tensor", "matmul", PS0[:, 0:512], ones_b[:, 0:128], brow[:, 1536:2048],
                      wait=[("V", v_oT)], start=True, stop=False)
            p.op("tensor", "matmul", PS1[:, 0:512], ones_b[:, 0:128], brow[:, 2048:2560],
                 start=True, stop=False)
            for k in range(8):
                sp = (k == 7)
                p.op("tensor", "matmul", PS0[:, 0:512], oTt[:, k, :],
                     Wo[:, k, 0:512], start=False, stop=sp)
                t_y = p.op("tensor", "matmul", PS1[:, 0:512], oTt[:, k, :],
                           Wo[:, k, 512:1024], start=False, stop=sp, inc="T")
            a_ep = p.op("vector", "tensor_scalar", lnep[:], xt[half][:],
                        stt[:, tau, 1:2], stt[:, tau, 2:3], ALU.mult, ALU.add,
                        wait=[("V", v_scan), ("V", ep_v.get(tau - 1, 0))], inc="V")
            p.op("vector", "tensor_tensor", ysb[:, 0:512], PS0[:, 0:512], lnep[:, 0:512], ALU.add,
                 wait=[("T", t_y), ("V", a_ep)])
            v_y = p.op("vector", "tensor_tensor", ysb[:, 512:1024], PS1[:, 0:512], lnep[:, 512:1024],
                       ALU.add, inc="V")
            dbg_tap(cx, "y", ysb[:], (tau * 128, tau * 128 + 128), wait=[("V", v_y)])
            mu = stt[:, tau, 0:1]; rstd = stt[:, tau, 1:2]; nmr = stt[:, tau, 2:3]; tmp = stt[:, tau, 3:4]
            vr = ln_stats2(cx, ysb[:], sqb2[:], mu, rstd, nmr, tmp,
                           wait_v=[("V", v_y)], wait_a=[("V", v_y)])
            a_z = p.op("vector", "tensor_scalar", lnep[:], ysb[:], rstd, nmr,
                       ALU.mult, ALU.add, wait=[("V", vr)], inc="V")
            a_z = p.op("vector", "tensor_tensor", lnep[:], lnep[:], gamb[:], ALU.mult,
                       wait=[("V", a_z)], inc="V")
            a_z = p.op("vector", "tensor_tensor", lnep[:], lnep[:], betb[:], ALU.add,
                       wait=[("V", a_z)], inc="V")
            wst = [("ST", st_v[tau - 2])] if tau - 2 in st_v else []
            v_ep = p.op("vector", "tensor_tensor", Rt[half][:], lnep[:], xt[half][:], ALU.add,
                        wait=wst + [("V", a_z)], inc="V")
            ep_v[tau] = v_ep
            st_v[tau] = p.op("sync", "dma_start", dst_dram[tau * 128:(tau + 1) * 128, :], Rt[half][:],
                             wait=[("V", v_ep)], inc="ST", incn=16)
    p.barrier()


def emit_l0f(cx, src_dram, dst_dram):
    p, nc, ar = cx.p, cx.nc, cx.ar
    stage_fence(cx)
    ar.reset()
    d = cx.dram
    PS = [cx.PS0, cx.PS1, cx.PS2, cx.PS3, cx.PS4, cx.PS5]
    PSR, PSB = cx.PSR, cx.PSB
    ones_b = cx.ones_b

    gwT = ar.tile("gwT", [128, 8, 8], BF16)
    sgT = ar.tile("sgT", [128, 8, 512], BF16)
    suT = ar.tile("suT", [128, 8, 512], BF16)
    sdT = ar.tile("sdT", [128, 4, 1024], BF16)
    wg_b = [ar.tile(f"wg{i}", [128, 8, 512], BF16) for i in range(2)]
    wu_b = [ar.tile(f"wu{i}", [128, 8, 512], BF16) for i in range(2)]
    wd_b = [ar.tile(f"wd{i}", [128, 4, 1024], BF16) for i in range(2)]

    p.op("gpsimd", "dma_start", gwT[:], d["T_gwT"].rearrange("(k p) n -> p k n", p=128), inc="WT", incn=16)
    p.op("gpsimd", "dma_start", sgT[:], d["T_sgT"].rearrange("(k p) n -> p k n", p=128), inc="WT", incn=16)
    p.op("gpsimd", "dma_start", suT[:], d["T_suT"].rearrange("(k p) n -> p k n", p=128), inc="WT", incn=16)
    wt_sh = p.op("gpsimd", "dma_start", sdT[:], d["T_sdT"].rearrange("(k p) n -> p k n", p=128), inc="WT", incn=16)

    def load_expert(e, buf):
        p.op("gpsimd", "dma_start", wg_b[buf][:], d["T_wgT"][e].rearrange("(k p) n -> p k n", p=128), inc="WT", incn=16)
        p.op("gpsimd", "dma_start", wu_b[buf][:], d["T_wuT"][e].rearrange("(k p) n -> p k n", p=128), inc="WT", incn=16)
        return p.op("gpsimd", "dma_start", wd_b[buf][:], d["T_wdT"][e].rearrange("(k p) n -> p k n", p=128), inc="WT", incn=16)

    xtg = ar.tile("xtg", [128, 3, 1024], F32)
    sqb = ar.tile("sqb", [128, 1024], BF16)
    lnb = ar.tile("lnb", [128, 1024], BF16)
    xTg = ar.tile("xTg", [128, 3, 8, 128], BF16)
    stt = ar.tile("stt", [128, 3, 4], F32)
    smf = ar.tile("smf", [128, 3, 8], F32)
    cw = ar.tile("cw", [128, 3, 8], F32)
    rnk = ar.tile("rnk", [128, 8], F32)
    t8 = ar.tile("t8", [128, 8], F32)
    den8 = ar.tile("den8", [128, 2], F32)
    sgl = ar.tile("sgl", [128, 512], BF16)
    hx = ar.tile("hx", [128, 512], BF16)
    hT = ar.tile("hT", [128, 4, 128], BF16)
    Rt = [ar.tile(f"Rt{i}", [128, 1024], F32) for i in range(2)]

    groups = [list(range(g, min(g + 3, NT))) for g in range(0, NT, 3)]
    wt_e = {}
    t_wuse = {}
    v_ep_prev = 0
    st_v = {}
    nst = 0

    for gi, tiles in enumerate(groups):
        G = len(tiles)
        for e in range(2):
            if gi == 0 and e < 2:
                pass
        a_ln_prev = p.val("T")
        v_ln = {}
        for ti, tau in enumerate(tiles):
            ld = p.op("sync", "dma_start", xtg[:, ti, :], src_dram[tau * 128:(tau + 1) * 128, :],
                      wait=[("V", v_ep_prev)], inc="LD", incn=16)
            mu = stt[:, ti, 0:1]; rstd = stt[:, ti, 1:2]; nmr = stt[:, ti, 2:3]; tmp = stt[:, ti, 3:4]
            vr = ln_stats2(cx, xtg[:, ti, :], sqb[:], mu, rstd, nmr, tmp,
                           wait_v=[("LD", ld)], wait_a=[("LD", ld)])
            a_ln = p.op("vector", "tensor_scalar", lnb[:], xtg[:, ti, :], rstd, nmr,
                        ALU.mult, ALU.add, wait=[("V", vr), ("T", p.val("T"))], inc="V")
            for c in range(8):
                w = [("V", a_ln)] if c == 0 else []
                tt = p.op("tensor", "transpose", PSB[:, c * 128:(c + 1) * 128],
                          lnb[:, c * 128:(c + 1) * 128], cx.eye_b[:], wait=w, inc="T")
            v_xt = p.op("vector", "tensor_copy", xTg[:, ti, :, :].rearrange("p a b -> p (a b)"),
                        PSB[:, 0:1024], wait=[("T", tt)], inc="V")
            for k in range(8):
                w = [("V", v_xt), ("WT", wt_sh)] if k == 0 else []
                t_s = p.op("tensor", "matmul", PS[0][:, ti * 8:(ti + 1) * 8], xTg[:, ti, k, :],
                           gwT[:, k, :], wait=w, start=(k == 0), stop=(k == 7), inc="T")
            v1 = p.op("vector", "reduce_max", den8[:, 0:1], PS[0][:, ti * 8:(ti + 1) * 8],
                      axis=AX.X, negate=True, wait=[("T", t_s)], inc="V")
            a_e = p.op("scalar", "activation", smf[:, ti, :], PS[0][:, ti * 8:(ti + 1) * 8],
                       AF.Exp, bias=den8[:, 0:1], accum_out=den8[:, 1:2],
                       wait=[("V", v1)], inc="A")
            v2 = p.op("vector", "reciprocal", den8[:, 0:1], den8[:, 1:2], wait=[("A", a_e)], inc="V")
            v3 = p.op("vector", "tensor_scalar", smf[:, ti, :], smf[:, ti, :], den8[:, 0:1], None,
                      ALU.mult, wait=[("V", v2)], inc="V")
            vm = p.op("vector", "memset", rnk[:], 0.0, inc="V")
            for e in range(8):
                vc = p.op("vector", "tensor_scalar", t8[:], smf[:, ti, :], smf[:, ti, e:e + 1], None,
                          ALU.is_lt, wait=[("V", v3 if e == 0 else vm)], inc="V")
                vm = p.op("vector", "tensor_tensor", rnk[:], rnk[:], t8[:], ALU.add,
                          wait=[("V", vc)], inc="V")
            vs = p.op("vector", "tensor_scalar", t8[:], rnk[:], 1.5, None, ALU.is_lt,
                      wait=[("V", vm)], inc="V")
            v_cw = p.op("vector", "tensor_tensor", cw[:, ti, :], smf[:, ti, :], t8[:], ALU.mult,
                        wait=[("V", vs)], inc="V")
            v_ln[ti] = v_cw

        def gu_h(ti, wgt, wut, cw_ap, wait_first):
            for k in range(8):
                w = list(wait_first) if k == 0 else []
                t_g = p.op("tensor", "matmul", PSR[:, 0:512], xTg[:, ti, k, :], wgt[:, k, :],
                           wait=w, start=(k == 0), stop=(k == 7), inc="T" if k == 7 else None)
            a_s = p.op("scalar", "activation", sgl[:], PSR[:, 0:512], AF.Silu,
                       wait=[("T", t_g)], inc="A")
            for k in range(8):
                w = [("A", a_s)] if k == 0 else []
                t_u = p.op("tensor", "matmul", PSR[:, 0:512], xTg[:, ti, k, :], wut[:, k, :],
                           wait=w, start=(k == 0), stop=(k == 7), inc="T" if k == 7 else None)
            if cw_ap is None:
                v_h = p.op("vector", "tensor_tensor", hx[:], PSR[:, 0:512], sgl[:], ALU.mult,
                           wait=[("T", t_u)], inc="V")
            else:
                v_h = p.op("vector", "scalar_tensor_tensor", hx[:], PSR[:, 0:512], cw_ap, sgl[:],
                           ALU.mult, ALU.mult, wait=[("T", t_u)], inc="V")
            return v_h

        def h_down(ti, wdt, v_h, first, last):
            for c in range(4):
                w = [("V", v_h)] if c == 0 else []
                tt = p.op("tensor", "transpose", PSB[:, c * 128:(c + 1) * 128],
                          hx[:, c * 128:(c + 1) * 128], cx.eye_b[:], wait=w, inc="T")
            v_hT = p.op("vector", "tensor_copy", hT[:].rearrange("p a b -> p (a b)"),
                        PSB[:, 0:512], wait=[("T", tt)], inc="V")
            t_y = None
            for c in range(4):
                w = [("V", v_hT)] if c == 0 else []
                sp = last and c == 3
                p.op("tensor", "matmul", PS[2 * ti][:, 0:512], hT[:, c, :], wdt[:, c, 0:512],
                     wait=w, start=(first and c == 0), stop=sp)
                t_y = p.op("tensor", "matmul", PS[2 * ti + 1][:, 0:512], hT[:, c, :], wdt[:, c, 512:1024],
                           start=(first and c == 0), stop=sp, inc="T")
            return t_y

        t_y_tile = {}
        for ti in range(G):
            v_h = gu_h(ti, sgT, suT, None, [("V", v_ln[ti]), ("V", p.val("V"))])
            t_y_tile[ti] = h_down(ti, sdT, v_h, True, False)
        for e in range(8):
            buf = e % 2
            if gi == 0 and e < 2:
                wt_e[buf] = load_expert(e, buf)
            for ti in range(G):
                cw_ap = cw[:, ti, e:e + 1]
                wf = [("WT", wt_e[buf]), ("V", p.val("V"))]
                v_h = gu_h(ti, wg_b[buf], wu_b[buf], cw_ap, wf)
                t_y_tile[ti] = h_down(ti, wd_b[buf], v_h, False, e == 7)
            nxt = e + 2 if e < 6 else ((e - 6) if gi + 1 < len(groups) else None)
            if nxt is not None:
                p.wait("gpsimd", [("T", p.val("T"))])
                wt_e[buf] = load_expert(nxt, buf)
        for ti, tau in enumerate(tiles):
            wst = [("ST", st_v[nst - 2])] if nst - 2 in st_v else []
            p.op("vector", "tensor_tensor", Rt[nst % 2][:, 0:512], PS[2 * ti][:, 0:512],
                 xtg[:, ti, 0:512], ALU.add, wait=wst + [("T", t_y_tile[ti])])
            v_ep = p.op("vector", "tensor_tensor", Rt[nst % 2][:, 512:1024], PS[2 * ti + 1][:, 0:512],
                        xtg[:, ti, 512:1024], ALU.add, inc="V")
            st_v[nst] = p.op("sync", "dma_start", dst_dram[tau * 128:(tau + 1) * 128, :],
                             Rt[nst % 2][:], wait=[("V", v_ep)], inc="ST", incn=16)
            nst += 1
            v_ep_prev = v_ep
    p.barrier()


def emit_l1a(cx, src_dram, dst_dram):
    p, nc, ar = cx.p, cx.nc, cx.ar
    stage_fence(cx)
    ar.reset()
    d = cx.dram
    PS0, PS1, PS2, PS3, PS4, PS5 = cx.PS0, cx.PS1, cx.PS2, cx.PS3, cx.PS4, cx.PS5
    PSB = cx.PSB
    ones_b, ones_f = cx.ones_b, cx.ones_f

    WgT = ar.tile("WgT", [128, 8, 1024], BF16)
    Wu = ar.tile("Wu", [128, 8, 16], BF16)
    Wsf = ar.tile("Wsf", [128, 8, 16], F32)
    Wif = ar.tile("Wif", [128, 8, 16], F32)
    WoTf = ar.tile("WoTf", [16, 1024], F32)
    ATf = ar.tile("ATf", [16, 16], F32)
    borow = ar.tile("borow", [1, 1024], F32)
    bgrow = ar.tile("bgrow", [1, 1024], BF16)
    bsrow = ar.tile("bsrow", [1, 16], F32)
    birow = ar.tile("birow", [1, 16], F32)
    burow = ar.tile("burow", [1, 16], BF16)
    lnwb = ar.tile("lnwb", [128, 1024], F32)
    lnbb = ar.tile("lnbb", [128, 1024], F32)
    grow = ar.tile("grow", [1, 2048], F32)
    UT = ar.tile("UT", [16, S], F32)
    H = ar.tile("H", [16, S], F32)
    h0 = ar.tile("h0", [16, 1], F32)

    p.op("gpsimd", "dma_start", WgT[:], d["T_WgT"].rearrange("(k p) n -> p k n", p=128), inc="WT", incn=16)
    wt_g = p.op("gpsimd", "dma_start", bgrow[:], d["L1_attn_bg"].unsqueeze(0), inc="WT", incn=16)
    p.op("sync", "dma_start", Wsf[:], d["T_WsT"].rearrange("(k p) n -> p k n", p=128), inc="LD", incn=16)
    p.op("sync", "dma_start", Wif[:], d["T_WiT"].rearrange("(k p) n -> p k n", p=128), inc="LD", incn=16)
    p.op("sync", "dma_start", WoTf[:], d["T_WoT1"], inc="LD", incn=16)
    p.op("sync", "dma_start", ATf[:], d["T_AT"], inc="LD", incn=16)
    p.op("sync", "dma_start", borow[:], d["L1_attn_bo"].unsqueeze(0), inc="LD", incn=16)
    p.op("sync", "dma_start", bsrow[:], d["L1_attn_bs"].unsqueeze(0), inc="LD", incn=16)
    p.op("sync", "dma_start", birow[:], d["L1_attn_bi"].unsqueeze(0), inc="LD", incn=16)
    p.op("sync", "dma_start", grow[:, 0:1024], d["L1_attn_ln_w"].unsqueeze(0), inc="LD", incn=16)
    ld_w = p.op("sync", "dma_start", grow[:, 1024:2048], d["L1_attn_ln_b"].unsqueeze(0), inc="LD", incn=16)
    v_wu = p.op("vector", "tensor_tensor", Wu[:], Wsf[:], Wif[:], ALU.add,
                wait=[("LD", ld_w)], inc="V")
    v_bu = p.op("vector", "tensor_tensor", burow[:], bsrow[:], birow[:], ALU.add, inc="V")
    p.op("vector", "memset", h0[:], 0.0)
    t0 = p.op("tensor", "matmul", PS0[:, 0:512], ones_f[:, 0:128], grow[:, 0:512],
              wait=[("LD", ld_w), ("WT", wt_g)], start=True, stop=True)
    p.op("tensor", "matmul", PS1[:, 0:512], ones_f[:, 0:128], grow[:, 512:1024], start=True, stop=True)
    p.op("tensor", "matmul", PS2[:, 0:512], ones_f[:, 0:128], grow[:, 1024:1536], start=True, stop=True)
    t0 = p.op("tensor", "matmul", PS3[:, 0:512], ones_f[:, 0:128], grow[:, 1536:2048], start=True, stop=True, inc="T")
    p.op("vector", "tensor_copy", lnwb[:, 0:512], PS0[:, 0:512], wait=[("T", t0)])
    p.op("vector", "tensor_copy", lnwb[:, 512:1024], PS1[:, 0:512])
    p.op("vector", "tensor_copy", lnbb[:, 0:512], PS2[:, 0:512])
    p.op("vector", "tensor_copy", lnbb[:, 512:1024], PS3[:, 0:512], inc="V")
    p.barrier()

    xt = [ar.tile(f"xt{i}", [128, 1024], F32) for i in range(2)]
    sqb = ar.tile("sqb", [128, 1024], BF16)
    ln2 = ar.tile("ln2", [128, 1024], F32)
    xnb = ar.tile("xnb", [128, 1024], BF16)
    xnT = ar.tile("xnT", [128, 8, 128], BF16)
    usb = ar.tile("usb", [128, 16], F32)
    stt = ar.tile("stt", [128, NT, 8], F32)
    g_sb = ar.tile("g_sb", [128, 1024], F32)
    t1f = ar.tile("t1f", [128, 1024], F32)
    Rt = [ar.tile(f"Rt{i}", [128, 1024], F32) for i in range(2)]

    def two_ln(tau, half, xt_ap, first_wait):
        mu = stt[:, tau, 4:5]; tmp = stt[:, tau, 5:6]
        r1 = stt[:, tau, 0:1]; n1 = stt[:, tau, 1:2]
        r2 = stt[:, tau, 2:3]; n2 = stt[:, tau, 3:4]
        vr1 = ln_stats2(cx, xt_ap, sqb[:], mu, r1, n1, tmp, wait_v=first_wait, wait_a=first_wait)
        v_l2 = p.op("vector", "tensor_scalar", ln2[:], xt_ap, r1, n1, ALU.mult, ALU.add,
                    wait=[("V", vr1)], inc="V")
        vr2 = ln_stats2(cx, ln2[:], sqb[:], mu, r2, n2, tmp,
                        wait_v=[("V", v_l2)], wait_a=[("V", v_l2)])
        v_xr = p.op("vector", "tensor_scalar", t1f[:], ln2[:], r2, n2, ALU.mult, ALU.add,
                    wait=[("V", vr2)], inc="V")
        v_xa = p.op("vector", "tensor_tensor", t1f[:], t1f[:], lnwb[:], ALU.mult,
                    wait=[("V", v_xr)], inc="V")
        v_xn = p.op("vector", "tensor_tensor", xnb[:], t1f[:], lnbb[:], ALU.add,
                    wait=[("V", v_xa)], inc="V")
        return v_xn

    def xn_transposes(v_xn):
        for c in range(8):
            w = [("V", v_xn)] if c == 0 else []
            tt = p.op("tensor", "transpose", PSB[:, c * 128:(c + 1) * 128],
                      xnb[:, c * 128:(c + 1) * 128], cx.eye_b[:], wait=w, inc="T")
        return p.op("vector", "tensor_copy", xnT[:].rearrange("p a b -> p (a b)"),
                    PSB[:, 0:1024], wait=[("T", tt)], inc="V")

    ep_v = {}
    for tau in range(NT):
        w = [("V", ep_v[tau - 2])] if tau - 2 in ep_v else [("V", p.val("V"))]
        ld = p.op("sync", "dma_start", xt[tau % 2][:], src_dram[tau * 128:(tau + 1) * 128, :],
                  wait=w, inc="LD", incn=16)
        v_xn = two_ln(tau, tau % 2, xt[tau % 2][:], [("LD", ld), ("T", p.val("T"))])
        v_xT = xn_transposes(v_xn)
        p.op("tensor", "matmul", PS0[:, 0:16], ones_b[:, 0:128], burow[:],
             wait=[("V", v_xT), ("V", v_bu), ("WT", wt_g), ("V", p.val("V"))],
             start=True, stop=False)
        for k in range(8):
            t_u = p.op("tensor", "matmul", PS0[:, 0:16], xnT[:, k, :], Wu[:, k, :],
                       start=False, stop=(k == 7), inc="T" if k == 7 else None)
        v_u = p.op("vector", "tensor_copy", usb[:], PS0[:, 0:16], wait=[("T", t_u)], inc="V")
        t_uT = p.op("tensor", "transpose", PS1[0:16, 0:128], usb[:], cx.eye_f[:],
                    wait=[("V", v_u)], inc="T")
        v_UT = p.op("vector", "tensor_copy", UT[:, tau * 128:(tau + 1) * 128], PS1[0:16, 0:128],
                    wait=[("T", t_uT)], inc="V")
        ep_v[tau] = v_UT
    p.barrier()

    a_prev = p.val("A")
    t_prev = None
    for t in range(S):
        wmm = [("A", p.val("A"))] if t == 0 else [("A", a_prev)]
        psr = (PS2 if t % 2 == 0 else PS3)
        rhs = h0[:] if t == 0 else H[:, t - 1:t]
        t_mm = p.op("tensor", "matmul", psr[0:16, 0:1], ATf[:], rhs, wait=wmm,
                    start=True, stop=True, inc="T")
        a_prev = p.op("scalar", "activation", H[:, t:t + 1], psr[0:16, 0:1], AF.Tanh,
                      bias=UT[:, t:t + 1], wait=[("T", t_mm)], inc="A")
    p.barrier()

    st_v = {}
    ep_v = {}
    for tau in range(NT):
        w = [("V", ep_v[tau - 2])] if tau - 2 in ep_v else [("V", p.val("V"))]
        ld = p.op("sync", "dma_start", xt[tau % 2][:], src_dram[tau * 128:(tau + 1) * 128, :],
                  wait=w, inc="LD", incn=16)
        r1 = stt[:, tau, 0:1]; n1 = stt[:, tau, 1:2]
        r2 = stt[:, tau, 2:3]; n2 = stt[:, tau, 3:4]
        v_l2 = p.op("vector", "tensor_scalar", ln2[:], xt[tau % 2][:], r1, n1, ALU.mult, ALU.add,
                    wait=[("LD", ld), ("T", p.val("T")), ("V", p.val("V"))], inc="V")
        v_xr = p.op("vector", "tensor_scalar", t1f[:], ln2[:], r2, n2, ALU.mult, ALU.add,
                    wait=[("V", v_l2)], inc="V")
        v_xa = p.op("vector", "tensor_tensor", t1f[:], t1f[:], lnwb[:], ALU.mult,
                    wait=[("V", v_xr)], inc="V")
        v_xn = p.op("vector", "tensor_tensor", xnb[:], t1f[:], lnbb[:], ALU.add,
                    wait=[("V", v_xa)], inc="V")
        v_xT = xn_transposes(v_xn)
        p.op("tensor", "matmul", PS4[:, 0:512], ones_b[:, 0:128], bgrow[:, 0:512],
             wait=[("V", v_xT)], start=True, stop=False)
        p.op("tensor", "matmul", PS5[:, 0:512], ones_b[:, 0:128], bgrow[:, 512:1024],
             start=True, stop=False)
        for k in range(8):
            p.op("tensor", "matmul", PS4[:, 0:512], xnT[:, k, :], WgT[:, k, 0:512],
                 start=False, stop=(k == 7))
            t_gt = p.op("tensor", "matmul", PS5[:, 0:512], xnT[:, k, :], WgT[:, k, 512:1024],
                        start=False, stop=(k == 7), inc="T" if k == 7 else None)
        a_g1 = p.op("scalar", "activation", g_sb[:, 0:512], PS4[:, 0:512], AF.Sigmoid,
                    wait=[("T", t_gt), ("V", ep_v.get(tau - 1, 0))], inc="A")
        a_g = p.op("scalar", "activation", g_sb[:, 512:1024], PS5[:, 0:512], AF.Sigmoid, inc="A")
        p.op("tensor", "matmul", PS0[:, 0:512], ones_f[:, 0:128], borow[:, 0:512],
             wait=[("V", p.val("V"))], start=True, stop=False)
        p.op("tensor", "matmul", PS0[:, 0:512], H[:, tau * 128:(tau + 1) * 128], WoTf[:, 0:512],
             start=False, stop=True)
        p.op("tensor", "matmul", PS1[:, 0:512], ones_f[:, 0:128], borow[:, 512:1024],
             start=True, stop=False)
        t_o = p.op("tensor", "matmul", PS1[:, 0:512], H[:, tau * 128:(tau + 1) * 128],
                   WoTf[:, 512:1024], start=False, stop=True, inc="T")
        wst = [("ST", st_v[tau - 2])] if tau - 2 in st_v else []
        p.op("vector", "tensor_tensor", t1f[:, 0:512], PS0[:, 0:512], g_sb[:, 0:512], ALU.mult,
             wait=[("T", t_o), ("A", a_g)] + wst)
        v_t1 = p.op("vector", "tensor_tensor", t1f[:, 512:1024], PS1[:, 0:512], g_sb[:, 512:1024],
                    ALU.mult, inc="V")
        v_t2 = p.op("vector", "tensor_tensor", t1f[:], t1f[:], ln2[:], ALU.add,
                    wait=[("V", v_t1)], inc="V")
        v_ep = p.op("vector", "tensor_tensor", Rt[tau % 2][:], t1f[:], xt[tau % 2][:], ALU.add,
                    wait=[("V", v_t2)], inc="V")
        ep_v[tau] = v_ep
        st_v[tau] = p.op("sync", "dma_start", dst_dram[tau * 128:(tau + 1) * 128, :],
                         Rt[tau % 2][:], wait=[("V", v_ep)], inc="ST", incn=16)
    p.barrier()


def emit_l1f_init(cx, src_dram, dst_dram):
    p, ar = cx.p, cx.ar
    stage_fence(cx)
    ar.reset()
    xt = [ar.tile(f"xt{i}", [128, 1024], F32) for i in range(2)]
    ot = [ar.tile(f"ot{i}", [128, 1024], F32) for i in range(2)]
    sqb = ar.tile("sqb", [128, 1024], BF16)
    stt = ar.tile("stt", [128, 4], F32)
    st_v = {}
    ep_v = {}
    for tau in range(NT):
        w = [("V", ep_v[tau - 2])] if tau - 2 in ep_v else []
        ld = p.op("sync", "dma_start", xt[tau % 2][:], src_dram[tau * 128:(tau + 1) * 128, :],
                  wait=w, inc="LD", incn=16)
        mu = stt[:, 0:1]; rstd = stt[:, 1:2]; nmr = stt[:, 2:3]; tmp = stt[:, 3:4]
        vr = ln_stats2(cx, xt[tau % 2][:], sqb[:], mu, rstd, nmr, tmp,
                       wait_v=[("LD", ld)], wait_a=[("LD", ld)])
        wst = [("ST", st_v[tau - 2])] if tau - 2 in st_v else []
        v_ep = p.op("vector", "tensor_scalar", ot[tau % 2][:], xt[tau % 2][:], rstd, nmr,
                    ALU.mult, ALU.add, wait=[("V", vr)] + wst, inc="V")
        ep_v[tau] = v_ep
        st_v[tau] = p.op("sync", "dma_start", dst_dram[tau * 128:(tau + 1) * 128, :],
                         ot[tau % 2][:], wait=[("V", v_ep)], inc="ST", incn=16)
    p.barrier()


def emit_l1f_attn(cx, lay, x2):
    p, nc, ar, d = cx.p, cx.nc, cx.ar, cx.dram
    stage_fence(cx)
    ar.reset()
    PS0, PS1, PS2, PS3, PS4, PS5 = cx.PS0, cx.PS1, cx.PS2, cx.PS3, cx.PS4, cx.PS5
    PSR, PSB = cx.PSR, cx.PSB
    ones_b = cx.ones_b

    Wqkv = ar.tile("Wqkv", [128, 8, 3072], BF16)
    Wo = ar.tile("Wo", [128, 8, 1024], BF16)
    tabs = ar.tile("tabs", [128, 4, 32, 64], BF16)
    tbl = ar.tile("tbl", [128, 32, 64], BF16)
    srow = ar.tile("srow", [1, 4, 64], BF16)
    sb64 = ar.tile("sb64", [128, 4, 64], BF16)
    mask1 = ar.tile("mask1", [128, 384], F32)
    kTr = ar.tile("kTr", [128, 8, 3, 128], BF16)
    vr_ = ar.tile("vr", [128, 3, 1024], BF16)
    xs = [ar.tile(f"xs{i}", [128, 512], F32) for i in range(2)]
    x2u = [ar.tile(f"x2u{i}", [128, 512], F32) for i in range(3)]
    S1f = ar.tile("S1f", [128, 128], F32)
    E127f = ar.tile("E127f", [128, 128], F32)
    xsb = ar.tile("xsb", [128, 1024], BF16)
    xsT = ar.tile("xsT", [128, 8, 128], BF16)
    q_t = ar.tile("q_t", [128, 1024], F32)
    k_t = ar.tile("k_t", [128, 1024], F32)
    sqq = ar.tile("sqq", [128, 1024], BF16)
    ss16 = ar.tile("ss16", [128, 16], F32)
    l2r = ar.tile("l2r", [128, 16], F32)
    qn = ar.tile("qn", [128, 1024], BF16)
    ta = ar.tile("ta", [128, 1024], BF16)
    tb = ar.tile("tb", [128, 1024], BF16)
    qp = ar.tile("qp", [128, 1024], BF16)
    qT = ar.tile("qT", [128, 8, 128], BF16)
    smf = ar.tile("smf", [128, 2, 384], F32)
    pb = ar.tile("pb", [128, 2, 384], BF16)
    pT = ar.tile("pT", [128, 2, 3, 128], BF16)
    nmd = ar.tile("nmd", [128, 2, 4], F32)
    o_t = ar.tile("o_t", [128, 1024], BF16)
    oT = ar.tile("oT", [128, 8, 128], BF16)
    X2n = [ar.tile(f"X2n{i}", [128, 1024], F32) for i in range(2)]

    p.op("gpsimd", "dma_start", Wqkv[:], d["L1_ffn_Wqkv"][lay].rearrange("(k p) n -> p k n", p=128),
         inc="WT", incn=16)
    p.op("gpsimd", "dma_start", Wo[:], d["L1_ffn_Wo"][lay].rearrange("(k p) n -> p k n", p=128),
         inc="WT", incn=16)
    p.op("gpsimd", "dma_start", mask1[:], d["T_mask1"], inc="WT", incn=16)
    p.op("sync", "dma_start", S1f[:], d["T_shift1"], inc="LD", incn=16)
    ld_sh = p.op("sync", "dma_start", E127f[:], d["T_e127"], inc="LD", incn=16)
    p.op("gpsimd", "dma_start", srow[:, 0, :], d["L1_ffn_q_scale"][lay].unsqueeze(0), inc="WT", incn=16)
    p.op("gpsimd", "dma_start", srow[:, 1, :], d["T_qsrot"][lay].unsqueeze(0), inc="WT", incn=16)
    p.op("gpsimd", "dma_start", srow[:, 2, :], d["L1_ffn_k_scale"][lay].unsqueeze(0), inc="WT", incn=16)
    wt_s = p.op("gpsimd", "dma_start", srow[:, 3, :], d["T_ksrot"][lay].unsqueeze(0), inc="WT", incn=16)
    for j in range(4):
        w = [("WT", wt_s)] if j == 0 else []
        t_b = p.op("tensor", "matmul", PS0[:, j * 64:(j + 1) * 64], ones_b[:, 0:128],
                   srow[:, j, :], wait=w, start=True, stop=True, inc="T")
    v_s = p.op("vector", "tensor_copy", sb64[:].rearrange("p a b -> p (a b)"), PS0[:, 0:256],
               wait=[("T", t_b)], inc="V")
    v_tb = v_s
    for j, src in enumerate(["T_CS", "T_SSgn", "T_CI", "T_SIgn"]):
        wld = p.op("gpsimd", "dma_start", tbl[:], d[src].rearrange("(t p) n -> p t n", p=128),
                   wait=[("V", v_tb)], inc="WT", incn=16)
        v_tb = p.op("vector", "tensor_tensor", tabs[:, j, :, :], tbl[:],
                    sb64[:, j, :].unsqueeze(1).broadcast_to((128, 32, 64)), ALU.mult,
                    wait=[("WT", wld), ("V", v_s)], inc="V")
        dbg_tap(cx, "l1f_tab", tabs[:, j, 0:16, :].rearrange("p a b -> p (a b)"),
                (j * 128, j * 128 + 128), wait=[("V", v_tb)])
    p.barrier()

    st_v = {}
    ld_v = {}
    ep_v = {}
    cx.simb_free = [0, 0]
    cx.smf_free = [0, 0]

    def load_xs(tau):
        w = [("V", ep_v[tau - 2])] if tau - 2 in ep_v else []
        p.op("sync", "dma_start", xs[tau % 2][:], x2[tau * 128:(tau + 1) * 128, 0:512],
             wait=w, inc="LD", incn=16)
        return p.op("sync", "dma_start", x2u[tau % 3][:], x2[tau * 128:(tau + 1) * 128, 512:1024],
                    inc="LD", incn=16)

    def rot_apply(src_f32, tA, tB, out_bf, wsem, v_in):
        a_sq = p.op("scalar", "activation", sqq[:], src_f32[:], AF.Square,
                    wait=[(wsem, v_in)], inc="A")
        v1 = p.op("vector", "reduce_sum", ss16[:],
                  sqq[:].rearrange("p (h f) -> p h f", h=16), axis=AX.X,
                  wait=[("A", a_sq)], inc="V")
        a_sr = p.op("scalar", "activation", ss16[:], ss16[:], AF.Sqrt,
                    wait=[("V", v1)], inc="A")
        v2 = p.op("vector", "tensor_scalar_max", ss16[:], ss16[:], 1e-12,
                  wait=[("A", a_sr)], inc="V")
        v3 = p.op("vector", "reciprocal", l2r[:], ss16[:], wait=[("V", v2)], inc="V")
        v4 = p.op("vector", "tensor_tensor", qn[:].rearrange("p (h f) -> p h f", h=16),
                  src_f32[:].rearrange("p (h f) -> p h f", h=16),
                  l2r[:].unsqueeze(2).broadcast_to((128, 16, 64)), ALU.mult,
                  wait=[("V", v3)], inc="V")
        v5 = p.op("vector", "tensor_tensor", ta[:].rearrange("p (h f) -> p h f", h=16),
                  qn[:].rearrange("p (h f) -> p h f", h=16),
                  tA.unsqueeze(1).broadcast_to((128, 16, 64)), ALU.mult,
                  wait=[("V", v4)], inc="V")
        p.op("vector", "tensor_tensor", tb[:].rearrange("p (h u f) -> p h u f", h=16, u=2)[:, :, 0, :],
             qn[:].rearrange("p (h u f) -> p h u f", h=16, u=2)[:, :, 1, :],
             tB[:, 0:32].unsqueeze(1).broadcast_to((128, 16, 32)), ALU.mult,
             wait=[("V", v4)])
        v6 = p.op("vector", "tensor_tensor", tb[:].rearrange("p (h u f) -> p h u f", h=16, u=2)[:, :, 1, :],
                  qn[:].rearrange("p (h u f) -> p h u f", h=16, u=2)[:, :, 0, :],
                  tB[:, 32:64].unsqueeze(1).broadcast_to((128, 16, 32)), ALU.mult, inc="V")
        v7 = p.op("vector", "tensor_tensor", out_bf[:], ta[:], tb[:], ALU.add,
                  wait=[("V", v6), ("V", v5)], inc="V")
        if DBG in ("l1f_qn", "l1f_ta", "l1f_tb") and out_bf is qp:
            src_map = {"l1f_qn": qn, "l1f_ta": ta, "l1f_tb": tb}
            dbg_tap(cx, DBG, src_map[DBG][:], (dbg_rows[0], dbg_rows[1]), wait=[("V", v7)])
        return v7

    def transp8(src_bf, dst, v_in, extra=()):
        for c in range(8):
            w = [("V", v_in)] + list(extra) if c == 0 else []
            tt = p.op("tensor", "transpose", PSB[:, c * 128:(c + 1) * 128],
                      src_bf[:, c * 128:(c + 1) * 128], cx.eye_b[:], wait=w, inc="T")
        return p.op("vector", "tensor_copy", dst, PSB[:, 0:1024].rearrange("p (a b) -> p a b", a=8),
                    wait=[("T", tt)], inc="V")

    ld_v[0] = load_xs(0)
    for tau in range(NT):
        slot = tau % 3
        ld = ld_v[tau]
        if tau + 1 < NT:
            ld_v[tau + 1] = load_xs(tau + 1)
        w0 = [("LD", ld), ("LD", ld_sh), ("V", p.val("V"))]
        if tau == 0:
            t_sh = p.op("tensor", "matmul", PSR[:, 0:512], S1f[:], x2u[0][:],
                        wait=w0, start=True, stop=True, inc="T")
        else:
            p.op("tensor", "matmul", PSR[:, 0:512], S1f[:], x2u[tau % 3][:],
                 wait=w0, start=True, stop=False)
            t_sh = p.op("tensor", "matmul", PSR[:, 0:512], E127f[:], x2u[(tau - 1) % 3][:],
                        start=False, stop=True, inc="T")
        v_xb = p.op("vector", "tensor_copy", xsb[:, 0:512], xs[tau % 2][:],
                    wait=[("LD", ld), ("T", p.val("T"))], inc="V")
        v_sh = p.op("vector", "tensor_copy", xsb[:, 512:1024], PSR[:, 0:512],
                    wait=[("T", t_sh)], inc="V")
        v_xT = transp8(xsb, xsT[:], v_sh)
        dbg_tap(cx, "l1f_xs", xsb[:], (tau * 128, tau * 128 + 128), wait=[("V", v_sh)])
        dbg_tap(cx, "l1f_xsb", xsb[:], (tau * 128, tau * 128 + 128), wait=[("V", v_xb)])
        dbg_tap(cx, "l1f_xsT", xsT[:].rearrange("p a b -> p (a b)"), (tau * 128, tau * 128 + 128),
                wait=[("V", v_xT)])
        for k in range(8):
            w = [("V", v_xT), ("WT", p.val("WT")), ("V", p.val("V"))] if k == 0 else []
            st = (k == 0)
            sp = (k == 7)
            p.op("tensor", "matmul", PS0[:, 0:512], xsT[:, k, :], Wqkv[:, k, 0:512],
                 wait=w, start=st, stop=sp)
            p.op("tensor", "matmul", PS1[:, 0:512], xsT[:, k, :], Wqkv[:, k, 512:1024], start=st, stop=sp)
            p.op("tensor", "matmul", PS2[:, 0:512], xsT[:, k, :], Wqkv[:, k, 1024:1536], start=st, stop=sp)
            p.op("tensor", "matmul", PS3[:, 0:512], xsT[:, k, :], Wqkv[:, k, 1536:2048], start=st, stop=sp)
            p.op("tensor", "matmul", PS4[:, 0:512], xsT[:, k, :], Wqkv[:, k, 2048:2560], start=st, stop=sp)
            t_qkv = p.op("tensor", "matmul", PS5[:, 0:512], xsT[:, k, :], Wqkv[:, k, 2560:3072],
                         start=st, stop=sp, inc="T" if sp else None)
        a_q1 = p.op("scalar", "copy", q_t[:, 0:512], PS0[:, 0:512], wait=[("T", t_qkv)], inc="A")
        a_q2 = p.op("scalar", "copy", q_t[:, 512:1024], PS1[:, 0:512], inc="A")
        p.op("vector", "tensor_copy", k_t[:, 0:512], PS2[:, 0:512], wait=[("T", t_qkv)])
        v_k = p.op("vector", "tensor_copy", k_t[:, 512:1024], PS3[:, 0:512], inc="V")
        p.op("vector", "tensor_copy", vr_[:, slot, 0:512], PS4[:, 0:512])
        v_v = p.op("vector", "tensor_copy", vr_[:, slot, 512:1024], PS5[:, 0:512], inc="V")
        dbg_rows = (tau * 128, tau * 128 + 128)
        dbg_tap(cx, "l1f_qt", q_t[:], (tau * 128, tau * 128 + 128), wait=[("A", a_q2)])
        v_qp = rot_apply(q_t, tabs[:, 0, tau, :], tabs[:, 1, tau, :], qp, "A", a_q2)
        v_qT = transp8(qp, qT[:], v_qp)
        v_kp = rot_apply(k_t, tabs[:, 2, tau, :], tabs[:, 3, tau, :], qp, "V", v_k)
        v_kT = transp8(qp, kTr[:, :, slot, :], v_kp)
        nc_ = min(tau + 1, 3)
        for qh in range(16):
            par = qh % 2
            SIMB = PS0 if par == 0 else PS2
            PVB = PS1 if par == 0 else PS3
            b = par * 64
            qsl = qT[b:b + 64, qh // 2, :]
            for j in range(nc_):
                ksl = kTr[b:b + 64, qh // 2, (tau - nc_ + 1 + j) % 3, :]
                w = []
                if j == 0:
                    w = [("V", v_kT)]
                    if cx.simb_free[par]:
                        w.append(("V", cx.simb_free[par]))
                t_sim = p.op("tensor", "matmul", SIMB[:, j * 128:(j + 1) * 128], qsl, ksl,
                             wait=w, start=True, stop=True, inc="T" if j == nc_ - 1 else None)
            mw = 384 - nc_ * 128
            v_m = p.op("vector", "scalar_tensor_tensor", smf[:, par, 0:nc_ * 128],
                       SIMB[:, 0:nc_ * 128], 8.0, mask1[:, mw:384], ALU.mult, ALU.add,
                       wait=[("T", t_sim), ("A", cx.smf_free[par])], inc="V")
            cx.simb_free[par] = v_m
            v_nm = p.op("vector", "reduce_max", nmd[:, par, 0:1], smf[:, par, 0:nc_ * 128],
                        axis=AX.X, negate=True, wait=[("V", v_m)], inc="V")
            a_e = p.op("scalar", "activation", pb[:, par, 0:nc_ * 128], smf[:, par, 0:nc_ * 128],
                       AF.Exp, bias=nmd[:, par, 0:1], accum_out=nmd[:, par, 1:2],
                       wait=[("V", v_nm)], inc="A")
            cx.smf_free[par] = a_e
            v_rc = p.op("vector", "reciprocal", nmd[:, par, 2:3], nmd[:, par, 1:2],
                        wait=[("A", a_e)], inc="V")
            for j in range(nc_):
                w = [("V", v_rc)] if j == 0 else []
                tt = p.op("tensor", "transpose", PSB[:, par * 512 + j * 128:par * 512 + (j + 1) * 128],
                          pb[:, par, j * 128:(j + 1) * 128], cx.eye_b[:], wait=w, inc="T")
            v_pT = p.op("vector", "tensor_copy", pT[:, par, 0:nc_, :].rearrange("p a b -> p (a b)"),
                        PSB[:, par * 512:par * 512 + nc_ * 128], wait=[("T", tt)], inc="V")
            for j in range(nc_):
                w = [("V", v_pT)] if j == 0 else []
                t_pv = p.op("tensor", "matmul", PVB[:, 0:64], pT[:, par, j, :],
                            vr_[:, (tau - nc_ + 1 + j) % 3, qh * 64:(qh + 1) * 64],
                            wait=w, start=(j == 0), stop=(j == nc_ - 1),
                            inc="T" if j == nc_ - 1 else None)
            v_o = p.op("vector", "tensor_scalar", o_t[:, qh * 64:(qh + 1) * 64], PVB[:, 0:64],
                       nmd[:, par, 2:3], None, ALU.mult, wait=[("T", t_pv), ("V", v_rc)], inc="V")
        dbg_tap(cx, "l1f_ot", o_t[:], (tau * 128, tau * 128 + 128), wait=[("V", v_o)])
        dbg_tap(cx, "l1f_qp", qT[:].rearrange("p a b -> p (a b)"), (tau * 128, tau * 128 + 128),
                wait=[("V", v_qT)])
        v_oT = transp8(o_t, oT[:], v_o)
        for k in range(8):
            w = [("V", v_oT)] if k == 0 else []
            st = (k == 0)
            sp = (k == 7)
            p.op("tensor", "matmul", PS2[:, 0:512], oT[:, k, :], Wo[:, k, 0:512],
                 wait=w, start=st, stop=sp)
            t_wo = p.op("tensor", "matmul", PS3[:, 0:512], oT[:, k, :], Wo[:, k, 512:1024],
                        start=st, stop=sp, inc="T" if sp else None)
        wst = [("ST", st_v[tau - 2])] if tau - 2 in st_v else []
        p.op("vector", "tensor_tensor", X2n[tau % 2][:, 0:512], PS2[:, 0:512],
             xs[tau % 2][:, 0:512], ALU.add, wait=[("T", t_wo)] + wst)
        v_ep = p.op("vector", "tensor_tensor", X2n[tau % 2][:, 512:1024], PS3[:, 0:512],
                    x2u[tau % 3][:], ALU.add, inc="V")
        ep_v[tau] = v_ep
        st_v[tau] = p.op("sync", "dma_start", x2[tau * 128:(tau + 1) * 128, :], X2n[tau % 2][:],
                         wait=[("V", v_ep)], inc="ST", incn=16)
        if DBG_SERIAL and tau < 3:
            stage_fence(cx)
    p.barrier()


def emit_l1f_ffn(cx, lay, x2):
    p, nc, ar, d = cx.p, cx.nc, cx.ar, cx.dram
    stage_fence(cx)
    ar.reset()
    PS = [cx.PS0, cx.PS1, cx.PS2, cx.PS3, cx.PS4, cx.PS5]
    PSR, PSB = cx.PSR, cx.PSB
    ones_b, ones_f = cx.ones_b, cx.ones_f
    NKF = 22

    W1p = ar.tile("W1p", [128, 8, 5632], BF16)
    W2s = [ar.tile(f"W2s{i}", [128, 1024], BF16) for i in range(3)]
    b1c = ar.tile("b1c", [128, 44], F32)
    b2row = ar.tile("b2row", [1, 1024], BF16)
    grow = ar.tile("grow", [1, 2048], F32)
    lnwb = ar.tile("lnwb", [128, 1024], F32)
    lnbb = ar.tile("lnbb", [128, 1024], F32)
    xsg = [ar.tile(f"xsg{i}", [128, 3, 1024], F32) for i in range(2)]
    x2u = [ar.tile(f"x2u{i}", [128, 3, 512], F32) for i in range(2)]
    S1f = ar.tile("S1f", [128, 128], F32)
    E127f = ar.tile("E127f", [128, 128], F32)
    sqb = ar.tile("sqb", [128, 1024], BF16)
    lnb = ar.tile("lnb", [128, 1024], BF16)
    xnT = ar.tile("xnT", [128, 8, 3, 128], BF16)
    stt = ar.tile("stt", [128, 3, 4], F32)
    a_sb = [ar.tile(f"a_sb{i}", [128, 384], BF16) for i in range(2)]
    gel = [ar.tile(f"gel{i}", [128, 384], BF16) for i in range(2)]
    hk = [ar.tile(f"hk{i}", [128, 384], BF16) for i in range(2)]
    hx_dbg = ar.tile("hx_dbg", [128, 512], F32)
    X2n = ar.tile("X2n", [128, 3, 1024], F32)

    p.op("gpsimd", "dma_start", W1p[:], d["T_W1p"][lay].rearrange("(k p) n -> p k n", p=128),
         inc="WT", incn=16)
    p.op("sync", "dma_start", b1c[:], d["T_b1c"][lay], inc="LD", incn=16)
    p.op("sync", "dma_start", S1f[:], d["T_shift1"], inc="LD", incn=16)
    ld_sh = p.op("sync", "dma_start", E127f[:], d["T_e127"], inc="LD", incn=16)
    wt_b = p.op("gpsimd", "dma_start", b2row[:], d["L1_ffn_b2"][lay].unsqueeze(0), inc="WT", incn=16)
    p.op("sync", "dma_start", grow[:, 0:1024], d["L1_ffn_ln_g"][lay].unsqueeze(0), inc="LD", incn=16)
    ld_g = p.op("sync", "dma_start", grow[:, 1024:2048], d["L1_ffn_ln_b"][lay].unsqueeze(0),
                inc="LD", incn=16)
    t0 = p.op("tensor", "matmul", cx.PS0[:, 0:512], ones_f[:, 0:128], grow[:, 0:512],
              wait=[("LD", ld_g)], start=True, stop=True)
    p.op("tensor", "matmul", cx.PS1[:, 0:512], ones_f[:, 0:128], grow[:, 512:1024], start=True, stop=True)
    p.op("tensor", "matmul", cx.PS2[:, 0:512], ones_f[:, 0:128], grow[:, 1024:1536], start=True, stop=True)
    t0 = p.op("tensor", "matmul", cx.PS3[:, 0:512], ones_f[:, 0:128], grow[:, 1536:2048],
              start=True, stop=True, inc="T")
    p.op("vector", "tensor_copy", lnwb[:, 0:512], cx.PS0[:, 0:512], wait=[("T", t0)])
    p.op("vector", "tensor_copy", lnwb[:, 512:1024], cx.PS1[:, 0:512])
    p.op("vector", "tensor_copy", lnbb[:, 0:512], cx.PS2[:, 0:512])
    p.op("vector", "tensor_copy", lnbb[:, 512:1024], cx.PS3[:, 0:512], inc="V")
    p.barrier()

    groups = [list(range(g, min(g + 3, NT))) for g in range(0, NT, 3)]
    st_v = {}
    nst = 0
    ep_v = 0
    ld_blk = {}
    w2_wt = {}

    def load_block(bi):
        tiles = groups[bi]
        buf = bi % 2
        last = None
        for ti, tau in enumerate(tiles):
            p.op("sync", "dma_start", xsg[buf][:, ti, 0:512],
                 x2[tau * 128:(tau + 1) * 128, 0:512], inc="LD", incn=16)
            last = p.op("sync", "dma_start", x2u[buf][:, ti, :],
                        x2[tau * 128:(tau + 1) * 128, 512:1024], inc="LD", incn=16)
        return last

    ld_blk[0] = load_block(0)
    w2_wt[0] = p.op("gpsimd", "dma_start", W2s[0][:], d["T_W2p"][lay][0:128, :], inc="WT", incn=16)
    v_h_hist = {}
    t_y_hist = {}
    a_gl_prev = 0
    kf_glob = [0]
    for bi, tiles in enumerate(groups):
        G = len(tiles)
        buf = bi % 2
        ld = ld_blk[bi]
        v_shift = 0
        for ti, tau in enumerate(tiles):
            w0 = [("LD", ld), ("LD", ld_sh), ("V", p.val("V")), ("A", p.val("A"))]
            if tau == 0:
                t_sh = p.op("tensor", "matmul", PSR[:, 0:512], S1f[:], x2u[buf][:, 0, :],
                            wait=w0, start=True, stop=True, inc="T")
            else:
                prev = x2u[buf][:, ti - 1, :] if ti > 0 else x2u[1 - buf][:, len(groups[bi - 1]) - 1, :]
                p.op("tensor", "matmul", PSR[:, 0:512], S1f[:], x2u[buf][:, ti, :],
                     wait=w0, start=True, stop=False)
                t_sh = p.op("tensor", "matmul", PSR[:, 0:512], E127f[:], prev,
                            start=False, stop=True, inc="T")
            v_shift = p.op("vector", "tensor_copy", xsg[buf][:, ti, 512:1024], PSR[:, 0:512],
                           wait=[("T", t_sh), ("V", ep_v)], inc="V")
        v_xT = {}
        for ti in range(G):
            mu = stt[:, ti, 0:1]; rstd = stt[:, ti, 1:2]; nmr = stt[:, ti, 2:3]; tmp = stt[:, ti, 3:4]
            vr = ln_stats2(cx, xsg[buf][:, ti, :], sqb[:], mu, rstd, nmr, tmp,
                           wait_v=[("LD", ld), ("V", v_shift)], wait_a=[("LD", ld), ("V", v_shift)])
            v1 = p.op("vector", "tensor_scalar", lnb[:], xsg[buf][:, ti, :], rstd, nmr,
                      ALU.mult, ALU.add, wait=[("V", vr), ("T", p.val("T"))], inc="V")
            v2 = p.op("vector", "tensor_tensor", lnb[:], lnb[:], lnwb[:], ALU.mult,
                      wait=[("V", v1)], inc="V")
            v3 = p.op("vector", "tensor_tensor", lnb[:], lnb[:], lnbb[:], ALU.add,
                      wait=[("V", v2)], inc="V")
            dbg_tap(cx, "l1f_xn", lnb[:], (tiles[ti] * 128, tiles[ti] * 128 + 128),
                    wait=[("V", v3)])
            for c in range(8):
                w = [("V", v3)] if c == 0 else []
                tt = p.op("tensor", "transpose", PSB[:, c * 128:(c + 1) * 128],
                          lnb[:, c * 128:(c + 1) * 128], cx.eye_b[:], wait=w, inc="T")
            v_xT[ti] = p.op("vector", "tensor_copy", xnT[:, :, ti, :],
                            PSB[:, 0:1024].rearrange("p (a b) -> p a b", a=8),
                            wait=[("T", tt)], inc="V")
        if bi + 1 < len(groups):
            p.wait("sync", [("V", v_shift)])
            ld_blk[bi + 1] = load_block(bi + 1)
        NB = G * 128
        xn_rhs = xnT[:].rearrange("p a g b -> p a (g b)")
        t_y = {}
        NKF_RUN = DBG_NKF if DBG_NKF else NKF

        def emit_w2(kf, kg):
            sp = (kf == NKF_RUN - 1)
            hkk = hk[kg % 2]
            v_h = v_h_hist[kg]
            for ti in range(G):
                if kf == 0:
                    p.op("tensor", "matmul", PS[2 * ti][:, 0:512], ones_b[:, 0:128],
                         b2row[:, 0:512],
                         wait=([("V", v_h), ("WT", w2_wt[kg % 3])] if ti == 0 else []),
                         start=True, stop=False)
                    p.op("tensor", "matmul", PS[2 * ti + 1][:, 0:512], ones_b[:, 0:128],
                         b2row[:, 512:1024], start=True, stop=False)
                w = [("V", v_h), ("WT", w2_wt[kg % 3])] if (ti == 0 and kf > 0) else []
                p.op("tensor", "matmul", PS[2 * ti][:, 0:512], hkk[:, ti * 128:(ti + 1) * 128],
                     W2s[kg % 3][:, 0:512], wait=w, start=False, stop=sp)
                t_y[ti] = p.op("tensor", "matmul", PS[2 * ti + 1][:, 0:512],
                               hkk[:, ti * 128:(ti + 1) * 128], W2s[kg % 3][:, 512:1024],
                               start=False, stop=sp, inc="T")
            t_y_hist[kg] = t_y[G - 1]

        for kf in range(NKF_RUN):
            kg0 = kf_glob[0]
            if kf + 1 < NKF_RUN or bi + 1 < len(groups):
                nxt = kf + 1 if kf + 1 < NKF_RUN else 0
                p.wait("gpsimd", [("T", p.val("T"))])
                w2_wt[(kg0 + 1) % 3] = p.op("gpsimd", "dma_start", W2s[(kg0 + 1) % 3][:],
                                            d["T_W2p"][lay][nxt * 128:(nxt + 1) * 128, :],
                                            inc="WT", incn=16)
            for k in range(8):
                w = [("V", v_xT[G - 1]), ("A", a_gl_prev), ("WT", wt_b)] if k == 0 else []
                t_a = p.op("tensor", "matmul", PSR[:, 0:NB], W1p[:, k, kf * 128:(kf + 1) * 128],
                           xn_rhs[:, k, 0:NB], wait=w, start=(k == 0), stop=(k == 7),
                           inc="T" if k == 7 else None)
            v_a = p.op("vector", "tensor_scalar", a_sb[kg0 % 2][:, 0:NB], PSR[:, 0:NB],
                       b1c[:, kf:kf + 1], None, ALU.add, wait=[("T", t_a)], inc="V")
            for k in range(8):
                w = [("V", v_a)] if k == 0 else []
                t_g = p.op("tensor", "matmul", PSR[:, 0:NB],
                           W1p[:, k, 2816 + kf * 128:2816 + (kf + 1) * 128],
                           xn_rhs[:, k, 0:NB], wait=w, start=(k == 0), stop=(k == 7),
                           inc="T" if k == 7 else None)
            kg = kg0
            a_gl = p.op("scalar", "activation", gel[kg % 2][:, 0:NB], PSR[:, 0:NB], AF.Gelu,
                        bias=b1c[:, 22 + kf:23 + kf],
                        wait=[("T", t_g), ("V", v_h_hist.get(kg - 2, 0))], inc="A")
            a_gl_prev = a_gl
            v_h = p.op("vector", "tensor_tensor", hk[kg % 2][:, 0:NB], a_sb[kg % 2][:, 0:NB],
                       gel[kg % 2][:, 0:NB], ALU.mult,
                       wait=[("A", a_gl), ("T", t_y_hist.get(kg - 2, 0))], inc="V")
            v_h_hist[kg] = v_h
            kf_glob[0] += 1
            if kf > 0:
                emit_w2(kf - 1, kg - 1)
        emit_w2(NKF_RUN - 1, kf_glob[0] - 1)
        for ti, tau in enumerate(tiles):
            wst = [("ST", st_v[nst - 3])] if nst - 3 in st_v else []
            p.op("vector", "tensor_tensor", X2n[:, ti, 0:512], PS[2 * ti][:, 0:512],
                 xsg[buf][:, ti, 0:512], ALU.add, wait=wst + [("T", t_y[ti])])
            v_ep = p.op("vector", "tensor_tensor", X2n[:, ti, 512:1024], PS[2 * ti + 1][:, 0:512],
                        x2u[buf][:, ti, :], ALU.add, inc="V")
            st_v[nst] = p.op("sync", "dma_start", x2[tau * 128:(tau + 1) * 128, :],
                             X2n[:, ti, :], wait=[("V", v_ep)], inc="ST", incn=16)
            nst += 1
        ep_v = v_ep
    p.barrier()


def emit_fin(cx, a_dram, b_dram, out_dram):
    p, ar = cx.p, cx.ar
    stage_fence(cx)
    ar.reset()
    at = [ar.tile(f"fa{i}", [128, 1024], F32) for i in range(2)]
    bt = [ar.tile(f"fb{i}", [128, 1024], F32) for i in range(2)]
    ot = [ar.tile(f"fo{i}", [128, 1024], F32) for i in range(2)]
    st_v = {}
    for tau in range(NT):
        w = [("ST", st_v[tau - 2])] if tau - 2 in st_v else []
        p.op("sync", "dma_start", at[tau % 2][:], a_dram[tau * 128:(tau + 1) * 128, :],
             wait=w, inc="LD", incn=16)
        ld = p.op("sync", "dma_start", bt[tau % 2][:], b_dram[tau * 128:(tau + 1) * 128, :],
                  inc="LD", incn=16)
        v_o = p.op("vector", "tensor_tensor", ot[tau % 2][:], at[tau % 2][:], bt[tau % 2][:],
                   ALU.add, wait=[("LD", ld), ("ST", st_v.get(tau - 2, 0))], inc="V")
        st_v[tau] = p.op("sync", "dma_start", out_dram[tau * 128:(tau + 1) * 128, :],
                         ot[tau % 2][:], wait=[("V", v_o)], inc="ST", incn=16)
    p.barrier()


def build_nc(stages):
    nc = bass.Bass("TRN2", target_bir_lowering=False)
    cx = Ctx()
    cx.nc = nc
    cx.sbuf_usable0 = int(nc.sbuf_bytes_remaining)
    cx.p = P(nc)
    p = cx.p
    for s in ("LD", "WT", "ST", "T", "V", "A", "G"):
        p.sem(s)

    d = {}
    d["x"] = nc.dram_tensor("x", [S, D], F32, kind="ExternalInput").ap()
    WLIST = [
        ("L0_attn_Wq", [D, D]), ("L0_attn_bq", [D]), ("L0_attn_Wk", [D, 256]), ("L0_attn_bk", [256]),
        ("L0_attn_Wv", [D, 256]), ("L0_attn_bv", [256]), ("L0_attn_Wc", [256, 64]), ("L0_attn_We", [64, 256]),
        ("L0_attn_Wo", [D, D]), ("L0_attn_bo", [D]), ("L0_attn_gamma", [D]), ("L0_attn_beta", [D]),
        ("L0_ffn_gate_w", [8, D]), ("L0_ffn_wg", [8, 512, D]), ("L0_ffn_wu", [8, 512, D]),
        ("L0_ffn_wd", [8, D, 512]), ("L0_ffn_sg", [512, D]), ("L0_ffn_su", [512, D]), ("L0_ffn_sd", [D, 512]),
        ("L1_attn_Ws", [16, D]), ("L1_attn_bs", [16]), ("L1_attn_Wi", [16, D]), ("L1_attn_bi", [16]),
        ("L1_attn_Wo", [D, 16]), ("L1_attn_bo", [D]), ("L1_attn_Wg", [D, D]), ("L1_attn_bg", [D]),
        ("L1_attn_ln_w", [D]), ("L1_attn_ln_b", [D]), ("L1_attn_A", [16, 16]),
        ("L1_ffn_Wqkv", [6, D, 3072]), ("L1_ffn_q_scale", [6, 64]), ("L1_ffn_k_scale", [6, 64]),
        ("L1_ffn_Wo", [6, D, D]), ("L1_ffn_ln_g", [6, D]), ("L1_ffn_ln_b", [6, D]),
        ("L1_ffn_W1", [6, D, 5460]), ("L1_ffn_b1", [6, 5460]), ("L1_ffn_W2", [6, 2730, D]),
        ("L1_ffn_b2", [6, D]),
        ("T_eye", [128, 128]), ("T_mask0", [256, 256]), ("T_mask1", [128, 384]),
        ("T_CS", [S, 64]), ("T_SSgn", [S, 64]), ("T_CI", [S, 64]), ("T_SIgn", [S, 64]),
        ("T_gwT", [D, 8]), ("T_wgT", [8, D, 512]), ("T_wuT", [8, D, 512]), ("T_wdT", [8, 512, D]),
        ("T_sgT", [D, 512]), ("T_suT", [D, 512]), ("T_sdT", [512, D]),
        ("T_WsT", [D, 16]), ("T_WiT", [D, 16]), ("T_WoT1", [16, D]), ("T_WgT", [D, D]),
        ("T_AT", [16, 16]), ("T_qsrot", [6, 64]), ("T_ksrot", [6, 64]), ("T_dupA", [64, 128]),
        ("T_shift1", [128, 128]), ("T_e127", [128, 128]),
        ("T_W1p", [6, D, 5632]), ("T_W2p", [6, 2816, D]), ("T_b1c", [6, 128, 44]),
    ]
    for nm, shp in WLIST:
        d[nm] = nc.dram_tensor(nm, shp, F32, kind="ExternalInput").ap()
    d["out"] = nc.dram_tensor("out", [S, D], F32, kind="ExternalOutput").ap()
    d["dbg"] = nc.dram_tensor("dbg", [S, D], F32, kind="ExternalOutput").ap()
    d["R"] = nc.dram_tensor("Rres", [S, D], F32, kind="Internal").ap()
    d["X2"] = nc.dram_tensor("X2res", [S, D], F32, kind="Internal").ap()
    cx.dram = d
    cx.g = None

    cx.PS0 = nc.alloc_psum_tensor("PS0", [128, 512], F32)
    cx.PS1 = nc.alloc_psum_tensor("PS1", [128, 512], F32)
    cx.PS2 = nc.alloc_psum_tensor("PS2", [128, 512], F32)
    cx.PS3 = nc.alloc_psum_tensor("PS3", [128, 512], F32)
    cx.PS4 = nc.alloc_psum_tensor("PS4", [128, 512], F32)
    cx.PS5 = nc.alloc_psum_tensor("PS5", [128, 512], F32)
    cx.PSB = nc.alloc_psum_tensor("PSB", [128, 1024], BF16)
    cx.PSR = nc.alloc_psum_tensor("PSR", [128, 512], F32)

    cx.eye_b = nc.alloc_sbuf_tensor("eye_b", [128, 128], BF16)
    cx.eye_f = nc.alloc_sbuf_tensor("eye_f", [128, 128], F32)
    cx.ones_b = nc.alloc_sbuf_tensor("ones_b", [1, 128], BF16)
    cx.ones_f = nc.alloc_sbuf_tensor("ones_f", [1, 128], F32)
    cx.c_epsln = nc.alloc_sbuf_tensor("c_epsln", [128, 1], F32)

    p.op("gpsimd", "dma_start", cx.eye_b[:], d["T_eye"], inc="WT", incn=16)
    ld0 = p.op("sync", "dma_start", cx.eye_f[:], d["T_eye"], inc="LD", incn=16)
    p.op("vector", "memset", cx.ones_b[:], 1.0)
    p.op("vector", "memset", cx.ones_f[:], 1.0)
    p.op("vector", "memset", cx.c_epsln[:], EPS_LN)
    p.wait("gpsimd", [("WT", p.val("WT"))])
    p.wait("sync", [("LD", ld0)])
    p.barrier()

    rem = int(nc.sbuf_bytes_remaining)
    top = int(nc.sbuf_top)
    base = (top - rem + 256 + 31) // 32 * 32
    cx.ar = Arena(nc, base, top - 2048 - base)

    cur = d["x"]
    for st in stages:
        if st == "l0a":
            emit_l0a(cx, cur, d["R"])
            cur = d["R"]
        elif st == "l0f":
            emit_l0f(cx, cur, d["R"])
            cur = d["R"]
        elif st == "l1a":
            emit_l1a(cx, cur, d["R"])
            cur = d["R"]
        elif st == "l1f":
            emit_l1f_init(cx, cur, d["X2"])
            for lay in range(L1F_LAYERS):
                if "a" in L1F_SUB or lay < L1F_LAYERS - 1:
                    emit_l1f_attn(cx, lay, d["X2"])
                if "f" in L1F_SUB or lay < L1F_LAYERS - 1:
                    emit_l1f_ffn(cx, lay, d["X2"])
            emit_fin(cx, cur, d["X2"], d["out"])
            cur = None
        else:
            raise ValueError(st)

    if cur is not None:
        emit_copyout(cx, cur, d["out"])

    p.wait("sync", [("ST", p.val("ST"))])
    p.barrier()
    p.build()
    return nc


def emit_copyout(cx, src, dst):
    p = cx.p
    ar = cx.ar
    buf = [ar.tile(f"cpo{i}", [128, 4096], F32) for i in range(2)]
    stv = {}
    for t4 in range(NT // 4):
        w = [("ST", p.val("ST"))] if t4 == 0 else []
        if t4 - 2 in stv:
            w.append(("ST", stv[t4 - 2]))
        ld = p.op("sync", "dma_start", buf[t4 % 2][:].rearrange("p (a n) -> p a n", a=4),
                  src[t4 * 512:(t4 + 1) * 512, :].rearrange("(a p) n -> p a n", p=128),
                  wait=w, inc="LD", incn=16)
        stv[t4] = p.op("sync", "dma_start",
                       dst[t4 * 512:(t4 + 1) * 512, :].rearrange("(a p) n -> p a n", p=128),
                       buf[t4 % 2][:].rearrange("p (a n) -> p a n", a=4),
                       wait=[("LD", ld)], inc="ST", incn=16)
    p.barrier()


STAGES_ALL = ["l0a", "l0f", "l1a", "l1f"]


def kernel(**inputs):
    consts = host_consts(inputs)
    nc = build_nc(STAGES_ALL)
    x = inputs["L0_attn_x"]
    base = {k: np.ascontiguousarray(v, dtype=np.float32) for k, v in inputs.items() if k != "L0_attn_x"}
    base.update({k: np.ascontiguousarray(v, dtype=np.float32) for k, v in consts.items()})
    in_maps = []
    for c in range(CORES):
        m = dict(base)
        m["x"] = np.ascontiguousarray(x[c], dtype=np.float32)
        in_maps.append(m)
    res = run_bass_kernel_spmd(nc, in_maps, core_ids=list(range(CORES)))
    out = np.stack([res.results[c]["out"] for c in range(CORES)], axis=0)
    return out


# version 05
# speedup vs baseline: 1.0967x; 1.0097x over previous
import math
import numpy as np
import concourse.bass as bass
import concourse.mybir as mybir
from concourse.bass_utils import run_bass_kernel_spmd

F32 = mybir.dt.float32
BF16 = mybir.dt.bfloat16
AF = mybir.ActivationFunctionType
ALU = mybir.AluOpType
AX = mybir.AxisListType

S, D = 4096, 1024
NT = S // 128
NCHUNK = S // 256
CORES = 8
EPS_LN = 1e-5
EPS_LA = 1e-6

ENGINES = ("tensor", "vector", "scalar", "gpsimd", "sync")
DBG = None
L1F_LAYERS = 6
L1F_SUB = "af"
DBG_KF = 0
DBG_NKF = 0
DBG_SERIAL = 0


def dbg_tap(cx, name, ap, rows, wait=()):
    if DBG != name:
        return
    p = cx.p
    v = p.op("gpsimd", "dma_start", cx.dram["dbg"][rows[0]:rows[1], :ap.free_size()],
             ap, wait=wait, inc="WT", incn=16)
    p.wait("vector", [("WT", v)])
    p.wait("scalar", [("WT", v)])
    p.wait("tensor", [("WT", v)])


def host_consts(ins):
    c = {}
    f32 = np.float32
    c["T_eye"] = np.eye(128, dtype=f32)
    tk = np.arange(256)
    c["T_mask0"] = (tk[:, None] <= tk[None, :]).astype(f32)
    i = np.arange(128)[:, None]
    jj = np.arange(384)[None, :]
    c["T_mask1"] = np.where((jj >= i) & (jj <= i + 256), 0.0, -1e30).astype(f32)
    half = np.arange(0, 64, 2, dtype=f32)
    inv_freq = (1.0 / 10000.0 ** (half / 64)).astype(f32)
    t = np.arange(S, dtype=f32)
    fr = t[:, None] * inv_freq[None, :]
    fr = np.concatenate([fr, fr], -1).astype(f32)
    base = ((half + 0.4 * 64) / (1.4 * 64)).astype(f32)
    power = (t - S // 2) / 128.0
    sc = (base[None, :] ** power[:, None]).astype(f32)
    sc = np.concatenate([sc, sc], -1).astype(f32)
    cos, sin = np.cos(fr).astype(f32), np.sin(fr).astype(f32)
    sign = np.concatenate([-np.ones(32, f32), np.ones(32, f32)])
    c["T_CS"] = cos * sc
    c["T_SSgn"] = sin * sc * sign
    c["T_CI"] = (cos / sc).astype(f32)
    c["T_SIgn"] = (sin / sc * sign).astype(f32)
    c["T_gwT"] = np.ascontiguousarray(ins["L0_ffn_gate_w"].T)
    c["T_wgT"] = np.ascontiguousarray(ins["L0_ffn_wg"].transpose(0, 2, 1))
    c["T_wuT"] = np.ascontiguousarray(ins["L0_ffn_wu"].transpose(0, 2, 1))
    c["T_wdT"] = np.ascontiguousarray(ins["L0_ffn_wd"].transpose(0, 2, 1))
    c["T_sgT"] = np.ascontiguousarray(ins["L0_ffn_sg"].T)
    c["T_suT"] = np.ascontiguousarray(ins["L0_ffn_su"].T)
    c["T_sdT"] = np.ascontiguousarray(ins["L0_ffn_sd"].T)
    c["T_WsT"] = np.ascontiguousarray(ins["L1_attn_Ws"].T)
    c["T_WiT"] = np.ascontiguousarray(ins["L1_attn_Wi"].T)
    c["T_WoT1"] = np.ascontiguousarray(ins["L1_attn_Wo"].T)
    c["T_WgT"] = np.ascontiguousarray(ins["L1_attn_Wg"].T)
    c["T_AT"] = np.ascontiguousarray(ins["L1_attn_A"].T)
    qs = ins["L1_ffn_q_scale"]
    ks = ins["L1_ffn_k_scale"]
    c["T_qsrot"] = np.ascontiguousarray(np.concatenate([qs[:, 32:], qs[:, :32]], 1))
    c["T_ksrot"] = np.ascontiguousarray(np.concatenate([ks[:, 32:], ks[:, :32]], 1))
    dup = np.zeros((64, 128), f32)
    dup[np.arange(64), np.arange(64)] = 1.0
    dup[np.arange(64), 64 + np.arange(64)] = 1.0
    c["T_dupA"] = dup
    c["T_shift1"] = np.eye(128, k=1).astype(f32)
    e127 = np.zeros((128, 128), f32)
    e127[127, 0] = 1.0
    c["T_e127"] = e127
    W1 = ins["L1_ffn_W1"]
    W2 = ins["L1_ffn_W2"]
    b1 = ins["L1_ffn_b1"]
    W1p = np.zeros((6, 1024, 5632), f32)
    W1p[:, :, 0:2730] = W1[:, :, 0:2730]
    W1p[:, :, 2816:2816 + 2730] = W1[:, :, 2730:5460]
    W2p = np.zeros((6, 2816, 1024), f32)
    W2p[:, 0:2730, :] = W2
    b1p = np.zeros((6, 5632), f32)
    b1p[:, 0:2730] = b1[:, 0:2730]
    b1p[:, 2816:2816 + 2730] = b1[:, 2730:5460]
    c["T_W1p"] = W1p
    c["T_W2p"] = W2p
    c["T_b1c"] = np.ascontiguousarray(b1p.reshape(6, 44, 128).transpose(0, 2, 1))
    return c


class P:
    def __init__(self, nc):
        self.nc = nc
        self.q = {e: [] for e in ENGINES}
        self.blocks = []
        self.sems = {}
        self.cnt = {}
        self.lastw = {e: {} for e in ENGINES}
        self.names = set()

    def sem(self, name):
        if name not in self.sems:
            self.sems[name] = self.nc.alloc_semaphore("s_" + name)
            self.cnt[name] = 0
        return name

    def val(self, name):
        return self.cnt[name]

    def op(self, eng, meth, *a, wait=(), inc=None, incn=1, **kw):
        ret = None
        if inc is not None:
            self.sem(inc)
            self.cnt[inc] += incn
            ret = self.cnt[inc]
        for s, _ in wait:
            self.sem(s)
        sems = self.sems
        lw = self.lastw[eng]
        rw = []
        for s, v in wait:
            if lw.get(s, -1) < v:
                lw[s] = v
                rw.append((s, v))
        def th(e, a=a, kw=kw, rw=rw, inc=inc, incn=incn, meth=meth):
            for nm, v in rw:
                e.wait_ge(sems[nm], v)
            ins = getattr(e, meth)(*a, **kw)
            if inc is not None:
                ins.then_inc(sems[inc], incn)
        self.q[eng].append(th)
        return ret

    def wait(self, eng, wait):
        self.op(eng, "nop", wait=wait)

    def raw(self, eng, fn):
        self.q[eng].append(fn)

    def barrier(self):
        if any(self.q[e] for e in ENGINES):
            self.blocks.append(self.q)
        self.q = {e: [] for e in ENGINES}
        for e in ENGINES:
            pass

    def build(self):
        self.barrier()
        for qd in self.blocks:
            with self.nc.Block() as blk:
                for eng in ENGINES:
                    if qd[eng]:
                        def runner(e, lst=qd[eng]):
                            for f in lst:
                                f(e)
                        getattr(blk, eng)(runner)


class Arena:
    def __init__(self, nc, base, size):
        self.nc = nc
        self.base = base
        self.size = size
        self.cur = base
        self.uid = 0

    def reset(self):
        self.cur = self.base

    def tile(self, name, shape, dtype):
        per = int(np.prod(shape[1:])) * mybir.dt.size(dtype)
        per = (per + 31) // 32 * 32
        off = self.cur
        self.cur += per
        assert self.cur <= self.base + self.size, \
            f"arena overflow at {name}: {self.cur-self.base} > {self.size}"
        self.uid += 1
        return self.nc.alloc_sbuf_tensor_at(f"ar_{name}_{self.uid}", shape, dtype, offset=off)


class Ctx:
    pass


def stage_fence(cx):
    p = cx.p
    w = [(s, p.val(s)) for s in ("ST", "LD", "WT")]
    for e in ENGINES:
        p.wait(e, list(w))
    p.barrier()


def ln_stats2(cx, xt_ap, sq_buf, mu, rstd, nmr, tmp, eps=EPS_LN, wait_v=(), wait_a=()):
    p = cx.p
    N = xt_ap.shape[-1]
    inv = 1.0 / N
    v1 = p.op("vector", "reduce_sum", mu, xt_ap, axis=AX.X, wait=wait_v, inc="V")
    a_sq = p.op("scalar", "activation", sq_buf, xt_ap, AF.Square, wait=wait_a, inc="A")
    v2 = p.op("vector", "reduce_sum", tmp, sq_buf, axis=AX.X, wait=[("A", a_sq)], inc="V")
    v3 = p.op("vector", "tensor_scalar_mul", mu, mu, inv, wait=[("V", v1)], inc="V")
    v4 = p.op("vector", "tensor_scalar_mul", tmp, tmp, inv, wait=[("V", v2)], inc="V")
    v5 = p.op("vector", "scalar_tensor_tensor", rstd, mu, -1.0, mu, ALU.mult, ALU.mult,
              wait=[("V", v3)], inc="V")
    va = p.op("vector", "tensor_tensor", tmp, tmp, rstd, ALU.add,
              wait=[("V", v5), ("V", v4)], inc="V")
    a_sd = p.op("scalar", "activation", tmp, tmp, AF.Sqrt, bias=cx.c_epsln[:],
                wait=[("V", va)], inc="A")
    v6 = p.op("vector", "reciprocal", rstd, tmp, wait=[("A", a_sd)], inc="V")
    vr = p.op("vector", "scalar_tensor_tensor", nmr, mu, -1.0, rstd, ALU.mult, ALU.mult,
              wait=[("V", v6)], inc="V")
    return vr


def emit_l0a(cx, src_dram, dst_dram):
    p, nc, ar = cx.p, cx.nc, cx.ar
    ar.reset()

    Wq = ar.tile("Wq", [128, 8, 1024], BF16)
    Wk = ar.tile("Wk", [128, 8, 256], BF16)
    Wv = ar.tile("Wv", [128, 8, 256], BF16)
    Wc = ar.tile("Wc", [128, 2, 64], BF16)
    We = ar.tile("We", [64, 256], BF16)
    Wo = ar.tile("Wo", [128, 8, 1024], BF16)
    brow = ar.tile("brow", [1, 4096], BF16)
    gamb = ar.tile("gamb", [128, 1024], F32)
    betb = ar.tile("betb", [128, 1024], F32)
    grow = ar.tile("grow", [1, 2048], F32)
    mask0 = ar.tile("mask0", [128, 2, 256], BF16)

    g = cx.g
    d = cx.dram
    p.op("gpsimd", "dma_start", Wq[:], d["L0_attn_Wq"].rearrange("(k p) n -> p k n", p=128), inc="WT", incn=16)
    p.op("gpsimd", "dma_start", Wk[:], d["L0_attn_Wk"].rearrange("(k p) n -> p k n", p=128), inc="WT", incn=16)
    p.op("gpsimd", "dma_start", Wv[:], d["L0_attn_Wv"].rearrange("(k p) n -> p k n", p=128), inc="WT", incn=16)
    p.op("gpsimd", "dma_start", Wc[:], d["L0_attn_Wc"].rearrange("(k p) n -> p k n", p=128), inc="WT", incn=16)
    p.op("gpsimd", "dma_start", We[:], d["L0_attn_We"], inc="WT", incn=16)
    p.op("gpsimd", "dma_start", Wo[:], d["L0_attn_Wo"].rearrange("(k p) n -> p k n", p=128), inc="WT", incn=16)
    p.op("gpsimd", "dma_start", brow[:, 0:1024], d["L0_attn_bq"].unsqueeze(0), inc="WT", incn=16)
    p.op("gpsimd", "dma_start", brow[:, 1024:1280], d["L0_attn_bk"].unsqueeze(0), inc="WT", incn=16)
    p.op("gpsimd", "dma_start", brow[:, 1280:1536], d["L0_attn_bv"].unsqueeze(0), inc="WT", incn=16)
    p.op("gpsimd", "dma_start", brow[:, 1536:2560], d["L0_attn_bo"].unsqueeze(0), inc="WT", incn=16)
    p.op("gpsimd", "dma_start", mask0[:], d["T_mask0"].rearrange("(r p) n -> p r n", p=128), inc="WT", incn=16)
    wt_g = p.op("sync", "dma_start", grow[:, 0:1024], d["L0_attn_gamma"].unsqueeze(0), inc="LD", incn=16)
    wt_g = p.op("sync", "dma_start", grow[:, 1024:2048], d["L0_attn_beta"].unsqueeze(0), inc="LD", incn=16)
    wt_w = p.val("WT")

    PS0, PS1, PS2, PS3, PS4, PS5 = cx.PS0, cx.PS1, cx.PS2, cx.PS3, cx.PS4, cx.PS5
    t0 = p.op("tensor", "matmul", PS0[:, 0:512], cx.ones_f[:, 0:128], grow[:, 0:512],
              wait=[("LD", wt_g)], start=True, stop=True, inc="T")
    p.op("tensor", "matmul", PS1[:, 0:512], cx.ones_f[:, 0:128], grow[:, 512:1024], start=True, stop=True)
    p.op("tensor", "matmul", PS2[:, 0:512], cx.ones_f[:, 0:128], grow[:, 1024:1536], start=True, stop=True)
    t0 = p.op("tensor", "matmul", PS3[:, 0:512], cx.ones_f[:, 0:128], grow[:, 1536:2048], start=True, stop=True, inc="T")
    p.op("vector", "tensor_copy", gamb[:, 0:512], PS0[:, 0:512], wait=[("T", t0)])
    p.op("vector", "tensor_copy", gamb[:, 512:1024], PS1[:, 0:512])
    p.op("vector", "tensor_copy", betb[:, 0:512], PS2[:, 0:512])
    v0 = p.op("vector", "tensor_copy", betb[:, 512:1024], PS3[:, 0:512], inc="V")
    p.barrier()

    xt = [ar.tile(f"xt{i}", [128, 1024], F32) for i in range(2)]
    sqb = ar.tile("sqb", [128, 1024], BF16)
    sqb2 = ar.tile("sqb2", [128, 1024], BF16)
    ebuf = ar.tile("ebuf", [128, 1024], BF16)
    lnb = ar.tile("lnb", [128, 1024], BF16)
    xTt = ar.tile("xTt", [128, 8, 128], BF16)
    stt = ar.tile("stt", [128, NT, 4], F32)
    qf_t = ar.tile("qf_t", [128, 1024], BF16)
    qfp = ar.tile("qfp", [128, 1024], BF16)
    k1t = ar.tile("k1t", [128, 256], BF16)
    v1t = ar.tile("v1t", [128, 256], BF16)
    k1T = ar.tile("k1T", [128, 2, 128], BF16)
    v1T = ar.tile("v1T", [128, 2, 128], BF16)
    k2t = ar.tile("k2t", [128, 64], BF16)
    v2t = ar.tile("v2t", [128, 64], BF16)
    k2T = ar.tile("k2T", [64, 128], BF16)
    v2T = ar.tile("v2T", [64, 128], BF16)
    qfT = ar.tile("qfT", [128, 8, 256], BF16)
    kfT = ar.tile("kfT", [128, 2, 256], BF16)
    kfc = ar.tile("kfc", [128, 2, 256], BF16)
    vag = ar.tile("vag", [128, 2, 4, 65], BF16)
    simb = ar.tile("simb", [128, 2, 2, 256], BF16)
    epsr = ar.tile("epsr", [1, 130], BF16)
    recd = ar.tile("recd", [128, 2, 2], F32)
    otc = ar.tile("otc", [128, 2, 1024], BF16)
    oTt = ar.tile("oTt", [128, 8, 128], BF16)
    dupA = ar.tile("dupA", [64, 128], BF16)
    Sbf0 = ar.tile("Sbf0", [64, 65], BF16)
    Saug = ar.tile("Saug", [64, 4, 65], F32)
    Sbf2 = ar.tile("Sbf2", [128, 2, 65], BF16)
    lnep = ar.tile("lnep", [128, 1024], F32)
    ysb = ar.tile("ysb", [128, 1024], F32)
    Rt = [ar.tile(f"Rt{i}", [128, 1024], F32) for i in range(2)]

    p.op("vector", "memset", Saug[:], 0.0)
    p.op("vector", "memset", Sbf2[:], 0.0)
    p.op("vector", "memset", vag[:], 1.0)
    p.op("vector", "memset", epsr[:], 0.0)
    p.op("vector", "memset", epsr[:, 64:65], EPS_LA)
    p.op("vector", "memset", epsr[:, 129:130], EPS_LA)
    wdup = p.op("gpsimd", "dma_start", dupA[:], d["T_dupA"], inc="WT", incn=16)
    p.wait("gpsimd", [("WT", wdup)])
    p.barrier()

    PSB = cx.PSB
    ones_b = cx.ones_b

    ep_v = {}
    st_v = {}
    t_tr_prev = 0
    t_qkv_prev = 0
    cx.l0_sim_free = [0, 0]
    cx.l0_num_free = [0, 0]
    for ch in range(NCHUNK):
        p.wait("vector", [("T", p.val("T"))])
        prep_v = {}
        for half in range(2):
            tau = 2 * ch + half
            w = [("V", ep_v[tau - 2])] if tau - 2 in ep_v else []
            ld = p.op("sync", "dma_start", xt[half][:], src_dram[tau * 128:(tau + 1) * 128, :],
                      wait=w, inc="LD", incn=16)
            mu = stt[:, tau, 0:1]; rstd = stt[:, tau, 1:2]; nmr = stt[:, tau, 2:3]; tmp = stt[:, tau, 3:4]
            vr = ln_stats2(cx, xt[half][:], sqb[:], mu, rstd, nmr, tmp,
                           wait_v=[("LD", ld)], wait_a=[("LD", ld)])
            a_ln = p.op("vector", "tensor_scalar", lnb[:], xt[half][:], rstd, nmr,
                        ALU.mult, ALU.add, wait=[("V", vr), ("T", t_tr_prev)], inc="V")
            dbg_tap(cx, "ln", lnb[:], (tau * 128, tau * 128 + 128), wait=[("V", a_ln)])
            for c in range(8):
                w = [("V", a_ln)] if c == 0 else []
                tt = p.op("tensor", "transpose", PSB[:, c * 128:(c + 1) * 128],
                          lnb[:, c * 128:(c + 1) * 128], cx.eye_b[:],
                          wait=w, inc="T")
            t_tr_prev = tt
            v_xt = p.op("vector", "tensor_copy", xTt[:].rearrange("p a b -> p (a b)"), PSB[:, 0:1024],
                        wait=[("T", tt), ("T", t_qkv_prev)], inc="V")

            tq0 = p.op("tensor", "matmul", PS0[:, 0:512], ones_b[:, 0:128], brow[:, 0:512],
                       wait=[("V", v_xt), ("WT", wt_w), ("V", p.val("V"))], start=True, stop=False)
            p.op("tensor", "matmul", PS1[:, 0:512], ones_b[:, 0:128], brow[:, 512:1024], start=True, stop=False)
            p.op("tensor", "matmul", PS2[:, 0:256], ones_b[:, 0:128], brow[:, 1024:1280], start=True, stop=False)
            p.op("tensor", "matmul", PS3[:, 0:256], ones_b[:, 0:128], brow[:, 1280:1536], start=True, stop=False)
            for k in range(8):
                st = False
                sp = (k == 7)
                p.op("tensor", "matmul", PS0[:, 0:512], xTt[:, k, :], Wq[:, k, 0:512], start=st, stop=sp)
                p.op("tensor", "matmul", PS1[:, 0:512], xTt[:, k, :], Wq[:, k, 512:1024], start=st, stop=sp)
                p.op("tensor", "matmul", PS2[:, 0:256], xTt[:, k, :], Wk[:, k, :], start=st, stop=sp)
                t_qkv = p.op("tensor", "matmul", PS3[:, 0:256], xTt[:, k, :], Wv[:, k, :],
                             start=st, stop=sp, inc="T")
            for h2 in range(2):
                ps = (PS0 if h2 == 0 else PS1)
                vm = p.op("vector", "tensor_scalar_min", ebuf[:, h2 * 512:(h2 + 1) * 512], ps[:, 0:512], 0.0,
                          wait=[("T", t_qkv)], inc="V")
                ae = p.op("scalar", "activation", ebuf[:, h2 * 512:(h2 + 1) * 512],
                          ebuf[:, h2 * 512:(h2 + 1) * 512], AF.Exp, wait=[("V", vm)], inc="A")
                v_qf = p.op("vector", "scalar_tensor_tensor", qf_t[:, h2 * 512:(h2 + 1) * 512],
                            ps[:, 0:512], 0.0, ebuf[:, h2 * 512:(h2 + 1) * 512], ALU.max, ALU.add,
                            wait=[("A", ae)], inc="V")
            dbg_tap(cx, "qf", qf_t[:], (tau * 128, tau * 128 + 128), wait=[("V", p.val("V"))])
            p.op("vector", "tensor_copy", k1t[:], PS2[:, 0:256], wait=[("T", t_qkv)])
            v_kv1 = p.op("vector", "tensor_copy", v1t[:], PS3[:, 0:256], inc="V")

            tt = None
            for c in range(2):
                w = [("V", v_kv1)] if c == 0 else []
                p.op("tensor", "transpose", PSB[:, c * 128:(c + 1) * 128], k1t[:, c * 128:(c + 1) * 128],
                     cx.eye_b[:], wait=w)
                tt = p.op("tensor", "transpose", PSB[:, 256 + c * 128:256 + (c + 1) * 128],
                          v1t[:, c * 128:(c + 1) * 128], cx.eye_b[:], inc="T")
            p.op("vector", "tensor_copy", k1T[:].rearrange("p a b -> p (a b)"), PSB[:, 0:256], wait=[("T", tt)])
            v_t1 = p.op("vector", "tensor_copy", v1T[:].rearrange("p a b -> p (a b)"), PSB[:, 256:512], inc="V")
            p.op("tensor", "matmul", PS4[:, 0:64], k1T[:, 0, :], Wc[:, 0, :],
                 wait=[("V", v_t1)], start=True, stop=False)
            p.op("tensor", "matmul", PS4[:, 0:64], k1T[:, 1, :], Wc[:, 1, :], start=False, stop=True)
            p.op("tensor", "matmul", PS4[:, 64:128], v1T[:, 0, :], Wc[:, 0, :], start=True, stop=False)
            t_k2 = p.op("tensor", "matmul", PS4[:, 64:128], v1T[:, 1, :], Wc[:, 1, :],
                        start=False, stop=True, inc="T")
            p.op("vector", "tensor_copy", k2t[:], PS4[:, 0:64], wait=[("T", t_k2)])
            v_k2 = p.op("vector", "tensor_copy", v2t[:], PS4[:, 64:128], inc="V")
            p.op("tensor", "transpose", PSB[0:64, 512:640], k2t[:], cx.eye_b[:], wait=[("V", v_k2)])
            t_k2T = p.op("tensor", "transpose", PSB[0:64, 640:768], v2t[:], cx.eye_b[:], inc="T")
            p.op("vector", "tensor_copy", k2T[:], PSB[0:64, 512:640], wait=[("T", t_k2T)])
            v_k2T = p.op("vector", "tensor_copy", v2T[:], PSB[0:64, 640:768], inc="V")
            p.op("tensor", "matmul", PS5[:, 0:256], k2T[:], We[:], wait=[("V", v_k2T)], start=True, stop=True)
            t_k3 = p.op("tensor", "matmul", PS5[:, 256:512], v2T[:], We[:], start=True, stop=True, inc="T")
            vm = p.op("vector", "tensor_scalar_min", ebuf[:, 0:256], PS5[:, 0:256], 0.0,
                      wait=[("T", t_k3)], inc="V")
            ae = p.op("scalar", "activation", ebuf[:, 0:256], ebuf[:, 0:256], AF.Exp,
                      wait=[("V", vm)], inc="A")
            v_kf = p.op("vector", "scalar_tensor_tensor", kfc[:, half, :], PS5[:, 0:256], 0.0,
                        ebuf[:, 0:256], ALU.max, ALU.add, wait=[("A", ae)], inc="V")
            p.op("vector", "tensor_copy",
                 vag[:, half, :, 0:64],
                 PS5[:, 256:512].rearrange("p (a b) -> p a b", a=4))
            v_qp = p.op("vector", "tensor_copy",
                        qfp[:].rearrange("p (g r q f) -> p g r q f", g=2, r=4, q=2),
                        qf_t[:].rearrange("p (g q r f) -> p g q r f", g=2, q=2, r=4)
                        .transpose([0, 1, 3, 2, 4]),
                        wait=[("T", t_qkv_prev), ("V", v_qf)], inc="V")
            tt = None
            for c in range(8):
                w = [("V", v_qp)] if c == 0 else []
                tt = p.op("tensor", "transpose", PSB[:, c * 128:(c + 1) * 128],
                          qfp[:, c * 128:(c + 1) * 128], cx.eye_b[:], wait=w, inc="T")
            v_qT = p.op("vector", "tensor_copy", qfT[:, :, half * 128:(half + 1) * 128],
                        PSB[:, 0:1024].rearrange("p (a b) -> p a b", a=8),
                        wait=[("T", tt)], inc="V")
            for c in range(2):
                tt = p.op("tensor", "transpose", PSB[:, c * 128:(c + 1) * 128],
                          kfc[:, half, c * 128:(c + 1) * 128], cx.eye_b[:],
                          wait=[("V", v_qT), ("V", v_kf)], inc="T")
            v_pr = p.op("vector", "tensor_copy", kfT[:, :, half * 128:(half + 1) * 128],
                        PSB[:, 0:256].rearrange("p (a b) -> p a b", a=2),
                        wait=[("T", tt)], inc="V")
            dbg_tap(cx, "kf", kfc[:, half, :], (tau * 128, tau * 128 + 128), wait=[("V", v_kf)])
            dbg_tap(cx, "vag", vag[:, half, :, :].rearrange("p a b -> p (a b)"),
                    (tau * 128, tau * 128 + 128), wait=[("V", p.val("V"))])
            prep_v[half] = v_pr
            t_qkv_prev = tt

        v_prep = prep_v[1]
        qi = 0
        for kvh in range(4):
            b = (kvh % 2) * 64
            for qh in range(kvh * 4, kvh * 4 + 4):
                par = qi % 2
                SIMB_ = PS0 if par == 0 else PS2
                NUMB_ = PS1 if par == 0 else PS5
                qi += 1
                qsl = qfT[b:b + 64, (qh // 8) * 4 + (qh % 4), :]
                for j in range(2):
                    ksl = kfT[b:b + 64, kvh // 2, j * 128:(j + 1) * 128]
                    w = [("V", v_prep)] if (qh == kvh * 4 and j == 0) else []
                    if j == 0:
                        w.append(("V", cx.l0_sim_free[par] or v_prep))
                    t_sim = p.op("tensor", "matmul", SIMB_[:, j * 256:(j + 1) * 256], ksl, qsl,
                                 wait=w, start=True, stop=True, inc="T")
                v_sim = p.op("vector", "tensor_tensor", simb[:, par], SIMB_[:, 0:512].rearrange("p (a b) -> p a b", a=2),
                             mask0[:], ALU.mult, wait=[("T", t_sim)], inc="V")
                cx.l0_sim_free[par] = v_sim
                t_num = None
                for h2 in range(2):
                    reg = NUMB_[:, h2 * 65:(h2 + 1) * 65]
                    w = [("V", v_sim)] if h2 == 0 else []
                    if h2 == 0 and cx.l0_num_free[par]:
                        w.append(("V", cx.l0_num_free[par]))
                    p.op("tensor", "matmul", reg, simb[:, par, 0, h2 * 128:(h2 + 1) * 128],
                         vag[:, 0, kvh, :], wait=w, start=True, stop=False)
                    p.op("tensor", "matmul", reg, simb[:, par, 1, h2 * 128:(h2 + 1) * 128],
                         vag[:, 1, kvh, :], start=False, stop=False)
                    p.op("tensor", "matmul", reg, qsl[:, h2 * 128:(h2 + 1) * 128],
                         Sbf2[b:b + 64, kvh // 2, :], start=False, stop=False)
                    t_num = p.op("tensor", "matmul", reg, ones_b[:, 0:128],
                                 epsr[:, h2 * 65:(h2 + 1) * 65],
                                 start=False, stop=True, inc="T")
                p.op("vector", "reciprocal", recd[:, par, 0:1], NUMB_[:, 64:65], wait=[("T", t_num)])
                v_rc = p.op("vector", "reciprocal", recd[:, par, 1:2], NUMB_[:, 129:130], inc="V")
                p.op("vector", "tensor_scalar", otc[:, 0, qh * 64:(qh + 1) * 64],
                     NUMB_[:, 0:64], recd[:, par, 0:1], None, ALU.mult, wait=[("V", v_rc)])
                v_o = p.op("vector", "tensor_scalar", otc[:, 1, qh * 64:(qh + 1) * 64],
                           NUMB_[:, 65:129], recd[:, par, 1:2], None, ALU.mult, inc="V")
                cx.l0_num_free[par] = v_o
            p.op("tensor", "matmul", PS4[0:64, 0:65], kfc[:, 0, kvh * 64:(kvh + 1) * 64],
                 vag[:, 0, kvh, :], wait=[("V", p.val("V"))], start=True, stop=False)
            t_st = p.op("tensor", "matmul", PS4[0:64, 0:65], kfc[:, 1, kvh * 64:(kvh + 1) * 64],
                        vag[:, 1, kvh, :], start=False, stop=True, inc="T")
            v_sa = p.op("vector", "tensor_tensor", Saug[:, kvh, :], Saug[:, kvh, :], PS4[0:64, 0:65],
                        ALU.add, wait=[("T", t_st)], inc="V")
            v_s = p.op("vector", "tensor_copy", Sbf0[:], Saug[:, kvh, :], wait=[("V", v_sa)], inc="V")
            t_dp = p.op("tensor", "matmul", PS3[:, 0:65], dupA[:], Sbf0[:],
                        wait=[("V", v_s)], start=True, stop=True, inc="T")
            p.op("vector", "tensor_copy", Sbf2[b:b + 64, kvh // 2, :], PS3[b:b + 64, 0:65],
                 wait=[("T", t_dp)])

        for hh in range(2):
            dbg_tap(cx, "ot", otc[:, hh, :], ((2 * ch + hh) * 128, (2 * ch + hh) * 128 + 128),
                    wait=[("V", p.val("V"))])
        v_scan = p.val("V")
        v_gate = v_scan
        for half in range(2):
            tau = 2 * ch + half
            for c in range(8):
                w = [("V", v_gate)] if c == 0 else []
                tt = p.op("tensor", "transpose", PSB[:, c * 128:(c + 1) * 128],
                          otc[:, half, c * 128:(c + 1) * 128], cx.eye_b[:], wait=w, inc="T")
            v_oT = p.op("vector", "tensor_copy", oTt[:].rearrange("p a b -> p (a b)"), PSB[:, 0:1024],
                        wait=[("T", tt)], inc="V")
            v_gate = v_oT
            t0 = p.op("tensor", "matmul", PS0[:, 0:512], ones_b[:, 0:128], brow[:, 1536:2048],
                      wait=[("V", v_oT)], start=True, stop=False)
            p.op("tensor", "matmul", PS1[:, 0:512], ones_b[:, 0:128], brow[:, 2048:2560],
                 start=True, stop=False)
            for k in range(8):
                sp = (k == 7)
                p.op("tensor", "matmul", PS0[:, 0:512], oTt[:, k, :],
                     Wo[:, k, 0:512], start=False, stop=sp)
                t_y = p.op("tensor", "matmul", PS1[:, 0:512], oTt[:, k, :],
                           Wo[:, k, 512:1024], start=False, stop=sp, inc="T")
            a_ep = p.op("vector", "tensor_scalar", lnep[:], xt[half][:],
                        stt[:, tau, 1:2], stt[:, tau, 2:3], ALU.mult, ALU.add,
                        wait=[("V", v_scan), ("V", ep_v.get(tau - 1, 0))], inc="V")
            p.op("vector", "tensor_tensor", ysb[:, 0:512], PS0[:, 0:512], lnep[:, 0:512], ALU.add,
                 wait=[("T", t_y), ("V", a_ep)])
            v_y = p.op("vector", "tensor_tensor", ysb[:, 512:1024], PS1[:, 0:512], lnep[:, 512:1024],
                       ALU.add, inc="V")
            dbg_tap(cx, "y", ysb[:], (tau * 128, tau * 128 + 128), wait=[("V", v_y)])
            mu = stt[:, tau, 0:1]; rstd = stt[:, tau, 1:2]; nmr = stt[:, tau, 2:3]; tmp = stt[:, tau, 3:4]
            vr = ln_stats2(cx, ysb[:], sqb2[:], mu, rstd, nmr, tmp,
                           wait_v=[("V", v_y)], wait_a=[("V", v_y)])
            a_z = p.op("vector", "tensor_scalar", lnep[:], ysb[:], rstd, nmr,
                       ALU.mult, ALU.add, wait=[("V", vr)], inc="V")
            a_z = p.op("vector", "tensor_tensor", lnep[:], lnep[:], gamb[:], ALU.mult,
                       wait=[("V", a_z)], inc="V")
            a_z = p.op("vector", "tensor_tensor", lnep[:], lnep[:], betb[:], ALU.add,
                       wait=[("V", a_z)], inc="V")
            wst = [("ST", st_v[tau - 2])] if tau - 2 in st_v else []
            v_ep = p.op("vector", "tensor_tensor", Rt[half][:], lnep[:], xt[half][:], ALU.add,
                        wait=wst + [("V", a_z)], inc="V")
            ep_v[tau] = v_ep
            st_v[tau] = p.op("sync", "dma_start", dst_dram[tau * 128:(tau + 1) * 128, :], Rt[half][:],
                             wait=[("V", v_ep)], inc="ST", incn=16)
    p.barrier()


def emit_l0f(cx, src_dram, dst_dram):
    p, nc, ar = cx.p, cx.nc, cx.ar
    stage_fence(cx)
    ar.reset()
    d = cx.dram
    PS = [cx.PS0, cx.PS1, cx.PS2, cx.PS3, cx.PS4, cx.PS5]
    PSR, PSB = cx.PSR, cx.PSB
    ones_b = cx.ones_b

    gwT = ar.tile("gwT", [128, 8, 8], BF16)
    sgT = ar.tile("sgT", [128, 8, 512], BF16)
    suT = ar.tile("suT", [128, 8, 512], BF16)
    sdT = ar.tile("sdT", [128, 4, 1024], BF16)
    wg_b = [ar.tile(f"wg{i}", [128, 8, 512], BF16) for i in range(2)]
    wu_b = [ar.tile(f"wu{i}", [128, 8, 512], BF16) for i in range(2)]
    wd_b = [ar.tile(f"wd{i}", [128, 4, 1024], BF16) for i in range(2)]

    p.op("gpsimd", "dma_start", gwT[:], d["T_gwT"].rearrange("(k p) n -> p k n", p=128), inc="WT", incn=16)
    p.op("gpsimd", "dma_start", sgT[:], d["T_sgT"].rearrange("(k p) n -> p k n", p=128), inc="WT", incn=16)
    p.op("gpsimd", "dma_start", suT[:], d["T_suT"].rearrange("(k p) n -> p k n", p=128), inc="WT", incn=16)
    wt_sh = p.op("gpsimd", "dma_start", sdT[:], d["T_sdT"].rearrange("(k p) n -> p k n", p=128), inc="WT", incn=16)

    def load_expert(e, buf):
        p.op("gpsimd", "dma_start", wg_b[buf][:], d["T_wgT"][e].rearrange("(k p) n -> p k n", p=128), inc="WT", incn=16)
        p.op("gpsimd", "dma_start", wu_b[buf][:], d["T_wuT"][e].rearrange("(k p) n -> p k n", p=128), inc="WT", incn=16)
        return p.op("gpsimd", "dma_start", wd_b[buf][:], d["T_wdT"][e].rearrange("(k p) n -> p k n", p=128), inc="WT", incn=16)

    xtg = ar.tile("xtg", [128, 3, 1024], F32)
    sqb = ar.tile("sqb", [128, 1024], BF16)
    lnb = ar.tile("lnb", [128, 1024], BF16)
    xTg = ar.tile("xTg", [128, 3, 8, 128], BF16)
    stt = ar.tile("stt", [128, 3, 4], F32)
    smf = ar.tile("smf", [128, 3, 8], F32)
    cw = ar.tile("cw", [128, 3, 8], F32)
    rnk = ar.tile("rnk", [128, 8], F32)
    t8 = ar.tile("t8", [128, 8], F32)
    den8 = ar.tile("den8", [128, 2], F32)
    sgl = ar.tile("sgl", [128, 512], BF16)
    hx = ar.tile("hx", [128, 512], BF16)
    hT = ar.tile("hT", [128, 4, 128], BF16)
    Rt = [ar.tile(f"Rt{i}", [128, 1024], F32) for i in range(2)]

    groups = [list(range(g, min(g + 3, NT))) for g in range(0, NT, 3)]
    wt_e = {}
    t_wuse = {}
    v_ep_prev = 0
    st_v = {}
    nst = 0

    for gi, tiles in enumerate(groups):
        G = len(tiles)
        for e in range(2):
            if gi == 0 and e < 2:
                pass
        a_ln_prev = p.val("T")
        v_ln = {}
        for ti, tau in enumerate(tiles):
            ld = p.op("sync", "dma_start", xtg[:, ti, :], src_dram[tau * 128:(tau + 1) * 128, :],
                      wait=[("V", v_ep_prev)], inc="LD", incn=16)
            mu = stt[:, ti, 0:1]; rstd = stt[:, ti, 1:2]; nmr = stt[:, ti, 2:3]; tmp = stt[:, ti, 3:4]
            vr = ln_stats2(cx, xtg[:, ti, :], sqb[:], mu, rstd, nmr, tmp,
                           wait_v=[("LD", ld)], wait_a=[("LD", ld)])
            a_ln = p.op("vector", "tensor_scalar", lnb[:], xtg[:, ti, :], rstd, nmr,
                        ALU.mult, ALU.add, wait=[("V", vr), ("T", p.val("T"))], inc="V")
            for c in range(8):
                w = [("V", a_ln)] if c == 0 else []
                tt = p.op("tensor", "transpose", PSB[:, c * 128:(c + 1) * 128],
                          lnb[:, c * 128:(c + 1) * 128], cx.eye_b[:], wait=w, inc="T")
            v_xt = p.op("vector", "tensor_copy", xTg[:, ti, :, :].rearrange("p a b -> p (a b)"),
                        PSB[:, 0:1024], wait=[("T", tt)], inc="V")
            for k in range(8):
                w = [("V", v_xt), ("WT", wt_sh)] if k == 0 else []
                t_s = p.op("tensor", "matmul", PS[0][:, ti * 8:(ti + 1) * 8], xTg[:, ti, k, :],
                           gwT[:, k, :], wait=w, start=(k == 0), stop=(k == 7), inc="T")
            v1 = p.op("vector", "reduce_max", den8[:, 0:1], PS[0][:, ti * 8:(ti + 1) * 8],
                      axis=AX.X, negate=True, wait=[("T", t_s)], inc="V")
            a_e = p.op("scalar", "activation", smf[:, ti, :], PS[0][:, ti * 8:(ti + 1) * 8],
                       AF.Exp, bias=den8[:, 0:1], accum_out=den8[:, 1:2],
                       wait=[("V", v1)], inc="A")
            v2 = p.op("vector", "reciprocal", den8[:, 0:1], den8[:, 1:2], wait=[("A", a_e)], inc="V")
            v3 = p.op("vector", "tensor_scalar", smf[:, ti, :], smf[:, ti, :], den8[:, 0:1], None,
                      ALU.mult, wait=[("V", v2)], inc="V")
            vm = p.op("vector", "memset", rnk[:], 0.0, inc="V")
            for e in range(8):
                vc = p.op("vector", "tensor_scalar", t8[:], smf[:, ti, :], smf[:, ti, e:e + 1], None,
                          ALU.is_lt, wait=[("V", v3 if e == 0 else vm)], inc="V")
                vm = p.op("vector", "tensor_tensor", rnk[:], rnk[:], t8[:], ALU.add,
                          wait=[("V", vc)], inc="V")
            vs = p.op("vector", "tensor_scalar", t8[:], rnk[:], 1.5, None, ALU.is_lt,
                      wait=[("V", vm)], inc="V")
            v_cw = p.op("vector", "tensor_tensor", cw[:, ti, :], smf[:, ti, :], t8[:], ALU.mult,
                        wait=[("V", vs)], inc="V")
            v_ln[ti] = v_cw

        def gu_h(ti, wgt, wut, cw_ap, wait_first):
            for k in range(8):
                w = list(wait_first) if k == 0 else []
                t_g = p.op("tensor", "matmul", PSR[:, 0:512], xTg[:, ti, k, :], wgt[:, k, :],
                           wait=w, start=(k == 0), stop=(k == 7), inc="T" if k == 7 else None)
            a_s = p.op("scalar", "activation", sgl[:], PSR[:, 0:512], AF.Silu,
                       wait=[("T", t_g)], inc="A")
            for k in range(8):
                w = [("A", a_s)] if k == 0 else []
                t_u = p.op("tensor", "matmul", PSR[:, 0:512], xTg[:, ti, k, :], wut[:, k, :],
                           wait=w, start=(k == 0), stop=(k == 7), inc="T" if k == 7 else None)
            if cw_ap is None:
                v_h = p.op("vector", "tensor_tensor", hx[:], PSR[:, 0:512], sgl[:], ALU.mult,
                           wait=[("T", t_u)], inc="V")
            else:
                v_h = p.op("vector", "scalar_tensor_tensor", hx[:], PSR[:, 0:512], cw_ap, sgl[:],
                           ALU.mult, ALU.mult, wait=[("T", t_u)], inc="V")
            return v_h

        def h_down(ti, wdt, v_h, first, last):
            for c in range(4):
                w = [("V", v_h)] if c == 0 else []
                tt = p.op("tensor", "transpose", PSB[:, c * 128:(c + 1) * 128],
                          hx[:, c * 128:(c + 1) * 128], cx.eye_b[:], wait=w, inc="T")
            v_hT = p.op("vector", "tensor_copy", hT[:].rearrange("p a b -> p (a b)"),
                        PSB[:, 0:512], wait=[("T", tt)], inc="V")
            t_y = None
            for c in range(4):
                w = [("V", v_hT)] if c == 0 else []
                sp = last and c == 3
                p.op("tensor", "matmul", PS[2 * ti][:, 0:512], hT[:, c, :], wdt[:, c, 0:512],
                     wait=w, start=(first and c == 0), stop=sp)
                t_y = p.op("tensor", "matmul", PS[2 * ti + 1][:, 0:512], hT[:, c, :], wdt[:, c, 512:1024],
                           start=(first and c == 0), stop=sp, inc="T")
            return t_y

        t_y_tile = {}
        for ti in range(G):
            v_h = gu_h(ti, sgT, suT, None, [("V", v_ln[ti]), ("V", p.val("V"))])
            t_y_tile[ti] = h_down(ti, sdT, v_h, True, False)
        for e in range(8):
            buf = e % 2
            if gi == 0 and e < 2:
                wt_e[buf] = load_expert(e, buf)
            for ti in range(G):
                cw_ap = cw[:, ti, e:e + 1]
                wf = [("WT", wt_e[buf]), ("V", p.val("V"))]
                v_h = gu_h(ti, wg_b[buf], wu_b[buf], cw_ap, wf)
                t_y_tile[ti] = h_down(ti, wd_b[buf], v_h, False, e == 7)
            nxt = e + 2 if e < 6 else ((e - 6) if gi + 1 < len(groups) else None)
            if nxt is not None:
                p.wait("gpsimd", [("T", p.val("T"))])
                wt_e[buf] = load_expert(nxt, buf)
        for ti, tau in enumerate(tiles):
            wst = [("ST", st_v[nst - 2])] if nst - 2 in st_v else []
            p.op("vector", "tensor_tensor", Rt[nst % 2][:, 0:512], PS[2 * ti][:, 0:512],
                 xtg[:, ti, 0:512], ALU.add, wait=wst + [("T", t_y_tile[ti])])
            v_ep = p.op("vector", "tensor_tensor", Rt[nst % 2][:, 512:1024], PS[2 * ti + 1][:, 0:512],
                        xtg[:, ti, 512:1024], ALU.add, inc="V")
            st_v[nst] = p.op("sync", "dma_start", dst_dram[tau * 128:(tau + 1) * 128, :],
                             Rt[nst % 2][:], wait=[("V", v_ep)], inc="ST", incn=16)
            nst += 1
            v_ep_prev = v_ep
    p.barrier()


def emit_l1a(cx, src_dram, dst_dram):
    p, nc, ar = cx.p, cx.nc, cx.ar
    stage_fence(cx)
    ar.reset()
    d = cx.dram
    PS0, PS1, PS2, PS3, PS4, PS5 = cx.PS0, cx.PS1, cx.PS2, cx.PS3, cx.PS4, cx.PS5
    PSB = cx.PSB
    ones_b, ones_f = cx.ones_b, cx.ones_f

    WgT = ar.tile("WgT", [128, 8, 1024], BF16)
    Wu = ar.tile("Wu", [128, 8, 16], BF16)
    Wsf = ar.tile("Wsf", [128, 8, 16], F32)
    Wif = ar.tile("Wif", [128, 8, 16], F32)
    WoTf = ar.tile("WoTf", [16, 1024], F32)
    ATf = ar.tile("ATf", [16, 16], F32)
    borow = ar.tile("borow", [1, 1024], F32)
    bgrow = ar.tile("bgrow", [1, 1024], BF16)
    bsrow = ar.tile("bsrow", [1, 16], F32)
    birow = ar.tile("birow", [1, 16], F32)
    burow = ar.tile("burow", [1, 16], BF16)
    lnwb = ar.tile("lnwb", [128, 1024], F32)
    lnbb = ar.tile("lnbb", [128, 1024], F32)
    grow = ar.tile("grow", [1, 2048], F32)
    UT = ar.tile("UT", [16, S], F32)
    H = ar.tile("H", [16, S], F32)
    h0 = ar.tile("h0", [16, 1], F32)

    p.op("gpsimd", "dma_start", WgT[:], d["T_WgT"].rearrange("(k p) n -> p k n", p=128), inc="WT", incn=16)
    wt_g = p.op("gpsimd", "dma_start", bgrow[:], d["L1_attn_bg"].unsqueeze(0), inc="WT", incn=16)
    p.op("sync", "dma_start", Wsf[:], d["T_WsT"].rearrange("(k p) n -> p k n", p=128), inc="LD", incn=16)
    p.op("sync", "dma_start", Wif[:], d["T_WiT"].rearrange("(k p) n -> p k n", p=128), inc="LD", incn=16)
    p.op("sync", "dma_start", WoTf[:], d["T_WoT1"], inc="LD", incn=16)
    p.op("sync", "dma_start", ATf[:], d["T_AT"], inc="LD", incn=16)
    p.op("sync", "dma_start", borow[:], d["L1_attn_bo"].unsqueeze(0), inc="LD", incn=16)
    p.op("sync", "dma_start", bsrow[:], d["L1_attn_bs"].unsqueeze(0), inc="LD", incn=16)
    p.op("sync", "dma_start", birow[:], d["L1_attn_bi"].unsqueeze(0), inc="LD", incn=16)
    p.op("sync", "dma_start", grow[:, 0:1024], d["L1_attn_ln_w"].unsqueeze(0), inc="LD", incn=16)
    ld_w = p.op("sync", "dma_start", grow[:, 1024:2048], d["L1_attn_ln_b"].unsqueeze(0), inc="LD", incn=16)
    v_wu = p.op("vector", "tensor_tensor", Wu[:], Wsf[:], Wif[:], ALU.add,
                wait=[("LD", ld_w)], inc="V")
    v_bu = p.op("vector", "tensor_tensor", burow[:], bsrow[:], birow[:], ALU.add, inc="V")
    p.op("vector", "memset", h0[:], 0.0)
    t0 = p.op("tensor", "matmul", PS0[:, 0:512], ones_f[:, 0:128], grow[:, 0:512],
              wait=[("LD", ld_w), ("WT", wt_g)], start=True, stop=True)
    p.op("tensor", "matmul", PS1[:, 0:512], ones_f[:, 0:128], grow[:, 512:1024], start=True, stop=True)
    p.op("tensor", "matmul", PS2[:, 0:512], ones_f[:, 0:128], grow[:, 1024:1536], start=True, stop=True)
    t0 = p.op("tensor", "matmul", PS3[:, 0:512], ones_f[:, 0:128], grow[:, 1536:2048], start=True, stop=True, inc="T")
    p.op("vector", "tensor_copy", lnwb[:, 0:512], PS0[:, 0:512], wait=[("T", t0)])
    p.op("vector", "tensor_copy", lnwb[:, 512:1024], PS1[:, 0:512])
    p.op("vector", "tensor_copy", lnbb[:, 0:512], PS2[:, 0:512])
    p.op("vector", "tensor_copy", lnbb[:, 512:1024], PS3[:, 0:512], inc="V")
    p.barrier()

    xt = [ar.tile(f"xt{i}", [128, 1024], F32) for i in range(2)]
    sqb = ar.tile("sqb", [128, 1024], BF16)
    ln2 = ar.tile("ln2", [128, 1024], F32)
    xnb = ar.tile("xnb", [128, 1024], BF16)
    xnT = ar.tile("xnT", [128, 8, 128], BF16)
    usb = ar.tile("usb", [128, 16], F32)
    stt = ar.tile("stt", [128, NT, 8], F32)
    g_sb = ar.tile("g_sb", [128, 1024], F32)
    t1f = ar.tile("t1f", [128, 1024], F32)
    Rt = [ar.tile(f"Rt{i}", [128, 1024], F32) for i in range(2)]

    def two_ln(tau, half, xt_ap, first_wait):
        mu = stt[:, tau, 4:5]; tmp = stt[:, tau, 5:6]
        r1 = stt[:, tau, 0:1]; n1 = stt[:, tau, 1:2]
        r2 = stt[:, tau, 2:3]; n2 = stt[:, tau, 3:4]
        vr1 = ln_stats2(cx, xt_ap, sqb[:], mu, r1, n1, tmp, wait_v=first_wait, wait_a=first_wait)
        v_l2 = p.op("vector", "tensor_scalar", ln2[:], xt_ap, r1, n1, ALU.mult, ALU.add,
                    wait=[("V", vr1)], inc="V")
        vr2 = ln_stats2(cx, ln2[:], sqb[:], mu, r2, n2, tmp,
                        wait_v=[("V", v_l2)], wait_a=[("V", v_l2)])
        v_xr = p.op("vector", "tensor_scalar", t1f[:], ln2[:], r2, n2, ALU.mult, ALU.add,
                    wait=[("V", vr2)], inc="V")
        v_xa = p.op("vector", "tensor_tensor", t1f[:], t1f[:], lnwb[:], ALU.mult,
                    wait=[("V", v_xr)], inc="V")
        v_xn = p.op("vector", "tensor_tensor", xnb[:], t1f[:], lnbb[:], ALU.add,
                    wait=[("V", v_xa)], inc="V")
        return v_xn

    def xn_transposes(v_xn):
        for c in range(8):
            w = [("V", v_xn)] if c == 0 else []
            tt = p.op("tensor", "transpose", PSB[:, c * 128:(c + 1) * 128],
                      xnb[:, c * 128:(c + 1) * 128], cx.eye_b[:], wait=w, inc="T")
        return p.op("vector", "tensor_copy", xnT[:].rearrange("p a b -> p (a b)"),
                    PSB[:, 0:1024], wait=[("T", tt)], inc="V")

    ep_v = {}
    for tau in range(NT):
        w = [("V", ep_v[tau - 2])] if tau - 2 in ep_v else [("V", p.val("V"))]
        ld = p.op("sync", "dma_start", xt[tau % 2][:], src_dram[tau * 128:(tau + 1) * 128, :],
                  wait=w, inc="LD", incn=16)
        v_xn = two_ln(tau, tau % 2, xt[tau % 2][:], [("LD", ld), ("T", p.val("T"))])
        v_xT = xn_transposes(v_xn)
        p.op("tensor", "matmul", PS0[:, 0:16], ones_b[:, 0:128], burow[:],
             wait=[("V", v_xT), ("V", v_bu), ("WT", wt_g), ("V", p.val("V"))],
             start=True, stop=False)
        for k in range(8):
            t_u = p.op("tensor", "matmul", PS0[:, 0:16], xnT[:, k, :], Wu[:, k, :],
                       start=False, stop=(k == 7), inc="T" if k == 7 else None)
        v_u = p.op("vector", "tensor_copy", usb[:], PS0[:, 0:16], wait=[("T", t_u)], inc="V")
        t_uT = p.op("tensor", "transpose", PS1[0:16, 0:128], usb[:], cx.eye_f[:],
                    wait=[("V", v_u)], inc="T")
        v_UT = p.op("vector", "tensor_copy", UT[:, tau * 128:(tau + 1) * 128], PS1[0:16, 0:128],
                    wait=[("T", t_uT)], inc="V")
        ep_v[tau] = v_UT
    p.barrier()

    a_prev = p.val("A")
    t_prev = None
    for t in range(S):
        wmm = [("A", p.val("A"))] if t == 0 else [("A", a_prev)]
        psr = (PS2 if t % 2 == 0 else PS3)
        rhs = h0[:] if t == 0 else H[:, t - 1:t]
        t_mm = p.op("tensor", "matmul", psr[0:16, 0:1], ATf[:], rhs, wait=wmm,
                    start=True, stop=True, inc="T")
        a_prev = p.op("scalar", "activation", H[:, t:t + 1], psr[0:16, 0:1], AF.Tanh,
                      bias=UT[:, t:t + 1], wait=[("T", t_mm)], inc="A")
    p.barrier()

    st_v = {}
    ep_v = {}
    for tau in range(NT):
        w = [("V", ep_v[tau - 2])] if tau - 2 in ep_v else [("V", p.val("V"))]
        ld = p.op("sync", "dma_start", xt[tau % 2][:], src_dram[tau * 128:(tau + 1) * 128, :],
                  wait=w, inc="LD", incn=16)
        r1 = stt[:, tau, 0:1]; n1 = stt[:, tau, 1:2]
        r2 = stt[:, tau, 2:3]; n2 = stt[:, tau, 3:4]
        v_l2 = p.op("vector", "tensor_scalar", ln2[:], xt[tau % 2][:], r1, n1, ALU.mult, ALU.add,
                    wait=[("LD", ld), ("T", p.val("T")), ("V", p.val("V"))], inc="V")
        v_xr = p.op("vector", "tensor_scalar", t1f[:], ln2[:], r2, n2, ALU.mult, ALU.add,
                    wait=[("V", v_l2)], inc="V")
        v_xa = p.op("vector", "tensor_tensor", t1f[:], t1f[:], lnwb[:], ALU.mult,
                    wait=[("V", v_xr)], inc="V")
        v_xn = p.op("vector", "tensor_tensor", xnb[:], t1f[:], lnbb[:], ALU.add,
                    wait=[("V", v_xa)], inc="V")
        v_xT = xn_transposes(v_xn)
        p.op("tensor", "matmul", PS4[:, 0:512], ones_b[:, 0:128], bgrow[:, 0:512],
             wait=[("V", v_xT)], start=True, stop=False)
        p.op("tensor", "matmul", PS5[:, 0:512], ones_b[:, 0:128], bgrow[:, 512:1024],
             start=True, stop=False)
        for k in range(8):
            p.op("tensor", "matmul", PS4[:, 0:512], xnT[:, k, :], WgT[:, k, 0:512],
                 start=False, stop=(k == 7))
            t_gt = p.op("tensor", "matmul", PS5[:, 0:512], xnT[:, k, :], WgT[:, k, 512:1024],
                        start=False, stop=(k == 7), inc="T" if k == 7 else None)
        a_g1 = p.op("scalar", "activation", g_sb[:, 0:512], PS4[:, 0:512], AF.Sigmoid,
                    wait=[("T", t_gt), ("V", ep_v.get(tau - 1, 0))], inc="A")
        a_g = p.op("scalar", "activation", g_sb[:, 512:1024], PS5[:, 0:512], AF.Sigmoid, inc="A")
        p.op("tensor", "matmul", PS0[:, 0:512], ones_f[:, 0:128], borow[:, 0:512],
             wait=[("V", p.val("V"))], start=True, stop=False)
        p.op("tensor", "matmul", PS0[:, 0:512], H[:, tau * 128:(tau + 1) * 128], WoTf[:, 0:512],
             start=False, stop=True)
        p.op("tensor", "matmul", PS1[:, 0:512], ones_f[:, 0:128], borow[:, 512:1024],
             start=True, stop=False)
        t_o = p.op("tensor", "matmul", PS1[:, 0:512], H[:, tau * 128:(tau + 1) * 128],
                   WoTf[:, 512:1024], start=False, stop=True, inc="T")
        wst = [("ST", st_v[tau - 2])] if tau - 2 in st_v else []
        p.op("vector", "tensor_tensor", t1f[:, 0:512], PS0[:, 0:512], g_sb[:, 0:512], ALU.mult,
             wait=[("T", t_o), ("A", a_g)] + wst)
        v_t1 = p.op("vector", "tensor_tensor", t1f[:, 512:1024], PS1[:, 0:512], g_sb[:, 512:1024],
                    ALU.mult, inc="V")
        v_t2 = p.op("vector", "tensor_tensor", t1f[:], t1f[:], ln2[:], ALU.add,
                    wait=[("V", v_t1)], inc="V")
        v_ep = p.op("vector", "tensor_tensor", Rt[tau % 2][:], t1f[:], xt[tau % 2][:], ALU.add,
                    wait=[("V", v_t2)], inc="V")
        ep_v[tau] = v_ep
        st_v[tau] = p.op("sync", "dma_start", dst_dram[tau * 128:(tau + 1) * 128, :],
                         Rt[tau % 2][:], wait=[("V", v_ep)], inc="ST", incn=16)
    p.barrier()


def emit_l1f_init(cx, src_dram, dst_dram):
    p, ar = cx.p, cx.ar
    stage_fence(cx)
    ar.reset()
    xt = [ar.tile(f"xt{i}", [128, 1024], F32) for i in range(2)]
    ot = [ar.tile(f"ot{i}", [128, 1024], F32) for i in range(2)]
    sqb = ar.tile("sqb", [128, 1024], BF16)
    stt = ar.tile("stt", [128, 4], F32)
    st_v = {}
    ep_v = {}
    for tau in range(NT):
        w = [("V", ep_v[tau - 2])] if tau - 2 in ep_v else []
        ld = p.op("sync", "dma_start", xt[tau % 2][:], src_dram[tau * 128:(tau + 1) * 128, :],
                  wait=w, inc="LD", incn=16)
        mu = stt[:, 0:1]; rstd = stt[:, 1:2]; nmr = stt[:, 2:3]; tmp = stt[:, 3:4]
        vr = ln_stats2(cx, xt[tau % 2][:], sqb[:], mu, rstd, nmr, tmp,
                       wait_v=[("LD", ld)], wait_a=[("LD", ld)])
        wst = [("ST", st_v[tau - 2])] if tau - 2 in st_v else []
        v_ep = p.op("vector", "tensor_scalar", ot[tau % 2][:], xt[tau % 2][:], rstd, nmr,
                    ALU.mult, ALU.add, wait=[("V", vr)] + wst, inc="V")
        ep_v[tau] = v_ep
        st_v[tau] = p.op("sync", "dma_start", dst_dram[tau * 128:(tau + 1) * 128, :],
                         ot[tau % 2][:], wait=[("V", v_ep)], inc="ST", incn=16)
    p.barrier()


def emit_l1f_attn(cx, lay, x2):
    p, nc, ar, d = cx.p, cx.nc, cx.ar, cx.dram
    stage_fence(cx)
    ar.reset()
    PS0, PS1, PS2, PS3, PS4, PS5 = cx.PS0, cx.PS1, cx.PS2, cx.PS3, cx.PS4, cx.PS5
    PSR, PSB = cx.PSR, cx.PSB
    ones_b = cx.ones_b

    Wqkv = ar.tile("Wqkv", [128, 8, 3072], BF16)
    Wo = ar.tile("Wo", [128, 8, 1024], BF16)
    tabs = ar.tile("tabs", [128, 4, 32, 64], BF16)
    tbl = ar.tile("tbl", [128, 32, 64], BF16)
    srow = ar.tile("srow", [1, 4, 64], BF16)
    sb64 = ar.tile("sb64", [128, 4, 64], BF16)
    mask1 = ar.tile("mask1", [128, 384], F32)
    kTr = ar.tile("kTr", [128, 8, 3, 128], BF16)
    vr_ = ar.tile("vr", [128, 3, 1024], BF16)
    xs = [ar.tile(f"xs{i}", [128, 512], F32) for i in range(2)]
    x2u = [ar.tile(f"x2u{i}", [128, 512], F32) for i in range(3)]
    S1f = ar.tile("S1f", [128, 128], F32)
    E127f = ar.tile("E127f", [128, 128], F32)
    xsb = ar.tile("xsb", [128, 1024], BF16)
    xsT = ar.tile("xsT", [128, 8, 128], BF16)
    q_t = ar.tile("q_t", [128, 1024], F32)
    k_t = ar.tile("k_t", [128, 1024], F32)
    sqq = ar.tile("sqq", [128, 1024], BF16)
    ss16 = ar.tile("ss16", [128, 16], F32)
    l2r = ar.tile("l2r", [128, 16], F32)
    qn = ar.tile("qn", [128, 1024], BF16)
    ta = ar.tile("ta", [128, 1024], BF16)
    tb = ar.tile("tb", [128, 1024], BF16)
    qp = ar.tile("qp", [128, 1024], BF16)
    qT = ar.tile("qT", [128, 8, 128], BF16)
    smf = ar.tile("smf", [128, 2, 384], F32)
    pb = ar.tile("pb", [128, 2, 384], BF16)
    pT = ar.tile("pT", [128, 2, 3, 128], BF16)
    nmd = ar.tile("nmd", [128, 2, 4], F32)
    o_t = ar.tile("o_t", [128, 1024], BF16)
    oT = ar.tile("oT", [128, 8, 128], BF16)
    X2n = [ar.tile(f"X2n{i}", [128, 1024], F32) for i in range(2)]

    p.op("gpsimd", "dma_start", Wqkv[:], d["L1_ffn_Wqkv"][lay].rearrange("(k p) n -> p k n", p=128),
         inc="WT", incn=16)
    p.op("gpsimd", "dma_start", Wo[:], d["L1_ffn_Wo"][lay].rearrange("(k p) n -> p k n", p=128),
         inc="WT", incn=16)
    p.op("gpsimd", "dma_start", mask1[:], d["T_mask1"], inc="WT", incn=16)
    p.op("sync", "dma_start", S1f[:], d["T_shift1"], inc="LD", incn=16)
    ld_sh = p.op("sync", "dma_start", E127f[:], d["T_e127"], inc="LD", incn=16)
    p.op("gpsimd", "dma_start", srow[:, 0, :], d["L1_ffn_q_scale"][lay].unsqueeze(0), inc="WT", incn=16)
    p.op("gpsimd", "dma_start", srow[:, 1, :], d["T_qsrot"][lay].unsqueeze(0), inc="WT", incn=16)
    p.op("gpsimd", "dma_start", srow[:, 2, :], d["L1_ffn_k_scale"][lay].unsqueeze(0), inc="WT", incn=16)
    wt_s = p.op("gpsimd", "dma_start", srow[:, 3, :], d["T_ksrot"][lay].unsqueeze(0), inc="WT", incn=16)
    for j in range(4):
        w = [("WT", wt_s)] if j == 0 else []
        t_b = p.op("tensor", "matmul", PS0[:, j * 64:(j + 1) * 64], ones_b[:, 0:128],
                   srow[:, j, :], wait=w, start=True, stop=True, inc="T")
    v_s = p.op("vector", "tensor_copy", sb64[:].rearrange("p a b -> p (a b)"), PS0[:, 0:256],
               wait=[("T", t_b)], inc="V")
    v_tb = v_s
    for j, src in enumerate(["T_CS", "T_SSgn", "T_CI", "T_SIgn"]):
        wld = p.op("gpsimd", "dma_start", tbl[:], d[src].rearrange("(t p) n -> p t n", p=128),
                   wait=[("V", v_tb)], inc="WT", incn=16)
        v_tb = p.op("vector", "tensor_tensor", tabs[:, j, :, :], tbl[:],
                    sb64[:, j, :].unsqueeze(1).broadcast_to((128, 32, 64)), ALU.mult,
                    wait=[("WT", wld), ("V", v_s)], inc="V")
        dbg_tap(cx, "l1f_tab", tabs[:, j, 0:16, :].rearrange("p a b -> p (a b)"),
                (j * 128, j * 128 + 128), wait=[("V", v_tb)])
    p.barrier()

    st_v = {}
    ld_v = {}
    ep_v = {}
    cx.simb_free = [0, 0]
    cx.smf_free = [0, 0]

    def load_xs(tau):
        w = [("V", ep_v[tau - 2])] if tau - 2 in ep_v else []
        p.op("sync", "dma_start", xs[tau % 2][:], x2[tau * 128:(tau + 1) * 128, 0:512],
             wait=w, inc="LD", incn=16)
        return p.op("sync", "dma_start", x2u[tau % 3][:], x2[tau * 128:(tau + 1) * 128, 512:1024],
                    inc="LD", incn=16)

    def rot_apply(src_f32, tA, tB, out_bf, wsem, v_in):
        a_sq = p.op("scalar", "activation", sqq[:], src_f32[:], AF.Square,
                    wait=[(wsem, v_in)], inc="A")
        v1 = p.op("vector", "reduce_sum", ss16[:],
                  sqq[:].rearrange("p (h f) -> p h f", h=16), axis=AX.X,
                  wait=[("A", a_sq)], inc="V")
        a_sr = p.op("scalar", "activation", ss16[:], ss16[:], AF.Sqrt,
                    wait=[("V", v1)], inc="A")
        v2 = p.op("vector", "tensor_scalar_max", ss16[:], ss16[:], 1e-12,
                  wait=[("A", a_sr)], inc="V")
        v3 = p.op("vector", "reciprocal", l2r[:], ss16[:], wait=[("V", v2)], inc="V")
        v4 = p.op("vector", "tensor_tensor", qn[:].rearrange("p (h f) -> p h f", h=16),
                  src_f32[:].rearrange("p (h f) -> p h f", h=16),
                  l2r[:].unsqueeze(2).broadcast_to((128, 16, 64)), ALU.mult,
                  wait=[("V", v3)], inc="V")
        v5 = p.op("vector", "tensor_tensor", ta[:].rearrange("p (h f) -> p h f", h=16),
                  qn[:].rearrange("p (h f) -> p h f", h=16),
                  tA.unsqueeze(1).broadcast_to((128, 16, 64)), ALU.mult,
                  wait=[("V", v4)], inc="V")
        p.op("vector", "tensor_tensor", tb[:].rearrange("p (h u f) -> p h u f", h=16, u=2)[:, :, 0, :],
             qn[:].rearrange("p (h u f) -> p h u f", h=16, u=2)[:, :, 1, :],
             tB[:, 0:32].unsqueeze(1).broadcast_to((128, 16, 32)), ALU.mult,
             wait=[("V", v4)])
        v6 = p.op("vector", "tensor_tensor", tb[:].rearrange("p (h u f) -> p h u f", h=16, u=2)[:, :, 1, :],
                  qn[:].rearrange("p (h u f) -> p h u f", h=16, u=2)[:, :, 0, :],
                  tB[:, 32:64].unsqueeze(1).broadcast_to((128, 16, 32)), ALU.mult, inc="V")
        v7 = p.op("vector", "tensor_tensor", out_bf[:], ta[:], tb[:], ALU.add,
                  wait=[("V", v6), ("V", v5)], inc="V")
        if DBG in ("l1f_qn", "l1f_ta", "l1f_tb") and out_bf is qp:
            src_map = {"l1f_qn": qn, "l1f_ta": ta, "l1f_tb": tb}
            dbg_tap(cx, DBG, src_map[DBG][:], (dbg_rows[0], dbg_rows[1]), wait=[("V", v7)])
        return v7

    def transp8(src_bf, dst, v_in, extra=()):
        for c in range(8):
            w = [("V", v_in)] + list(extra) if c == 0 else []
            tt = p.op("tensor", "transpose", PSB[:, c * 128:(c + 1) * 128],
                      src_bf[:, c * 128:(c + 1) * 128], cx.eye_b[:], wait=w, inc="T")
        return p.op("vector", "tensor_copy", dst, PSB[:, 0:1024].rearrange("p (a b) -> p a b", a=8),
                    wait=[("T", tt)], inc="V")

    ld_v[0] = load_xs(0)
    for tau in range(NT):
        slot = tau % 3
        ld = ld_v[tau]
        if tau + 1 < NT:
            ld_v[tau + 1] = load_xs(tau + 1)
        w0 = [("LD", ld), ("LD", ld_sh), ("V", p.val("V"))]
        if tau == 0:
            t_sh = p.op("tensor", "matmul", PSR[:, 0:512], S1f[:], x2u[0][:],
                        wait=w0, start=True, stop=True, inc="T")
        else:
            p.op("tensor", "matmul", PSR[:, 0:512], S1f[:], x2u[tau % 3][:],
                 wait=w0, start=True, stop=False)
            t_sh = p.op("tensor", "matmul", PSR[:, 0:512], E127f[:], x2u[(tau - 1) % 3][:],
                        start=False, stop=True, inc="T")
        v_xb = p.op("vector", "tensor_copy", xsb[:, 0:512], xs[tau % 2][:],
                    wait=[("LD", ld), ("T", p.val("T"))], inc="V")
        v_sh = p.op("vector", "tensor_copy", xsb[:, 512:1024], PSR[:, 0:512],
                    wait=[("T", t_sh)], inc="V")
        v_xT = transp8(xsb, xsT[:], v_sh)
        dbg_tap(cx, "l1f_xs", xsb[:], (tau * 128, tau * 128 + 128), wait=[("V", v_sh)])
        dbg_tap(cx, "l1f_xsb", xsb[:], (tau * 128, tau * 128 + 128), wait=[("V", v_xb)])
        dbg_tap(cx, "l1f_xsT", xsT[:].rearrange("p a b -> p (a b)"), (tau * 128, tau * 128 + 128),
                wait=[("V", v_xT)])
        for k in range(8):
            w = [("V", v_xT), ("WT", p.val("WT")), ("V", p.val("V"))] if k == 0 else []
            st = (k == 0)
            sp = (k == 7)
            p.op("tensor", "matmul", PS0[:, 0:512], xsT[:, k, :], Wqkv[:, k, 0:512],
                 wait=w, start=st, stop=sp)
            p.op("tensor", "matmul", PS1[:, 0:512], xsT[:, k, :], Wqkv[:, k, 512:1024], start=st, stop=sp)
            p.op("tensor", "matmul", PS2[:, 0:512], xsT[:, k, :], Wqkv[:, k, 1024:1536], start=st, stop=sp)
            p.op("tensor", "matmul", PS3[:, 0:512], xsT[:, k, :], Wqkv[:, k, 1536:2048], start=st, stop=sp)
            p.op("tensor", "matmul", PS4[:, 0:512], xsT[:, k, :], Wqkv[:, k, 2048:2560], start=st, stop=sp)
            t_qkv = p.op("tensor", "matmul", PS5[:, 0:512], xsT[:, k, :], Wqkv[:, k, 2560:3072],
                         start=st, stop=sp, inc="T" if sp else None)
        a_q1 = p.op("scalar", "copy", q_t[:, 0:512], PS0[:, 0:512], wait=[("T", t_qkv)], inc="A")
        a_q2 = p.op("scalar", "copy", q_t[:, 512:1024], PS1[:, 0:512], inc="A")
        p.op("vector", "tensor_copy", k_t[:, 0:512], PS2[:, 0:512], wait=[("T", t_qkv)])
        v_k = p.op("vector", "tensor_copy", k_t[:, 512:1024], PS3[:, 0:512], inc="V")
        p.op("vector", "tensor_copy", vr_[:, slot, 0:512], PS4[:, 0:512])
        v_v = p.op("vector", "tensor_copy", vr_[:, slot, 512:1024], PS5[:, 0:512], inc="V")
        dbg_rows = (tau * 128, tau * 128 + 128)
        dbg_tap(cx, "l1f_qt", q_t[:], (tau * 128, tau * 128 + 128), wait=[("A", a_q2)])
        v_qp = rot_apply(q_t, tabs[:, 0, tau, :], tabs[:, 1, tau, :], qp, "A", a_q2)
        v_qT = transp8(qp, qT[:], v_qp)
        v_kp = rot_apply(k_t, tabs[:, 2, tau, :], tabs[:, 3, tau, :], qp, "V", v_k)
        v_kT = transp8(qp, kTr[:, :, slot, :], v_kp)
        nc_ = min(tau + 1, 3)
        for qh in range(16):
            par = qh % 2
            SIMB = PS0 if par == 0 else PS2
            PVB = PS1 if par == 0 else PS3
            b = par * 64
            qsl = qT[b:b + 64, qh // 2, :]
            for j in range(nc_):
                ksl = kTr[b:b + 64, qh // 2, (tau - nc_ + 1 + j) % 3, :]
                w = []
                if j == 0:
                    w = [("V", v_kT)]
                    if cx.simb_free[par]:
                        w.append(("V", cx.simb_free[par]))
                t_sim = p.op("tensor", "matmul", SIMB[:, j * 128:(j + 1) * 128], qsl, ksl,
                             wait=w, start=True, stop=True, inc="T" if j == nc_ - 1 else None)
            mw = 384 - nc_ * 128
            v_m = p.op("vector", "scalar_tensor_tensor", smf[:, par, 0:nc_ * 128],
                       SIMB[:, 0:nc_ * 128], 8.0, mask1[:, mw:384], ALU.mult, ALU.add,
                       wait=[("T", t_sim), ("A", cx.smf_free[par])], inc="V")
            cx.simb_free[par] = v_m
            v_nm = p.op("vector", "reduce_max", nmd[:, par, 0:1], smf[:, par, 0:nc_ * 128],
                        axis=AX.X, negate=True, wait=[("V", v_m)], inc="V")
            a_e = p.op("scalar", "activation", pb[:, par, 0:nc_ * 128], smf[:, par, 0:nc_ * 128],
                       AF.Exp, bias=nmd[:, par, 0:1], accum_out=nmd[:, par, 1:2],
                       wait=[("V", v_nm)], inc="A")
            cx.smf_free[par] = a_e
            v_rc = p.op("vector", "reciprocal", nmd[:, par, 2:3], nmd[:, par, 1:2],
                        wait=[("A", a_e)], inc="V")
            for j in range(nc_):
                w = [("V", v_rc)] if j == 0 else []
                tt = p.op("tensor", "transpose", PSB[:, par * 512 + j * 128:par * 512 + (j + 1) * 128],
                          pb[:, par, j * 128:(j + 1) * 128], cx.eye_b[:], wait=w, inc="T")
            v_pT = p.op("vector", "tensor_copy", pT[:, par, 0:nc_, :].rearrange("p a b -> p (a b)"),
                        PSB[:, par * 512:par * 512 + nc_ * 128], wait=[("T", tt)], inc="V")
            for j in range(nc_):
                w = [("V", v_pT)] if j == 0 else []
                t_pv = p.op("tensor", "matmul", PVB[:, 0:64], pT[:, par, j, :],
                            vr_[:, (tau - nc_ + 1 + j) % 3, qh * 64:(qh + 1) * 64],
                            wait=w, start=(j == 0), stop=(j == nc_ - 1),
                            inc="T" if j == nc_ - 1 else None)
            v_o = p.op("vector", "tensor_scalar", o_t[:, qh * 64:(qh + 1) * 64], PVB[:, 0:64],
                       nmd[:, par, 2:3], None, ALU.mult, wait=[("T", t_pv), ("V", v_rc)], inc="V")
        dbg_tap(cx, "l1f_ot", o_t[:], (tau * 128, tau * 128 + 128), wait=[("V", v_o)])
        dbg_tap(cx, "l1f_qp", qT[:].rearrange("p a b -> p (a b)"), (tau * 128, tau * 128 + 128),
                wait=[("V", v_qT)])
        v_oT = transp8(o_t, oT[:], v_o)
        for k in range(8):
            w = [("V", v_oT)] if k == 0 else []
            st = (k == 0)
            sp = (k == 7)
            p.op("tensor", "matmul", PS2[:, 0:512], oT[:, k, :], Wo[:, k, 0:512],
                 wait=w, start=st, stop=sp)
            t_wo = p.op("tensor", "matmul", PS3[:, 0:512], oT[:, k, :], Wo[:, k, 512:1024],
                        start=st, stop=sp, inc="T" if sp else None)
        wst = [("ST", st_v[tau - 2])] if tau - 2 in st_v else []
        p.op("vector", "tensor_tensor", X2n[tau % 2][:, 0:512], PS2[:, 0:512],
             xs[tau % 2][:, 0:512], ALU.add, wait=[("T", t_wo)] + wst)
        v_ep = p.op("vector", "tensor_tensor", X2n[tau % 2][:, 512:1024], PS3[:, 0:512],
                    x2u[tau % 3][:], ALU.add, inc="V")
        ep_v[tau] = v_ep
        st_v[tau] = p.op("sync", "dma_start", x2[tau * 128:(tau + 1) * 128, :], X2n[tau % 2][:],
                         wait=[("V", v_ep)], inc="ST", incn=16)
        if DBG_SERIAL and tau < 3:
            stage_fence(cx)
    p.barrier()


def emit_l1f_ffn(cx, lay, x2):
    p, nc, ar, d = cx.p, cx.nc, cx.ar, cx.dram
    stage_fence(cx)
    ar.reset()
    PS = [cx.PS0, cx.PS1, cx.PS2, cx.PS3, cx.PS4, cx.PS5]
    PSR, PSB = cx.PSR, cx.PSB
    ones_b, ones_f = cx.ones_b, cx.ones_f
    NKF = 22

    W1p = ar.tile("W1p", [128, 8, 5632], BF16)
    W2s = [ar.tile(f"W2s{i}", [128, 1024], BF16) for i in range(3)]
    b1c = ar.tile("b1c", [128, 44], F32)
    b2row = ar.tile("b2row", [1, 1024], BF16)
    grow = ar.tile("grow", [1, 2048], F32)
    lnwb = ar.tile("lnwb", [128, 1024], F32)
    lnbb = ar.tile("lnbb", [128, 1024], F32)
    xsg = [ar.tile(f"xsg{i}", [128, 3, 1024], F32) for i in range(2)]
    x2u = [ar.tile(f"x2u{i}", [128, 3, 512], F32) for i in range(2)]
    S1f = ar.tile("S1f", [128, 128], F32)
    E127f = ar.tile("E127f", [128, 128], F32)
    sqb = ar.tile("sqb", [128, 1024], BF16)
    lnb = ar.tile("lnb", [128, 1024], BF16)
    xnT = ar.tile("xnT", [128, 8, 3, 128], BF16)
    stt = ar.tile("stt", [128, 3, 4], F32)
    a_sb = [ar.tile(f"a_sb{i}", [128, 384], BF16) for i in range(2)]
    gel = [ar.tile(f"gel{i}", [128, 384], BF16) for i in range(2)]
    hk = [ar.tile(f"hk{i}", [128, 384], BF16) for i in range(2)]
    hx_dbg = ar.tile("hx_dbg", [128, 512], F32)
    X2n = ar.tile("X2n", [128, 3, 1024], F32)

    p.op("gpsimd", "dma_start", W1p[:], d["T_W1p"][lay].rearrange("(k p) n -> p k n", p=128),
         inc="WT", incn=16)
    p.op("sync", "dma_start", b1c[:], d["T_b1c"][lay], inc="LD", incn=16)
    p.op("sync", "dma_start", S1f[:], d["T_shift1"], inc="LD", incn=16)
    ld_sh = p.op("sync", "dma_start", E127f[:], d["T_e127"], inc="LD", incn=16)
    wt_b = p.op("gpsimd", "dma_start", b2row[:], d["L1_ffn_b2"][lay].unsqueeze(0), inc="WT", incn=16)
    p.op("sync", "dma_start", grow[:, 0:1024], d["L1_ffn_ln_g"][lay].unsqueeze(0), inc="LD", incn=16)
    ld_g = p.op("sync", "dma_start", grow[:, 1024:2048], d["L1_ffn_ln_b"][lay].unsqueeze(0),
                inc="LD", incn=16)
    t0 = p.op("tensor", "matmul", cx.PS0[:, 0:512], ones_f[:, 0:128], grow[:, 0:512],
              wait=[("LD", ld_g)], start=True, stop=True)
    p.op("tensor", "matmul", cx.PS1[:, 0:512], ones_f[:, 0:128], grow[:, 512:1024], start=True, stop=True)
    p.op("tensor", "matmul", cx.PS2[:, 0:512], ones_f[:, 0:128], grow[:, 1024:1536], start=True, stop=True)
    t0 = p.op("tensor", "matmul", cx.PS3[:, 0:512], ones_f[:, 0:128], grow[:, 1536:2048],
              start=True, stop=True, inc="T")
    p.op("vector", "tensor_copy", lnwb[:, 0:512], cx.PS0[:, 0:512], wait=[("T", t0)])
    p.op("vector", "tensor_copy", lnwb[:, 512:1024], cx.PS1[:, 0:512])
    p.op("vector", "tensor_copy", lnbb[:, 0:512], cx.PS2[:, 0:512])
    p.op("vector", "tensor_copy", lnbb[:, 512:1024], cx.PS3[:, 0:512], inc="V")
    p.barrier()

    groups = [list(range(g, min(g + 3, NT))) for g in range(0, NT, 3)]
    st_v = {}
    nst = 0
    ep_v = 0
    ld_blk = {}
    w2_wt = {}

    def load_block(bi):
        tiles = groups[bi]
        buf = bi % 2
        last = None
        for ti, tau in enumerate(tiles):
            p.op("sync", "dma_start", xsg[buf][:, ti, 0:512],
                 x2[tau * 128:(tau + 1) * 128, 0:512], inc="LD", incn=16)
            last = p.op("sync", "dma_start", x2u[buf][:, ti, :],
                        x2[tau * 128:(tau + 1) * 128, 512:1024], inc="LD", incn=16)
        return last

    ld_blk[0] = load_block(0)
    w2_wt[0] = p.op("gpsimd", "dma_start", W2s[0][:], d["T_W2p"][lay][0:128, :], inc="WT", incn=16)
    v_h_hist = {}
    t_y_hist = {}
    a_gl_prev = 0
    kf_glob = [0]
    for bi, tiles in enumerate(groups):
        G = len(tiles)
        buf = bi % 2
        ld = ld_blk[bi]
        v_shift = 0
        for ti, tau in enumerate(tiles):
            w0 = [("LD", ld), ("LD", ld_sh), ("V", p.val("V")), ("A", p.val("A"))]
            if tau == 0:
                t_sh = p.op("tensor", "matmul", PSR[:, 0:512], S1f[:], x2u[buf][:, 0, :],
                            wait=w0, start=True, stop=True, inc="T")
            else:
                prev = x2u[buf][:, ti - 1, :] if ti > 0 else x2u[1 - buf][:, len(groups[bi - 1]) - 1, :]
                p.op("tensor", "matmul", PSR[:, 0:512], S1f[:], x2u[buf][:, ti, :],
                     wait=w0, start=True, stop=False)
                t_sh = p.op("tensor", "matmul", PSR[:, 0:512], E127f[:], prev,
                            start=False, stop=True, inc="T")
            v_shift = p.op("vector", "tensor_copy", xsg[buf][:, ti, 512:1024], PSR[:, 0:512],
                           wait=[("T", t_sh), ("V", ep_v)], inc="V")
        v_xT = {}
        for ti in range(G):
            mu = stt[:, ti, 0:1]; rstd = stt[:, ti, 1:2]; nmr = stt[:, ti, 2:3]; tmp = stt[:, ti, 3:4]
            vr = ln_stats2(cx, xsg[buf][:, ti, :], sqb[:], mu, rstd, nmr, tmp,
                           wait_v=[("LD", ld), ("V", v_shift)], wait_a=[("LD", ld), ("V", v_shift)])
            v1 = p.op("vector", "tensor_scalar", lnb[:], xsg[buf][:, ti, :], rstd, nmr,
                      ALU.mult, ALU.add, wait=[("V", vr), ("T", p.val("T"))], inc="V")
            v2 = p.op("vector", "tensor_tensor", lnb[:], lnb[:], lnwb[:], ALU.mult,
                      wait=[("V", v1)], inc="V")
            v3 = p.op("vector", "tensor_tensor", lnb[:], lnb[:], lnbb[:], ALU.add,
                      wait=[("V", v2)], inc="V")
            dbg_tap(cx, "l1f_xn", lnb[:], (tiles[ti] * 128, tiles[ti] * 128 + 128),
                    wait=[("V", v3)])
            for c in range(8):
                w = [("V", v3)] if c == 0 else []
                tt = p.op("tensor", "transpose", PSB[:, c * 128:(c + 1) * 128],
                          lnb[:, c * 128:(c + 1) * 128], cx.eye_b[:], wait=w, inc="T")
            v_xT[ti] = p.op("vector", "tensor_copy", xnT[:, :, ti, :],
                            PSB[:, 0:1024].rearrange("p (a b) -> p a b", a=8),
                            wait=[("T", tt)], inc="V")
        if bi + 1 < len(groups):
            p.wait("sync", [("V", v_shift)])
            ld_blk[bi + 1] = load_block(bi + 1)
        NB = G * 128
        xn_rhs = xnT[:].rearrange("p a g b -> p a (g b)")
        t_y = {}
        NKF_RUN = DBG_NKF if DBG_NKF else NKF

        def emit_w2(kf, kg):
            sp = (kf == NKF_RUN - 1)
            hkk = hk[kg % 2]
            v_h = v_h_hist[kg]
            for ti in range(G):
                if kf == 0:
                    p.op("tensor", "matmul", PS[2 * ti][:, 0:512], ones_b[:, 0:128],
                         b2row[:, 0:512],
                         wait=([("V", v_h), ("WT", w2_wt[kg % 3])] if ti == 0 else []),
                         start=True, stop=False)
                    p.op("tensor", "matmul", PS[2 * ti + 1][:, 0:512], ones_b[:, 0:128],
                         b2row[:, 512:1024], start=True, stop=False)
                w = [("V", v_h), ("WT", w2_wt[kg % 3])] if (ti == 0 and kf > 0) else []
                p.op("tensor", "matmul", PS[2 * ti][:, 0:512], hkk[:, ti * 128:(ti + 1) * 128],
                     W2s[kg % 3][:, 0:512], wait=w, start=False, stop=sp)
                t_y[ti] = p.op("tensor", "matmul", PS[2 * ti + 1][:, 0:512],
                               hkk[:, ti * 128:(ti + 1) * 128], W2s[kg % 3][:, 512:1024],
                               start=False, stop=sp, inc="T")
            t_y_hist[kg] = t_y[G - 1]

        for kf in range(NKF_RUN):
            kg0 = kf_glob[0]
            if kf + 1 < NKF_RUN or bi + 1 < len(groups):
                nxt = kf + 1 if kf + 1 < NKF_RUN else 0
                p.wait("gpsimd", [("T", p.val("T"))])
                w2_wt[(kg0 + 1) % 3] = p.op("gpsimd", "dma_start", W2s[(kg0 + 1) % 3][:],
                                            d["T_W2p"][lay][nxt * 128:(nxt + 1) * 128, :],
                                            inc="WT", incn=16)
            for k in range(8):
                w = [("V", v_xT[G - 1]), ("A", a_gl_prev), ("WT", wt_b)] if k == 0 else []
                t_a = p.op("tensor", "matmul", PSR[:, 0:NB], W1p[:, k, kf * 128:(kf + 1) * 128],
                           xn_rhs[:, k, 0:NB], wait=w, start=(k == 0), stop=(k == 7),
                           inc="T" if k == 7 else None)
            v_a = p.op("vector", "tensor_scalar", a_sb[kg0 % 2][:, 0:NB], PSR[:, 0:NB],
                       b1c[:, kf:kf + 1], None, ALU.add, wait=[("T", t_a)], inc="V")
            for k in range(8):
                w = [("V", v_a)] if k == 0 else []
                t_g = p.op("tensor", "matmul", PSR[:, 0:NB],
                           W1p[:, k, 2816 + kf * 128:2816 + (kf + 1) * 128],
                           xn_rhs[:, k, 0:NB], wait=w, start=(k == 0), stop=(k == 7),
                           inc="T" if k == 7 else None)
            kg = kg0
            a_gl = p.op("scalar", "activation", gel[kg % 2][:, 0:NB], PSR[:, 0:NB], AF.Gelu,
                        bias=b1c[:, 22 + kf:23 + kf],
                        wait=[("T", t_g), ("V", v_h_hist.get(kg - 2, 0))], inc="A")
            a_gl_prev = a_gl
            v_h = p.op("vector", "tensor_tensor", hk[kg % 2][:, 0:NB], a_sb[kg % 2][:, 0:NB],
                       gel[kg % 2][:, 0:NB], ALU.mult,
                       wait=[("A", a_gl), ("T", t_y_hist.get(kg - 2, 0))], inc="V")
            v_h_hist[kg] = v_h
            kf_glob[0] += 1
            if kf > 0:
                emit_w2(kf - 1, kg - 1)
        emit_w2(NKF_RUN - 1, kf_glob[0] - 1)
        for ti, tau in enumerate(tiles):
            wst = [("ST", st_v[nst - 3])] if nst - 3 in st_v else []
            p.op("vector", "tensor_tensor", X2n[:, ti, 0:512], PS[2 * ti][:, 0:512],
                 xsg[buf][:, ti, 0:512], ALU.add, wait=wst + [("T", t_y[ti])])
            v_ep = p.op("vector", "tensor_tensor", X2n[:, ti, 512:1024], PS[2 * ti + 1][:, 0:512],
                        x2u[buf][:, ti, :], ALU.add, inc="V")
            st_v[nst] = p.op("sync", "dma_start", x2[tau * 128:(tau + 1) * 128, :],
                             X2n[:, ti, :], wait=[("V", v_ep)], inc="ST", incn=16)
            nst += 1
        ep_v = v_ep
    p.barrier()


def emit_fin(cx, a_dram, b_dram, out_dram):
    p, ar = cx.p, cx.ar
    stage_fence(cx)
    ar.reset()
    at = [ar.tile(f"fa{i}", [128, 1024], F32) for i in range(2)]
    bt = [ar.tile(f"fb{i}", [128, 1024], F32) for i in range(2)]
    ot = [ar.tile(f"fo{i}", [128, 1024], F32) for i in range(2)]
    st_v = {}
    for tau in range(NT):
        w = [("ST", st_v[tau - 2])] if tau - 2 in st_v else []
        p.op("sync", "dma_start", at[tau % 2][:], a_dram[tau * 128:(tau + 1) * 128, :],
             wait=w, inc="LD", incn=16)
        ld = p.op("sync", "dma_start", bt[tau % 2][:], b_dram[tau * 128:(tau + 1) * 128, :],
                  inc="LD", incn=16)
        v_o = p.op("vector", "tensor_tensor", ot[tau % 2][:], at[tau % 2][:], bt[tau % 2][:],
                   ALU.add, wait=[("LD", ld), ("ST", st_v.get(tau - 2, 0))], inc="V")
        st_v[tau] = p.op("sync", "dma_start", out_dram[tau * 128:(tau + 1) * 128, :],
                         ot[tau % 2][:], wait=[("V", v_o)], inc="ST", incn=16)
    p.barrier()


def build_nc(stages):
    nc = bass.Bass("TRN2", target_bir_lowering=False)
    cx = Ctx()
    cx.nc = nc
    cx.sbuf_usable0 = int(nc.sbuf_bytes_remaining)
    cx.p = P(nc)
    p = cx.p
    for s in ("LD", "WT", "ST", "T", "V", "A", "G"):
        p.sem(s)

    d = {}
    d["x"] = nc.dram_tensor("x", [S, D], F32, kind="ExternalInput").ap()
    WLIST = [
        ("L0_attn_Wq", [D, D]), ("L0_attn_bq", [D]), ("L0_attn_Wk", [D, 256]), ("L0_attn_bk", [256]),
        ("L0_attn_Wv", [D, 256]), ("L0_attn_bv", [256]), ("L0_attn_Wc", [256, 64]), ("L0_attn_We", [64, 256]),
        ("L0_attn_Wo", [D, D]), ("L0_attn_bo", [D]), ("L0_attn_gamma", [D]), ("L0_attn_beta", [D]),
        ("L0_ffn_gate_w", [8, D]), ("L0_ffn_wg", [8, 512, D]), ("L0_ffn_wu", [8, 512, D]),
        ("L0_ffn_wd", [8, D, 512]), ("L0_ffn_sg", [512, D]), ("L0_ffn_su", [512, D]), ("L0_ffn_sd", [D, 512]),
        ("L1_attn_Ws", [16, D]), ("L1_attn_bs", [16]), ("L1_attn_Wi", [16, D]), ("L1_attn_bi", [16]),
        ("L1_attn_Wo", [D, 16]), ("L1_attn_bo", [D]), ("L1_attn_Wg", [D, D]), ("L1_attn_bg", [D]),
        ("L1_attn_ln_w", [D]), ("L1_attn_ln_b", [D]), ("L1_attn_A", [16, 16]),
        ("L1_ffn_Wqkv", [6, D, 3072]), ("L1_ffn_q_scale", [6, 64]), ("L1_ffn_k_scale", [6, 64]),
        ("L1_ffn_Wo", [6, D, D]), ("L1_ffn_ln_g", [6, D]), ("L1_ffn_ln_b", [6, D]),
        ("L1_ffn_W1", [6, D, 5460]), ("L1_ffn_b1", [6, 5460]), ("L1_ffn_W2", [6, 2730, D]),
        ("L1_ffn_b2", [6, D]),
        ("T_eye", [128, 128]), ("T_mask0", [256, 256]), ("T_mask1", [128, 384]),
        ("T_CS", [S, 64]), ("T_SSgn", [S, 64]), ("T_CI", [S, 64]), ("T_SIgn", [S, 64]),
        ("T_gwT", [D, 8]), ("T_wgT", [8, D, 512]), ("T_wuT", [8, D, 512]), ("T_wdT", [8, 512, D]),
        ("T_sgT", [D, 512]), ("T_suT", [D, 512]), ("T_sdT", [512, D]),
        ("T_WsT", [D, 16]), ("T_WiT", [D, 16]), ("T_WoT1", [16, D]), ("T_WgT", [D, D]),
        ("T_AT", [16, 16]), ("T_qsrot", [6, 64]), ("T_ksrot", [6, 64]), ("T_dupA", [64, 128]),
        ("T_shift1", [128, 128]), ("T_e127", [128, 128]),
        ("T_W1p", [6, D, 5632]), ("T_W2p", [6, 2816, D]), ("T_b1c", [6, 128, 44]),
    ]
    for nm, shp in WLIST:
        d[nm] = nc.dram_tensor(nm, shp, F32, kind="ExternalInput").ap()
    d["out"] = nc.dram_tensor("out", [S, D], F32, kind="ExternalOutput").ap()
    d["dbg"] = nc.dram_tensor("dbg", [S, D], F32, kind="ExternalOutput").ap()
    d["R"] = nc.dram_tensor("Rres", [S, D], F32, kind="Internal").ap()
    d["X2"] = nc.dram_tensor("X2res", [S, D], F32, kind="Internal").ap()
    cx.dram = d
    cx.g = None

    cx.PS0 = nc.alloc_psum_tensor("PS0", [128, 512], F32)
    cx.PS1 = nc.alloc_psum_tensor("PS1", [128, 512], F32)
    cx.PS2 = nc.alloc_psum_tensor("PS2", [128, 512], F32)
    cx.PS3 = nc.alloc_psum_tensor("PS3", [128, 512], F32)
    cx.PS4 = nc.alloc_psum_tensor("PS4", [128, 512], F32)
    cx.PS5 = nc.alloc_psum_tensor("PS5", [128, 512], F32)
    cx.PSB = nc.alloc_psum_tensor("PSB", [128, 1024], BF16)
    cx.PSR = nc.alloc_psum_tensor("PSR", [128, 512], F32)

    cx.eye_b = nc.alloc_sbuf_tensor("eye_b", [128, 128], BF16)
    cx.eye_f = nc.alloc_sbuf_tensor("eye_f", [128, 128], F32)
    cx.ones_b = nc.alloc_sbuf_tensor("ones_b", [1, 128], BF16)
    cx.ones_f = nc.alloc_sbuf_tensor("ones_f", [1, 128], F32)
    cx.c_epsln = nc.alloc_sbuf_tensor("c_epsln", [128, 1], F32)

    p.op("gpsimd", "dma_start", cx.eye_b[:], d["T_eye"], inc="WT", incn=16)
    ld0 = p.op("sync", "dma_start", cx.eye_f[:], d["T_eye"], inc="LD", incn=16)
    p.op("vector", "memset", cx.ones_b[:], 1.0)
    p.op("vector", "memset", cx.ones_f[:], 1.0)
    p.op("vector", "memset", cx.c_epsln[:], EPS_LN)
    p.wait("gpsimd", [("WT", p.val("WT"))])
    p.wait("sync", [("LD", ld0)])
    p.barrier()

    rem = int(nc.sbuf_bytes_remaining)
    top = int(nc.sbuf_top)
    base = (top - rem + 256 + 31) // 32 * 32
    cx.ar = Arena(nc, base, top - 2048 - base)

    cur = d["x"]
    for st in stages:
        if st == "l0a":
            emit_l0a(cx, cur, d["R"])
            cur = d["R"]
        elif st == "l0f":
            emit_l0f(cx, cur, d["R"])
            cur = d["R"]
        elif st == "l1a":
            emit_l1a(cx, cur, d["R"])
            cur = d["R"]
        elif st == "l1f":
            emit_l1f_init(cx, cur, d["X2"])
            for lay in range(L1F_LAYERS):
                if "a" in L1F_SUB or lay < L1F_LAYERS - 1:
                    emit_l1f_attn(cx, lay, d["X2"])
                if "f" in L1F_SUB or lay < L1F_LAYERS - 1:
                    emit_l1f_ffn(cx, lay, d["X2"])
            emit_fin(cx, cur, d["X2"], d["out"])
            cur = None
        else:
            raise ValueError(st)

    if cur is not None:
        emit_copyout(cx, cur, d["out"])

    p.wait("sync", [("ST", p.val("ST"))])
    p.barrier()
    p.build()
    return nc


def emit_copyout(cx, src, dst):
    p = cx.p
    ar = cx.ar
    buf = [ar.tile(f"cpo{i}", [128, 4096], F32) for i in range(2)]
    stv = {}
    for t4 in range(NT // 4):
        w = [("ST", p.val("ST"))] if t4 == 0 else []
        if t4 - 2 in stv:
            w.append(("ST", stv[t4 - 2]))
        ld = p.op("sync", "dma_start", buf[t4 % 2][:].rearrange("p (a n) -> p a n", a=4),
                  src[t4 * 512:(t4 + 1) * 512, :].rearrange("(a p) n -> p a n", p=128),
                  wait=w, inc="LD", incn=16)
        stv[t4] = p.op("sync", "dma_start",
                       dst[t4 * 512:(t4 + 1) * 512, :].rearrange("(a p) n -> p a n", p=128),
                       buf[t4 % 2][:].rearrange("p (a n) -> p a n", a=4),
                       wait=[("LD", ld)], inc="ST", incn=16)
    p.barrier()


STAGES_ALL = ["l0a", "l0f", "l1a", "l1f"]


def kernel(**inputs):
    consts = host_consts(inputs)
    nc = build_nc(STAGES_ALL)
    x = inputs["L0_attn_x"]
    base = {k: np.ascontiguousarray(v, dtype=np.float32) for k, v in inputs.items() if k != "L0_attn_x"}
    base.update({k: np.ascontiguousarray(v, dtype=np.float32) for k, v in consts.items()})
    in_maps = []
    for c in range(CORES):
        m = dict(base)
        m["x"] = np.ascontiguousarray(x[c], dtype=np.float32)
        in_maps.append(m)
    res = run_bass_kernel_spmd(nc, in_maps, core_ids=list(range(CORES)))
    out = np.stack([res.results[c]["out"] for c in range(CORES)], axis=0)
    return out
